# version 18
# baseline (speedup 1.0000x reference)
import contextlib
import numpy as np
import concourse.bass as bass
import concourse.mybir as mybir
from concourse.bass_utils import run_bass_kernel_spmd

F32 = mybir.dt.float32
BF16 = mybir.dt.bfloat16
AF = mybir.ActivationFunctionType
ALU = mybir.AluOpType
AX = mybir.AxisListType

D = 1024
T = 2048
NDT = 8
CH = 512
NCH = T // CH
NTT = T // 128
HEADS = 16
HD = 64
LN_EPS = 1e-5



class Ctx:

    def __init__(self, nc, es):
        self.nc = nc
        self.es = es
        self._sems = {}
        self.counts = {}

    def sem(self, name):
        if name not in self._sems:
            self._sems[name] = self.es.enter_context(self.nc.semaphore(name))
            self.counts[name] = 0
        return self._sems[name]

    def inc(self, instr, name, step=1):
        s = self.sem(name)
        self.counts[name] += step
        instr.then_inc(s, step)
        return (name, self.counts[name])

    def wait(self, eng, tok):
        if tok is None:
            return
        name, val = tok
        eng.wait_ge(self.sem(name), val)

    def waits(self, eng, toks):
        best = {}
        for t in toks:
            if t is None:
                continue
            n, v = t
            if n not in best or v > best[n]:
                best[n] = v
        for n, v in best.items():
            eng.wait_ge(self.sem(n), v)

    @property
    def t(self):
        return self.nc.tensor

    @property
    def v(self):
        if not hasattr(self, "_vp"):
            self._vp = EngProxy(self, self.nc.vector, "dve")
        return self._vp

    @property
    def s(self):
        if not hasattr(self, "_sp"):
            self._sp = EngProxy(self, self.nc.scalar, "act")
        return self._sp

    @property
    def g(self):
        if not hasattr(self, "_gp"):
            self._gp = EngProxy(self, self.nc.gpsimd, "pool")
        return self._gp

    @property
    def y(self):
        return self.nc.sync

    def S(self, name, shape, dtype):
        return self.es.enter_context(self.nc.sbuf_tensor(name, shape, dtype))

    def bar(self):
        return [(n, c) for n, c in self.counts.items() if c > 0]


class EngProxy:
    CHAINED = {
        "tensor_tensor", "tensor_scalar", "tensor_copy", "memset",
        "reciprocal", "scalar_tensor_tensor", "tensor_tensor_scan",
        "tensor_reduce", "reduce_max", "reduce_sum", "select",
        "transpose", "activation", "copy", "add", "mul", "iota",
        "tensor_scalar_mul", "tensor_scalar_add", "tensor_add", "tensor_mul",
        "tensor_sub", "stream_shuffle",
    }

    def __init__(self, ctx, eng, sem_name):
        self._ctx = ctx
        self._eng = eng
        self._sem = sem_name

    def __getattr__(self, name):
        real = getattr(self._eng, name)
        if name not in EngProxy.CHAINED:
            return real

        def wrap(*a, **kw):
            ctx = self._ctx
            if ctx.counts.get(self._sem, 0) > 0:
                self._eng.wait_ge(ctx.sem(self._sem), ctx.counts[self._sem])
            instr = real(*a, **kw)
            return ctx.inc(instr, self._sem)
        return wrap


def emit_dma_waited(ctx, q, out, in_, chan="qs", pre=()):
    ctx.waits(q, pre)
    d = q.dma_start(out=out, in_=in_)
    return ctx.inc(d, chan, 16)


def load_wcast(ctx, hbm2d, dst_ap, chan, pre=()):
    ctx.waits(ctx.g, list(pre))
    d = ctx.g.dma_start(out=dst_ap, in_=hbm2d)
    return ctx.inc(d, chan, 16)



class LNRows:
    def __init__(self, alloc, tag):
        self.mean = alloc(tag + "_mean", [1, CH], F32)
        self.msq = alloc(tag + "_msq", [1, CH], F32)
        self.rstd = alloc(tag + "_rstd", [1, CH], F32)
        self.b = alloc(tag + "_b", [1, CH], F32)


def emit_ln_rowmath(ctx, ps_mean, ps_msq, rows, done_toks, rms=False, eps_ap=None):
    v, s = ctx.v, ctx.s
    eps_ap = eps_ap if eps_ap is not None else ctx.eps_row
    ctx.waits(s, done_toks)
    if not rms:
        t1 = s.activation(out=rows.mean[:], in_=ps_mean[:], func=AF.Identity)
    t2 = s.activation(out=rows.msq[:], in_=ps_msq[:], func=AF.Identity)
    if rms:
        tl = s.activation(out=rows.rstd[:], in_=rows.msq[:], func=AF.Ln, bias=eps_ap)
        tl = s.activation(out=rows.rstd[:], in_=rows.rstd[:], func=AF.Exp, scale=-0.5)
        return tl
    ctx.waits(v, [t1, t2])
    tv = v.tensor_tensor(out=rows.rstd[:], in0=rows.mean[:], in1=rows.mean[:], op=ALU.mult)
    tv = v.tensor_tensor(out=rows.rstd[:], in0=rows.msq[:], in1=rows.rstd[:], op=ALU.subtract)
    ctx.wait(s, tv)
    tl = s.activation(out=rows.rstd[:], in_=rows.rstd[:], func=AF.Ln, bias=eps_ap)
    tl = s.activation(out=rows.rstd[:], in_=rows.rstd[:], func=AF.Exp, scale=-0.5)
    ctx.wait(v, tl)
    tb = v.scalar_tensor_tensor(
        out=rows.b[:], in0=rows.mean[:], scalar=-1.0, in1=rows.rstd[:],
        op0=ALU.mult, op1=ALU.mult)
    return tb


def emit_ln_stats_chunk(ctx, src_tiles, ps_mean, ps_msq, sq, CONST,
                        pre_toks, psum_free_toks, rms=False):
    t, s = ctx.t, ctx.s
    tokA = None
    if not rms:
        for dt in range(NDT):
            ctx.waits(t, list(pre_toks[dt]) + [psum_free_toks.get("mean")])
            mm = t.matmul(ps_mean[:], CONST["ones_scaled"][:, 0:1], src_tiles[dt],
                          start=(dt == 0), stop=(dt == NDT - 1))
        tokA = ctx.inc(mm, "pe")
    for dt in range(NDT):
        ctx.waits(s, list(pre_toks[dt]) + [psum_free_toks.get("msq")])
        at = s.activation(out=sq[:, dt * CH:(dt + 1) * CH], in_=src_tiles[dt], func=AF.Square)
        ctx.waits(t, [at])
        mm = t.matmul(ps_msq[:], CONST["ones_scaled_b"][:, 0:1], sq[:, dt * CH:(dt + 1) * CH],
                      start=(dt == 0), stop=(dt == NDT - 1))
    tokB = ctx.inc(mm, "pe")
    return tokB


def emit_bcast2(ctx, CONST, row1, row2, ps_bc1, ps_bc2, ready_tok, bc_free_toks):
    t = ctx.t
    ctx.waits(t, [ready_tok] + list(bc_free_toks))
    ctx.inc(t.matmul(ps_bc1[:], CONST["onesrow"][0:1, 0:128], row1[:], start=True, stop=True), "pe")
    mm = t.matmul(ps_bc2[:], CONST["onesrow"][0:1, 0:128], row2[:], start=True, stop=True)
    return ctx.inc(mm, "pe")


def emit_bcast1(ctx, CONST, row1, ps_bc1, ready_tok, bc_free_toks):
    t = ctx.t
    ctx.waits(t, [ready_tok] + list(bc_free_toks))
    mm = t.matmul(ps_bc1[:], CONST["onesrow"][0:1, 0:128], row1[:], start=True, stop=True)
    return ctx.inc(mm, "pe")



def phase_l0attn(ctx, xin, xout, W, CONST, start_toks=()):
    entry = ctx.bar() + list(start_toks)
    for _eng in (ctx.t, ctx.v, ctx.s, ctx.g, ctx.y):
        ctx.waits(_eng, entry)
    nc = ctx.nc
    v, s, t, g, y = ctx.v, ctx.s, ctx.t, ctx.g, ctx.y
    ph = {}

    with contextlib.ExitStack() as es0:
        S0 = lambda n, sh, dt: es0.enter_context(nc.sbuf_tensor(n, sh, dt))
        attn_o = S0("A_attnO", [128, NDT * T], BF16)

        with contextlib.ExitStack() as esq:
            Sq = lambda n, sh, dt: esq.enter_context(nc.sbuf_tensor(n, sh, dt))
            qT = Sq("A_qT", [128, NDT * T], BF16)
            kT = Sq("A_kT", [128, NDT * T], BF16)
            vaug = Sq("A_vaug", [128, NTT * HEADS * (HD + 1)], BF16)
            U = Sq("A_U", [128, NDT * T], BF16)

            with contextlib.ExitStack() as es1:
                S1 = lambda n, sh, dt: es1.enter_context(nc.sbuf_tensor(n, sh, dt))
                P1 = lambda n, sh: es1.enter_context(nc.psum_tensor(n, sh, F32))
                xc = S1("A_xc", [128, NDT * CH], F32)
                sq = S1("A_sq", [128, NDT * CH], BF16)
                rows = LNRows(S1, "A_r1")
                ps_mean = P1("A_psm", [1, CH])
                ps_msq = P1("A_psq", [1, CH])
                ps_bc1 = P1("A_psb1", [128, CH])
                ps_bc2 = P1("A_psb2", [128, CH])

                xld = [None] * NDT
                uc_tok = [None] * NDT
                psum_free = {}
                bc_free = []
                for c in range(NCH):
                    for dt in range(NDT):
                        xld[dt] = emit_dma_waited(
                            ctx, y, xc[:, dt * CH:(dt + 1) * CH],
                            xin[dt * 128:(dt + 1) * 128, c * CH:(c + 1) * CH],
                            chan=f"qx{dt}",
                            pre=[uc_tok[dt]] + (list(start_toks) if c == 0 else []))
                    xtiles = [xc[:, dt * CH:(dt + 1) * CH] for dt in range(NDT)]
                    st_tok = emit_ln_stats_chunk(ctx, xtiles, ps_mean, ps_msq, sq,
                                                 CONST, [[x] for x in xld], psum_free)
                    rdy = emit_ln_rowmath(ctx, ps_mean, ps_msq, rows, [st_tok])
                    psum_free["mean"] = rdy
                    psum_free["msq"] = rdy
                    bt = emit_bcast2(ctx, CONST, rows.rstd, rows.b, ps_bc1, ps_bc2, rdy, bc_free)
                    for dt in range(NDT):
                        ctx.waits(v, [bt])
                        v.tensor_tensor(out=U[:, dt * T + c * CH:dt * T + (c + 1) * CH],
                                        in0=xc[:, dt * CH:(dt + 1) * CH], in1=ps_bc1[:], op=ALU.mult)
                        tv = v.tensor_tensor(
                            out=U[:, dt * T + c * CH:dt * T + (c + 1) * CH],
                            in0=U[:, dt * T + c * CH:dt * T + (c + 1) * CH],
                            in1=ps_bc2[:], op=ALU.add)
                        uc_tok[dt] = tv
                    bc_free = [uc_tok[NDT - 1]]
                u_tok = [uc_tok[NDT - 1]]

            with contextlib.ExitStack() as es2:
                S2 = lambda n, sh, dt: es2.enter_context(nc.sbuf_tensor(n, sh, dt))
                P2 = lambda n, sh, *a: es2.enter_context(nc.psum_tensor(n, sh, *(a or (F32,))))
                wbufs = [S2("A_w0", [128, NDT * D], BF16), S2("A_w1", [128, NDT * D], BF16)]
                vT_s = [S2("A_vts0", [128, CH], BF16), S2("A_vts1", [128, CH], BF16)]
                ps_mm = [P2("A_pmm0", [128, CH]), P2("A_pmm1", [128, CH])]
                ps_tr = [P2("A_ptr0", [128, 128], BF16), P2("A_ptr1", [128, 128], BF16)]

                ones_tok = g.memset(vaug[:].rearrange("p (a b) -> p a b", b=HD + 1)[:, :, HD:HD + 1], 1.0)

                evac_toks = {}
                vts_tok = [None, None]
                tr_cons = [None, None]
                w_users = [[], []]
                scope_bar = ctx.bar()
                for wi, wname in enumerate(("wqT", "wkT", "wvT")):
                    ws = wi % 2
                    wbuf = wbufs[ws]
                    wtok = [load_wcast(ctx, W[wname], wbuf[:], f"qwA{ws}",
                                       pre=w_users[ws] + (scope_bar if wi < 2 else []))]
                    w_users[ws] = []
                    for ot in range(NDT):
                        for c in range(NCH):
                            pb = (ot * NCH + c) % 2
                            ps = ps_mm[pb]
                            ctx.waits(t, [evac_toks.get(pb)] + u_tok + wtok)
                            for dt in range(NDT):
                                mm = t.matmul(ps[:], wbuf[:, dt * D + ot * 128: dt * D + (ot + 1) * 128],
                                              U[:, dt * T + c * CH:dt * T + (c + 1) * CH],
                                              start=(dt == 0), stop=(dt == NDT - 1))
                            pt = ctx.inc(mm, "pe")
                            w_users[ws].append(pt)
                            if wi < 2:
                                outb = qT if wi == 0 else kT
                                ctx.wait(v, pt)
                                evac_toks[pb] = v.tensor_copy(
                                    out=outb[:, ot * T + c * CH: ot * T + (c + 1) * CH],
                                    in_=ps[:])
                            else:
                                ctx.wait(s, pt)
                                vb = pb
                                ctx.waits(s, [vts_tok[vb]])
                                ev_tok = s.activation(out=vT_s[vb][:], in_=ps[:], func=AF.Identity)
                                evac_toks[pb] = ev_tok
                                h0 = 2 * ot
                                for tj in range(4):
                                    tt = c * 4 + tj
                                    trb = tj % 2
                                    ctx.waits(t, [ev_tok, tr_cons[trb], ones_tok])
                                    tr = t.transpose(ps_tr[trb][:], vT_s[vb][:, tj * 128:(tj + 1) * 128],
                                                     CONST["identb"][:])
                                    ptok = ctx.inc(tr, "pe")
                                    w_users.append(ptok)
                                    ctx.wait(s, ptok)
                                    base = tt * HEADS * (HD + 1)
                                    s.activation(out=vaug[:, base + h0 * (HD + 1): base + h0 * (HD + 1) + HD],
                                                 in_=ps_tr[trb][:, 0:HD], func=AF.Identity)
                                    tr_cons[trb] = s.activation(
                                        out=vaug[:, base + (h0 + 1) * (HD + 1): base + (h0 + 1) * (HD + 1) + HD],
                                        in_=ps_tr[trb][:, HD:128], func=AF.Identity)
                                    vts_tok[vb] = tr_cons[trb]
                ph["attn_ready"] = [tr_cons[0], tr_cons[1], ones_tok,
                                    evac_toks.get(0), evac_toks.get(1)]

            with contextlib.ExitStack() as es3:
                S3 = lambda n, sh, dt: es3.enter_context(nc.sbuf_tensor(n, sh, dt))
                P3 = lambda n, sh: es3.enter_context(nc.psum_tensor(n, sh, F32))
                E = [S3("A_E0", [128, NTT * CH], BF16), S3("A_E1", [128, NTT * CH], BF16)]
                un = [S3("A_un0", [HD, CH], BF16), S3("A_un1", [HD, CH], BF16)]
                rd = [S3("A_rd0", [1, CH], F32), S3("A_rd1", [1, CH], F32)]
                ps_s = [[P3("A_pss00", [128, CH]), P3("A_pss01", [128, CH])],
                        [P3("A_pss10", [128, CH]), P3("A_pss11", [128, CH])]]
                ps_av = [P3("A_pav0", [128, CH]), P3("A_pav1", [128, CH])]
                ps_bc = P3("A_pbc", [HD, CH])

                ctx.waits(t, ph["attn_ready"] + ctx.bar())
                e_slot = [[None] * NTT, [None] * NTT]
                s_free = [[None, None], [None, None]]
                av_free = [None, None]
                bc_free = None
                un_free = [None, None]
                for j in range(HEADS // 2):
                    dt_h = j
                    for qc in range(NCH):
                        nkb = qc * 4 + 4
                        e_rdy = [[None] * nkb, [None] * nkb]
                        for kb in range(nkb):
                            sb = kb % 2
                            jj = kb - qc * 4
                            col0 = 0 if jj < 0 else jj * 128
                            ncols = CH - col0
                            sc_tok = [None, None]
                            for hp in range(2):
                                pb_h = 64 * hp
                                ctx.waits(t, [s_free[hp][sb]])
                                mm = t.matmul(
                                    ps_s[hp][sb][:, col0:CH],
                                    kT[pb_h:pb_h + HD, dt_h * T + kb * 128: dt_h * T + (kb + 1) * 128],
                                    qT[pb_h:pb_h + HD, dt_h * T + qc * CH + col0: dt_h * T + (qc + 1) * CH],
                                    start=True, stop=True, tile_position=(pb_h, 0))
                                sc_tok[hp] = ctx.inc(mm, "pe")
                            for hp in range(2):
                                ctx.waits(s, [sc_tok[hp], e_slot[hp][kb]])
                                ex_tok = s.activation(out=E[hp][:, kb * CH: kb * CH + ncols],
                                                      in_=ps_s[hp][sb][:, col0:CH],
                                                      func=AF.Exp, scale=0.125)
                                s_free[hp][sb] = ex_tok
                                e_rdy[hp][kb] = ex_tok
                                if jj >= 0:
                                    ctx.waits(v, [ex_tok])
                                    e_rdy[hp][kb] = v.tensor_tensor(
                                        out=E[hp][:, kb * CH: kb * CH + 128],
                                        in0=E[hp][:, kb * CH: kb * CH + 128],
                                        in1=CONST["trimask"][:], op=ALU.mult)
                        av_tok = [None, None]
                        for hp in range(2):
                            h = 2 * j + hp
                            ctx.waits(t, e_rdy[hp] + [av_free[hp]])
                            for kb in range(nkb):
                                jj = kb - qc * 4
                                col0 = 0 if jj < 0 else jj * 128
                                ncols = CH - col0
                                base = kb * HEADS * (HD + 1)
                                mm = t.matmul(
                                    ps_av[hp][0:HD + 1, col0:CH],
                                    vaug[:, base + h * (HD + 1): base + (h + 1) * (HD + 1)],
                                    E[hp][:, kb * CH: kb * CH + ncols],
                                    start=(kb == 0), stop=(kb == nkb - 1))
                            av_tok[hp] = ctx.inc(mm, "pe")
                            for kb in range(nkb):
                                e_slot[hp][kb] = av_tok[hp]
                        for hp in range(2):
                            h = 2 * j + hp
                            pb_h = 64 * hp
                            ctx.waits(s, [av_tok[hp], un_free[hp]])
                            s.activation(out=rd[hp][:], in_=ps_av[hp][64:65, :], func=AF.Ln)
                            rc_tok = s.activation(out=rd[hp][:], in_=rd[hp][:], func=AF.Exp, scale=-1.0)
                            un_tok = s.activation(out=un[hp][:], in_=ps_av[hp][0:HD, :], func=AF.Identity)
                            av_free[hp] = un_tok
                            ctx.waits(t, [rc_tok, bc_free])
                            mm = t.matmul(ps_bc[:], CONST["onesrow"][0:1, 0:HD], rd[hp][:],
                                          start=True, stop=True)
                            bc_tok = ctx.inc(mm, "pe")
                            ctx.waits(v, [bc_tok, un_tok])
                            nm_tok = v.tensor_tensor(
                                out=attn_o[pb_h:pb_h + HD, dt_h * T + qc * CH:dt_h * T + (qc + 1) * CH],
                                in0=un[hp][:], in1=ps_bc[:], op=ALU.mult)
                            bc_free = nm_tok
                            un_free[hp] = nm_tok
                ph["attn_done"] = [bc_free]

        with contextlib.ExitStack() as es4:
            S4 = lambda n, sh, dt: es4.enter_context(nc.sbuf_tensor(n, sh, dt))
            P4 = lambda n, sh: es4.enter_context(nc.psum_tensor(n, sh, F32))
            wo = S4("A_wo", [128, NDT * D], BF16)
            Y = S4("A_Y", [128, NDT * T], F32)
            sq = S4("A_sq2", [128, NDT * CH], BF16)
            rows = LNRows(S4, "A_r2")
            xr = [S4("A_xr0", [128, CH], F32), S4("A_xr1", [128, CH], F32)]
            ost = [S4("A_ost0", [128, CH], F32), S4("A_ost1", [128, CH], F32)]
            ps_mm = [P4("A_pmm2", [128, CH]), P4("A_pmm3", [128, CH])]
            ps_mean = P4("A_psm2", [1, CH])
            ps_msq = P4("A_psq2", [1, CH])
            ps_bc1 = P4("A_psb3", [128, CH])
            ps_bc2 = P4("A_psb4", [128, CH])

            ph["bar3"] = ctx.bar()
            wtok = [load_wcast(ctx, W["woT"], wo[:], "qwA0", pre=ctx.bar())]

            evac = {}
            for ot in range(NDT):
                for c in range(NCH):
                    pb = (ot * NCH + c) % 2
                    ctx.waits(t, [evac.get(pb)] + wtok + ph["attn_done"] + ph["bar3"])
                    for dt in range(NDT):
                        mm = t.matmul(ps_mm[pb][:],
                                      wo[:, dt * D + ot * 128: dt * D + (ot + 1) * 128],
                                      attn_o[:, dt * T + c * CH:dt * T + (c + 1) * CH],
                                      start=(dt == 0), stop=(dt == NDT - 1))
                    pt = ctx.inc(mm, "pe")
                    ctx.wait(s, pt)
                    evac[pb] = s.activation(out=Y[:, ot * T + c * CH: ot * T + (c + 1) * CH],
                                            in_=ps_mm[pb][:], func=AF.Identity)
            ytok = [evac.get(0), evac.get(1)]

            psum_free = {}
            bc_free = []
            st_done = {0: None, 1: None}
            for c in range(NCH):
                ytiles = [Y[:, dt * T + c * CH:dt * T + (c + 1) * CH] for dt in range(NDT)]
                st_tok = emit_ln_stats_chunk(ctx, ytiles, ps_mean, ps_msq, sq,
                                             CONST, [ytok] * NDT, psum_free)
                rdy = emit_ln_rowmath(ctx, ps_mean, ps_msq, rows, [st_tok])
                psum_free["mean"] = rdy
                psum_free["msq"] = rdy
                bt = emit_bcast2(ctx, CONST, rows.rstd, rows.b, ps_bc1, ps_bc2, rdy, bc_free)
                bc_free = []
                for dt in range(NDT):
                    b = dt % 2
                    xtok = emit_dma_waited(
                        ctx, y, xr[b][:], xin[dt * 128:(dt + 1) * 128, c * CH:(c + 1) * CH],
                        chan=f"qr{b}", pre=[st_done[b]] + (ph["bar3"] if (c == 0 and dt < 2) else []))
                    ctx.waits(v, [bt, st_done[b]])
                    v.tensor_tensor(out=ost[b][:], in0=Y[:, dt * T + c * CH:dt * T + (c + 1) * CH],
                                    in1=ps_bc1[:], op=ALU.mult)
                    v.tensor_tensor(out=ost[b][:], in0=ost[b][:], in1=ps_bc2[:], op=ALU.add)
                    ctx.waits(v, [xtok])
                    vtok = v.tensor_tensor(out=ost[b][:], in0=ost[b][:], in1=xr[b][:], op=ALU.add)
                    if dt == NDT - 1:
                        bc_free = [vtok]
                    ctx.wait(y, vtok)
                    dd = y.dma_start(out=xout[dt * 128:(dt + 1) * 128, c * CH:(c + 1) * CH],
                                     in_=ost[b][:])
                    st_done[b] = ctx.inc(dd, f"qo{b}", 16)
            return [st_done[0], st_done[1]]




NKV = 4
FFN = 2048
NEXP = 8


def emit_rms_apply(ctx, src_full, dst_full, w_pp, CONST, rows, sq, ps_msq,
                   ps_bc1, pre_toks, psum_free, bc_free):
    v = ctx.v
    last = None
    for c in range(NCH):
        tiles = [src_full[:, dt * T + c * CH:dt * T + (c + 1) * CH] for dt in range(NDT)]
        st_tok = emit_ln_stats_chunk(ctx, tiles, None, ps_msq, sq, CONST,
                                     [pre_toks] * NDT, psum_free, rms=True)
        rdy = emit_ln_rowmath(ctx, None, ps_msq, rows, [st_tok], rms=True)
        psum_free["msq"] = rdy
        bt = emit_bcast1(ctx, CONST, rows.rstd, ps_bc1, rdy, list(bc_free))
        del bc_free[:]
        for dt in range(NDT):
            ctx.waits(v, [bt])
            v.tensor_tensor(out=dst_full[:, dt * T + c * CH:dt * T + (c + 1) * CH],
                            in0=src_full[:, dt * T + c * CH:dt * T + (c + 1) * CH],
                            in1=ps_bc1[:], op=ALU.mult)
            last = v.tensor_scalar(out=dst_full[:, dt * T + c * CH:dt * T + (c + 1) * CH],
                                   in0=dst_full[:, dt * T + c * CH:dt * T + (c + 1) * CH],
                                   scalar1=w_pp[:, dt:dt + 1], scalar2=None, op0=ALU.mult)
        bc_free.append(last)
    return [last]


def emit_ln_residual_store(ctx, Yfull, xin, xout, CONST, alloc, palloc, y_toks, tag="T"):
    v, s, t, g, y = ctx.v, ctx.s, ctx.t, ctx.g, ctx.y
    sq = alloc(tag + "_sq", [128, NDT * CH], BF16)
    rows = LNRows(alloc, tag + "_r")
    xr = [alloc(tag + "_xr0", [128, CH], F32), alloc(tag + "_xr1", [128, CH], F32)]
    ost = [alloc(tag + "_ost0", [128, CH], F32), alloc(tag + "_ost1", [128, CH], F32)]
    ps_mean = palloc(tag + "_psm", [1, CH])
    ps_msq = palloc(tag + "_psq", [1, CH])
    ps_bc1 = palloc(tag + "_psb1", [128, CH])
    ps_bc2 = palloc(tag + "_psb2", [128, CH])
    bar = ctx.bar()
    psum_free = {}
    bc_free = []
    st_done = {0: None, 1: None}
    for c in range(NCH):
        ytiles = [Yfull[:, dt * T + c * CH:dt * T + (c + 1) * CH] for dt in range(NDT)]
        st_tok = emit_ln_stats_chunk(ctx, ytiles, ps_mean, ps_msq, sq,
                                     CONST, [y_toks] * NDT, psum_free)
        rdy = emit_ln_rowmath(ctx, ps_mean, ps_msq, rows, [st_tok])
        psum_free["mean"] = rdy
        psum_free["msq"] = rdy
        bt = emit_bcast2(ctx, CONST, rows.rstd, rows.b, ps_bc1, ps_bc2, rdy, bc_free)
        bc_free = []
        for dt in range(NDT):
            b = dt % 2
            xtok = emit_dma_waited(
                ctx, y, xr[b][:], xin[dt * 128:(dt + 1) * 128, c * CH:(c + 1) * CH],
                chan=f"qr{b}", pre=[st_done[b]] + (bar if (c == 0 and dt < 2) else []))
            ctx.waits(v, [bt, st_done[b]])
            v.tensor_tensor(out=ost[b][:], in0=Yfull[:, dt * T + c * CH:dt * T + (c + 1) * CH],
                            in1=ps_bc1[:], op=ALU.mult)
            v.tensor_tensor(out=ost[b][:], in0=ost[b][:], in1=ps_bc2[:], op=ALU.add)
            ctx.waits(v, [xtok])
            vtok = v.tensor_tensor(out=ost[b][:], in0=ost[b][:], in1=xr[b][:], op=ALU.add)
            if dt == NDT - 1:
                bc_free = [vtok]
            ctx.wait(y, vtok)
            dd = y.dma_start(out=xout[dt * 128:(dt + 1) * 128, c * CH:(c + 1) * CH],
                             in_=ost[b][:])
            st_done[b] = ctx.inc(dd, f"qo{b}", 16)
    return [st_done[0], st_done[1]]


def phase_l0ffn(ctx, xin, xout, W, CONST, start_toks=()):
    entry = ctx.bar() + list(start_toks)
    for _eng in (ctx.t, ctx.v, ctx.s, ctx.g, ctx.y):
        ctx.waits(_eng, entry)
    nc = ctx.nc
    v, s, t, g, y = ctx.v, ctx.s, ctx.t, ctx.g, ctx.y

    with contextlib.ExitStack() as es0:
        S0 = lambda n, sh, dt: es0.enter_context(nc.sbuf_tensor(n, sh, dt))
        R = S0("B_R", [128, NDT * T], F32)
        stage_toks_g = {}

        esq_ = contextlib.ExitStack()
        Sq = lambda n, sh, dt: esq_.enter_context(nc.sbuf_tensor(n, sh, dt))
        qT = Sq("B_qT", [128, NDT * T], BF16)
        kT = Sq("B_kT", [128, 2 * T], BF16)
        vaug = Sq("B_vaug", [128, NTT * NKV * (HD + 1)], BF16)
        esH1 = contextlib.ExitStack()
        H = esH1.enter_context(nc.sbuf_tensor("B_H1", [128, NDT * T], BF16))

        with contextlib.ExitStack() as es1:
            S1 = lambda n, sh, dt: es1.enter_context(nc.sbuf_tensor(n, sh, dt))
            P1 = lambda n, sh: es1.enter_context(nc.psum_tensor(n, sh, F32))
            xc = S1("B_xc", [128, NDT * CH], F32)
            sq = S1("B_sq", [128, NDT * CH], BF16)
            rows = LNRows(S1, "B_r1")
            ps_mean = P1("B_psm", [1, CH])
            ps_msq = P1("B_psq", [1, CH])
            ps_bc1 = P1("B_psb1", [128, CH])
            ps_bc2 = P1("B_psb2", [128, CH])
            xld = [None] * NDT
            rc_tok = [None] * NDT
            psum_free = {}
            bc_free = []
            for c in range(NCH):
                for dt in range(NDT):
                    xld[dt] = emit_dma_waited(
                        ctx, y, xc[:, dt * CH:(dt + 1) * CH],
                        xin[dt * 128:(dt + 1) * 128, c * CH:(c + 1) * CH],
                        chan=f"qx{dt}",
                        pre=[rc_tok[dt]] + ((ctx.bar() + list(start_toks)) if c == 0 else []))
                xtiles = [xc[:, dt * CH:(dt + 1) * CH] for dt in range(NDT)]
                st_tok = emit_ln_stats_chunk(ctx, xtiles, ps_mean, ps_msq, sq,
                                             CONST, [[x] for x in xld], psum_free)
                rdy = emit_ln_rowmath(ctx, ps_mean, ps_msq, rows, [st_tok])
                psum_free["mean"] = rdy
                psum_free["msq"] = rdy
                bt = emit_bcast2(ctx, CONST, rows.rstd, rows.b, ps_bc1, ps_bc2, rdy, bc_free)
                for dt in range(NDT):
                    ctx.waits(v, [bt])
                    v.tensor_tensor(out=R[:, dt * T + c * CH:dt * T + (c + 1) * CH],
                                    in0=xc[:, dt * CH:(dt + 1) * CH], in1=ps_bc1[:], op=ALU.mult)
                    rc_tok[dt] = v.tensor_tensor(
                        out=R[:, dt * T + c * CH:dt * T + (c + 1) * CH],
                        in0=R[:, dt * T + c * CH:dt * T + (c + 1) * CH],
                        in1=ps_bc2[:], op=ALU.add)
                bc_free = [rc_tok[NDT - 1]]
            r_tok = [rc_tok[NDT - 1]]

        with contextlib.ExitStack() as es2:
            S2 = lambda n, sh, dt: es2.enter_context(nc.sbuf_tensor(n, sh, dt))
            P2 = lambda n, sh: es2.enter_context(nc.psum_tensor(n, sh, F32))
            sq = S2("B_sq2", [128, NDT * CH], BF16)
            rows = LNRows(S2, "B_r2")
            ln1 = S2("B_ln1", [128, NDT], F32)
            ps_msq = P2("B_psq2", [1, CH])
            ps_bc1 = P2("B_psb3", [128, CH])
            lt = emit_dma_waited(ctx, y, ln1[:], W["ln1_pp"], chan="qv0", pre=ctx.bar())
            ctx.waits(v, [lt])
            h_tok = emit_rms_apply(ctx, R, H, ln1, CONST, rows, sq, ps_msq, ps_bc1,
                                   r_tok, {}, [])

        if True:
            with contextlib.ExitStack() as es3:
                S3 = lambda n, sh, dt: es3.enter_context(nc.sbuf_tensor(n, sh, dt))
                P3 = lambda n, sh, *a: es3.enter_context(nc.psum_tensor(n, sh, *(a or (F32,))))
                wbufs = [S3("B_wqkv0", [128, NDT * 768], BF16), S3("B_wqkv1", [128, NDT * 768], BF16)]
                vT_s = [S3("B_vts0", [128, CH], BF16), S3("B_vts1", [128, CH], BF16)]
                ps_mm = [P3("B_pmm0", [128, CH]), P3("B_pmm1", [128, CH])]
                ps_tr = [P3("B_ptr0", [128, 128], BF16), P3("B_ptr1", [128, 128], BF16)]

                ones_tok = g.memset(
                    vaug[:].rearrange("p (a b) -> p a b", b=HD + 1)[:, :, HD:HD + 1], 1.0)
                evac_toks = {}
                vts_tok = [None, None]
                tr_cons = [None, None]
                wtok = []
                w_users = []
                for ot in range(12):
                    if ot % 6 == 0:
                        half = ot // 6
                        wbuf = wbufs[half]
                        wtok = [load_wcast(
                            ctx, W["wqkvT"][:, half * 6144:(half + 1) * 6144], wbuf[:], f"qwB{half}",
                            pre=(ctx.bar() if ot == 0 else []))]
                    oth = ot % 6
                    for c in range(NCH):
                        pb = (ot * NCH + c) % 2
                        ps = ps_mm[pb]
                        ctx.waits(t, [evac_toks.get(pb)] + h_tok + wtok)
                        for dt in range(NDT):
                            mm = t.matmul(ps[:], wbuf[:, dt * 768 + oth * 128: dt * 768 + (oth + 1) * 128],
                                          H[:, dt * T + c * CH:dt * T + (c + 1) * CH],
                                          start=(dt == 0), stop=(dt == NDT - 1))
                        pt = ctx.inc(mm, "pe")
                        if ot < 8:
                            ctx.wait(v, pt)
                            evac_toks[pb] = v.tensor_copy(
                                out=qT[:, ot * T + c * CH: ot * T + (c + 1) * CH], in_=ps[:])
                        elif ot < 10:
                            ctx.wait(v, pt)
                            evac_toks[pb] = v.tensor_copy(
                                out=kT[:, (ot - 8) * T + c * CH: (ot - 8) * T + (c + 1) * CH], in_=ps[:])
                        else:
                            ctx.wait(s, pt)
                            vb = pb
                            ctx.waits(s, [vts_tok[vb]])
                            ev_tok = s.activation(out=vT_s[vb][:], in_=ps[:], func=AF.Identity)
                            evac_toks[pb] = ev_tok
                            kv0 = 2 * (ot - 10)
                            for tj in range(4):
                                tt = c * 4 + tj
                                trb = tj % 2
                                ctx.waits(t, [ev_tok, tr_cons[trb], ones_tok])
                                tr = t.transpose(ps_tr[trb][:], vT_s[vb][:, tj * 128:(tj + 1) * 128],
                                                 CONST["identb"][:])
                                ptok = ctx.inc(tr, "pe")
                                ctx.wait(s, ptok)
                                base = tt * NKV * (HD + 1)
                                s.activation(out=vaug[:, base + kv0 * (HD + 1): base + kv0 * (HD + 1) + HD],
                                             in_=ps_tr[trb][:, 0:HD], func=AF.Identity)
                                tr_cons[trb] = s.activation(
                                    out=vaug[:, base + (kv0 + 1) * (HD + 1): base + (kv0 + 1) * (HD + 1) + HD],
                                    in_=ps_tr[trb][:, HD:128], func=AF.Identity)
                                vts_tok[vb] = tr_cons[trb]
                qkv_done = [tr_cons[0], tr_cons[1], ones_tok,
                            evac_toks.get(0), evac_toks.get(1)]

            esH1.close()
            attn_o = Sq("B_attnO", [128, NDT * T], BF16)

            with contextlib.ExitStack() as es4:
                S4 = lambda n, sh, dt: es4.enter_context(nc.sbuf_tensor(n, sh, dt))
                cosf = S4("B_cos", [128, T], BF16)
                sinf = S4("B_sin", [128, T], BF16)
                tmp = [S4(f"B_rt{i}", [128, CH], F32) for i in range(3)]
                bar4 = ctx.bar()
                c1 = load_wcast(ctx, W["cosF"], cosf[:], "qwB", pre=bar4)
                c2 = load_wcast(ctx, W["sinF"], sinf[:], "qwB", pre=[])
                ctx.waits(v, [c1, c2] + qkv_done)
                pairs = [(kT, 0, kT, T)] + [(qT, p * T, qT, (4 + p) * T) for p in range(4)]
                rope_pair_tok = []
                for (b1, o1, b2, o2) in pairs:
                    rope_last = None
                    for c in range(NCH):
                        sl = slice(c * CH, (c + 1) * CH)
                        a1 = b1[:, o1 + c * CH: o1 + (c + 1) * CH]
                        a2 = b2[:, o2 + c * CH: o2 + (c + 1) * CH]
                        v.tensor_tensor(out=tmp[0][:], in0=a1, in1=sinf[:, sl], op=ALU.mult)
                        v.tensor_tensor(out=tmp[1][:], in0=a1, in1=cosf[:, sl], op=ALU.mult)
                        v.tensor_tensor(out=tmp[2][:], in0=a2, in1=sinf[:, sl], op=ALU.mult)
                        v.tensor_tensor(out=a1, in0=tmp[1][:], in1=tmp[2][:], op=ALU.subtract)
                        v.tensor_tensor(out=tmp[1][:], in0=a2, in1=cosf[:, sl], op=ALU.mult)
                        rope_last = v.tensor_tensor(out=a2, in0=tmp[1][:], in1=tmp[0][:], op=ALU.add)
                    rope_pair_tok.append(rope_last)
                rope_tok = [rope_pair_tok[-1]]

            with contextlib.ExitStack() as es5:
                S5 = lambda n, sh, dt: es5.enter_context(nc.sbuf_tensor(n, sh, dt))
                P5 = lambda n, sh: es5.enter_context(nc.psum_tensor(n, sh, F32))
                E = [S5("B_E0", [128, NTT * CH], BF16), S5("B_E1", [128, NTT * CH], BF16)]
                unb = S5("B_un", [128, CH], BF16)
                un = [unb[0:64, :], unb[64:128, :]]
                rd = [S5("B_rd0", [1, CH], BF16), S5("B_rd1", [1, CH], BF16)]
                ps_s = [[P5("B_pss00", [128, CH]), P5("B_pss01", [128, CH])],
                        [P5("B_pss10", [128, CH]), P5("B_pss11", [128, CH])]]
                ps_av = [P5("B_pav0", [128, CH]), P5("B_pav1", [128, CH])]
                ps_bc = P5("B_pbc", [HD, CH])

                bar5 = ctx.bar()
                e_slot = [[None] * NTT, [None] * NTT]
                s_free = [[None, None], [None, None]]
                av_free = [None, None]
                bc_free = None
                un_free = [None, None]
                for hpair in range(HEADS // 2):
                    h0 = 2 * hpair
                    kvh = h0 // 4
                    qt1 = h0 // 4
                    ctx.waits(t, [rope_pair_tok[0], rope_pair_tok[1 + qt1]] +
                              (bar5 if hpair == 0 else []))
                    for qc in range(NCH):
                        nkb = qc * 4 + 4
                        e_rdy = [[None] * nkb, [None] * nkb]
                        for kb in range(nkb):
                            sb = kb % 2
                            jj = kb - qc * 4
                            col0 = 0 if jj < 0 else jj * 128
                            ncols = CH - col0
                            sc_tok = [None, None]
                            for hp in range(2):
                                h = h0 + hp
                                pb_h = 32 * (h % 4)
                                tp = (pb_h, 0)
                                ctx.waits(t, [s_free[hp][sb]])
                                t.matmul(
                                    ps_s[hp][sb][:, col0:CH],
                                    kT[pb_h:pb_h + 32, kb * 128:(kb + 1) * 128],
                                    qT[pb_h:pb_h + 32, qt1 * T + qc * CH + col0: qt1 * T + (qc + 1) * CH],
                                    start=True, stop=False, tile_position=tp)
                                mm = t.matmul(
                                    ps_s[hp][sb][:, col0:CH],
                                    kT[pb_h:pb_h + 32, T + kb * 128: T + (kb + 1) * 128],
                                    qT[pb_h:pb_h + 32, (4 + qt1) * T + qc * CH + col0: (4 + qt1) * T + (qc + 1) * CH],
                                    start=False, stop=True, tile_position=tp)
                                sc_tok[hp] = ctx.inc(mm, "pe")
                            for hp in range(2):
                                ctx.waits(s, [sc_tok[hp], e_slot[hp][kb]])
                                ex_tok = s.activation(out=E[hp][:, kb * CH: kb * CH + ncols],
                                                      in_=ps_s[hp][sb][:, col0:CH],
                                                      func=AF.Exp, scale=0.125)
                                s_free[hp][sb] = ex_tok
                                e_rdy[hp][kb] = ex_tok
                                if jj >= 0:
                                    ctx.waits(v, [ex_tok])
                                    e_rdy[hp][kb] = v.tensor_tensor(
                                        out=E[hp][:, kb * CH: kb * CH + 128],
                                        in0=E[hp][:, kb * CH: kb * CH + 128],
                                        in1=CONST["trimask"][:], op=ALU.mult)
                        av_tok = [None, None]
                        for hp in range(2):
                            ctx.waits(t, e_rdy[hp] + [av_free[hp]])
                            for kb in range(nkb):
                                jj = kb - qc * 4
                                col0 = 0 if jj < 0 else jj * 128
                                ncols = CH - col0
                                base = kb * NKV * (HD + 1)
                                mm = t.matmul(
                                    ps_av[hp][0:HD + 1, col0:CH],
                                    vaug[:, base + kvh * (HD + 1): base + (kvh + 1) * (HD + 1)],
                                    E[hp][:, kb * CH: kb * CH + ncols],
                                    start=(kb == 0), stop=(kb == nkb - 1))
                            av_tok[hp] = ctx.inc(mm, "pe")
                            for kb in range(nkb):
                                e_slot[hp][kb] = av_tok[hp]
                        for hp in range(2):
                            h = h0 + hp
                            ctx.waits(s, [av_tok[hp], un_free[hp]])
                            s.activation(out=rd[hp][:], in_=ps_av[hp][64:65, :], func=AF.Ln)
                            rc_tok2 = s.activation(out=rd[hp][:], in_=rd[hp][:], func=AF.Exp, scale=-1.0)
                            un_tok = s.activation(out=un[hp], in_=ps_av[hp][0:HD, :], func=AF.Identity)
                            av_free[hp] = un_tok
                            ctx.waits(t, [rc_tok2, bc_free])
                            mm = t.matmul(ps_bc[:], CONST["onesrow_b"][0:1, 0:HD], rd[hp][:],
                                          start=True, stop=True)
                            bc_tok = ctx.inc(mm, "pe")
                            ctx.waits(v, [bc_tok, un_tok])
                            nm_tok = v.tensor_tensor(
                                out=attn_o[64 * (h % 2):64 * (h % 2) + HD,
                                           (h // 2) * T + qc * CH:(h // 2) * T + (qc + 1) * CH],
                                in0=un[hp], in1=ps_bc[:], op=ALU.mult)
                            bc_free = nm_tok
                            un_free[hp] = nm_tok
                attn_done = [bc_free]

            with contextlib.ExitStack() as es6:
                S6 = lambda n, sh, dt: es6.enter_context(nc.sbuf_tensor(n, sh, dt))
                P6 = lambda n, sh: es6.enter_context(nc.psum_tensor(n, sh, F32))
                wo = S6("B_wo", [128, NDT * D], BF16)
                ps_mm = [P6("B_pmm2", [128, CH]), P6("B_pmm3", [128, CH])]
                wtok = [load_wcast(ctx, W["woT"], wo[:], "qwB", pre=ctx.bar())]
                evac = {}
                bar6 = ctx.bar()
                radd = None
                for ot in range(NDT):
                    for c in range(NCH):
                        pb = (ot * NCH + c) % 2
                        ctx.waits(t, [evac.get(pb)] + wtok + attn_done + bar6)
                        for dt in range(NDT):
                            mm = t.matmul(ps_mm[pb][:],
                                          wo[:, dt * D + ot * 128: dt * D + (ot + 1) * 128],
                                          attn_o[:, dt * T + c * CH:dt * T + (c + 1) * CH],
                                          start=(dt == 0), stop=(dt == NDT - 1))
                        pt = ctx.inc(mm, "pe")
                        ctx.waits(v, [pt])
                        radd = v.tensor_tensor(
                            out=R[:, ot * T + c * CH: ot * T + (c + 1) * CH],
                            in0=R[:, ot * T + c * CH: ot * T + (c + 1) * CH],
                            in1=ps_mm[pb][:], op=ALU.add)
                        evac[pb] = radd
                r2_tok = [radd]

        esq_.close()
        esH2 = contextlib.ExitStack()
        H = esH2.enter_context(nc.sbuf_tensor("B_H2", [128, NDT * T], BF16))

        with contextlib.ExitStack() as es7:
            S7 = lambda n, sh, dt: es7.enter_context(nc.sbuf_tensor(n, sh, dt))
            P7 = lambda n, sh: es7.enter_context(nc.psum_tensor(n, sh, F32))
            sq = S7("B_sq7", [128, NDT * CH], BF16)
            rows = LNRows(S7, "B_r7")
            ln2 = S7("B_ln2", [128, NDT], F32)
            ps_msq = P7("B_psq7", [1, CH])
            ps_bc1 = P7("B_psb7", [128, CH])
            lt = emit_dma_waited(ctx, y, ln2[:], W["ln2_pp"], chan="qv0", pre=ctx.bar())
            ctx.waits(v, [lt])
            h2_tok = emit_rms_apply(ctx, R, H, ln2, CONST, rows, sq, ps_msq, ps_bc1,
                                    r2_tok, {}, [])

        wf_dram = nc.dram_tensor("B_wf_dram", [NEXP, T], F32, kind="Internal").ap()
        esM = contextlib.ExitStack()
        wfT = esM.enter_context(nc.sbuf_tensor("B_wfT", [NEXP, T], F32))

        with contextlib.ExitStack() as es8:
            S8 = lambda n, sh, dt: es8.enter_context(nc.sbuf_tensor(n, sh, dt))
            P8 = lambda n, sh: es8.enter_context(nc.psum_tensor(n, sh, F32))
            gw = S8("B_gw", [128, NDT * NEXP], BF16)
            logitT = S8("B_logT", [NEXP, T], F32)
            ltm = S8("B_ltm", [128, NTT * NEXP], F32)
            prob = S8("B_prob", [128, NEXP], F32)
            eq1 = S8("B_eq1", [128, NEXP], F32)
            m1 = S8("B_m1", [128, 1], F32)
            m2 = S8("B_m2", [128, 1], F32)
            ws = S8("B_ws", [128, 1], F32)
            ps_g = P8("B_psg", [NEXP, CH])
            ps_t = [P8("B_pst0", [128, 128]), P8("B_pst1", [128, 128])]
            gt = [load_wcast(ctx, W["gate_wT"], gw[:], "qwB", pre=ctx.bar())]
            glast = None
            for c in range(NCH):
                ctx.waits(t, gt + [glast] + h2_tok)
                for dt in range(NDT):
                    mm = t.matmul(ps_g[:], gw[:, dt * NEXP:(dt + 1) * NEXP],
                                  H[:, dt * T + c * CH:dt * T + (c + 1) * CH],
                                  start=(dt == 0), stop=(dt == NDT - 1))
                pt = ctx.inc(mm, "pe")
                ctx.wait(s, pt)
                glast = s.activation(out=logitT[:, c * CH:(c + 1) * CH], in_=ps_g[:],
                                     func=AF.Identity)
            tr_done = [None, None]
            tm_tok = [None] * NTT
            for tt in range(NTT):
                b = tt % 2
                ctx.waits(t, [glast, tr_done[b]])
                tr = t.transpose(ps_t[b][0:128, 0:NEXP], logitT[:, tt * 128:(tt + 1) * 128],
                                 CONST["ident"][0:NEXP, 0:NEXP])
                ptok = ctx.inc(tr, "pe")
                ctx.wait(s, ptok)
                tm_tok[tt] = s.activation(out=ltm[:, tt * NEXP:(tt + 1) * NEXP],
                                          in_=ps_t[b][0:128, 0:NEXP], func=AF.Identity)
                tr_done[b] = tm_tok[tt]
            for tt in range(NTT):
                sl = ltm[:, tt * NEXP:(tt + 1) * NEXP]
                ctx.waits(v, [tm_tok[tt]])
                v.tensor_reduce(out=m1[:], in_=sl, axis=AX.X, op=ALU.max)
                tv = v.tensor_scalar(out=prob[:], in0=sl, scalar1=m1[:], scalar2=None,
                                     op0=ALU.subtract)
                ctx.waits(s, [tv])
                et = s.activation(out=prob[:], in_=prob[:], func=AF.Exp)
                ctx.waits(v, [et])
                v.tensor_reduce(out=ws[:], in_=prob[:], axis=AX.X, op=ALU.add)
                v.reciprocal(out=ws[:], in_=ws[:])
                v.tensor_scalar(out=prob[:], in0=prob[:], scalar1=ws[:], scalar2=None, op0=ALU.mult)
                v.tensor_reduce(out=m1[:], in_=prob[:], axis=AX.X, op=ALU.max)
                v.tensor_scalar(out=eq1[:], in0=prob[:], scalar1=m1[:], scalar2=None, op0=ALU.is_ge)
                v.scalar_tensor_tensor(out=prob[:], in0=eq1[:], scalar=-1.0e30, in1=prob[:],
                                       op0=ALU.mult, op1=ALU.add)
                v.tensor_reduce(out=m2[:], in_=prob[:], axis=AX.X, op=ALU.max)
                v.tensor_scalar(out=prob[:], in0=prob[:], scalar1=m2[:], scalar2=None, op0=ALU.is_ge)
                v.tensor_tensor(out=ws[:], in0=m1[:], in1=m2[:], op=ALU.add)
                v.reciprocal(out=ws[:], in_=ws[:])
                v.tensor_tensor(out=m1[:], in0=m1[:], in1=ws[:], op=ALU.mult)
                v.tensor_tensor(out=m2[:], in0=m2[:], in1=ws[:], op=ALU.mult)
                v.tensor_scalar(out=eq1[:], in0=eq1[:], scalar1=m1[:], scalar2=None, op0=ALU.mult)
                v.scalar_tensor_tensor(out=sl, in0=prob[:], scalar=m2[:], in1=eq1[:],
                                       op0=ALU.mult, op1=ALU.add)
            tr_done = [None, None]
            wf_last = None
            for tt in range(NTT):
                b = tt % 2
                ctx.waits(t, [("dve", ctx.counts["dve"]), tr_done[b]])
                tr = t.transpose(ps_t[b][0:NEXP, 0:128], ltm[:, tt * NEXP:(tt + 1) * NEXP],
                                 CONST["ident"][:])
                ptok = ctx.inc(tr, "pe")
                ctx.wait(s, ptok)
                wf_last = s.activation(out=wfT[:, tt * 128:(tt + 1) * 128],
                                       in_=ps_t[b][0:NEXP, 0:128], func=AF.Identity)
                tr_done[b] = wf_last
            ctx.waits(y, [wf_last])
            wfd = y.dma_start(out=wf_dram, in_=wfT[:])
            wf_tok = [ctx.inc(wfd, "qwf", 16)]

        esM.close()

        esA = contextlib.ExitStack()
        SA = lambda n, sh, dt: esA.enter_context(nc.sbuf_tensor(n, sh, dt))
        PA = lambda n, sh: esA.enter_context(nc.psum_tensor(n, sh, F32))
        g_all = SA("B_gall", [128, NTT * 1024], BF16)
        w1s = [SA("B_w1s0", [128, NDT * CH], BF16), SA("B_w1s1", [128, NDT * CH], BF16)]
        w3s = [SA("B_w3s0", [128, NDT * CH], BF16), SA("B_w3s1", [128, NDT * CH], BF16)]
        _gs = SA("B_gsil0", [128, CH], F32)
        gsil = [_gs, _gs]
        w2b = SA("B_w2b", [128, NTT * 1024], BF16)
        wf_row = SA("B_wfrow", [1, CH], F32)
        wb_sb = SA("B_wb", [128, CH], F32)
        ps_g1 = PA("B_psg1", [128, CH])
        ps_g3 = PA("B_psg3", [128, CH])
        ps_o = [PA("B_pso0", [128, CH]), PA("B_pso1", [128, CH])]
        ps_wb = PA("B_pswb", [128, CH])
        tmpo = gsil[0]
        wf_guard = [None]
        radd = None
        b_done = None
        slab_guard = [None, None]
        gs_tok = [None, None]
        g1_free = None
        g3_free = None
        evac = {}
        wb_free = None
        w2_users = None
        bar9 = ctx.bar()
        for e in range(NEXP):
            if True:
                pre_toks = {}
                for slab in range(2):
                    sb = slab % 2
                    pre_toks[(1, slab)] = load_wcast(
                        ctx, W["w1T"][e * 128:(e + 1) * 128, slab * 4096:(slab + 1) * 4096],
                        w1s[sb][:], f"qs1{sb}",
                        pre=[slab_guard[sb]] + (bar9 if e == 0 else []))
                    pre_toks[(3, slab)] = load_wcast(
                        ctx, W["w3T"][e * 128:(e + 1) * 128, slab * 4096:(slab + 1) * 4096],
                        w3s[sb][:], f"qs3{sb}",
                        pre=[slab_guard[sb]] + (bar9 if e == 0 else []))
                w2tok = [load_wcast(ctx, W["w2T"][e * 128:(e + 1) * 128, :], w2b[:], "qw2B",
                                    pre=[b_done, w2_users] + (bar9 if e == 0 else []))]
                for half in range(2):
                    for slab in range(4):
                        sb = slab % 2
                        if (1, slab) in pre_toks:
                            t1 = pre_toks.pop((1, slab))
                            t3 = pre_toks.pop((3, slab))
                        else:
                            t1 = load_wcast(
                                ctx, W["w1T"][e * 128:(e + 1) * 128, slab * 4096:(slab + 1) * 4096],
                                w1s[sb][:], f"qs1{sb}", pre=[slab_guard[sb]])
                            t3 = load_wcast(
                                ctx, W["w3T"][e * 128:(e + 1) * 128, slab * 4096:(slab + 1) * 4096],
                                w3s[sb][:], f"qs3{sb}", pre=[slab_guard[sb]])
                        for fs in range(4):
                            fsg = slab * 4 + fs
                            for ch in range(2):
                                c = half * 2 + ch
                                ctx.waits(t, [t1, t3, g1_free, b_done] + h2_tok)
                                for dt in range(NDT):
                                    mm = t.matmul(
                                        ps_g1[:], w1s[sb][:, dt * CH + fs * 128: dt * CH + (fs + 1) * 128],
                                        H[:, dt * T + c * CH:dt * T + (c + 1) * CH],
                                        start=(dt == 0), stop=(dt == NDT - 1))
                                g1_tok = ctx.inc(mm, "pe")
                                ctx.waits(t, [g3_free])
                                for dt in range(NDT):
                                    mm = t.matmul(
                                        ps_g3[:], w3s[sb][:, dt * CH + fs * 128: dt * CH + (fs + 1) * 128],
                                        H[:, dt * T + c * CH:dt * T + (c + 1) * CH],
                                        start=(dt == 0), stop=(dt == NDT - 1))
                                g3_tok = ctx.inc(mm, "pe")
                                slab_guard[sb] = g3_tok
                                ctx.waits(s, [g1_tok, gs_tok[0]])
                                sil = s.activation(out=gsil[0][:], in_=ps_g1[:], func=AF.Silu)
                                g1_free = sil
                                ctx.waits(v, [sil, g3_tok, b_done])
                                gt2 = v.tensor_tensor(
                                    out=g_all[:, fsg * 1024 + ch * CH: fsg * 1024 + (ch + 1) * CH],
                                    in0=gsil[0][:], in1=ps_g3[:], op=ALU.mult)
                                gs_tok[0] = gt2
                                g3_free = gt2
                    ga_tok = [gs_tok[0], gs_tok[1]]
                    for ch in range(2):
                        c = half * 2 + ch
                        ctx.waits(y, wf_tok + [wf_guard[0]] + (bar9 if (e == 0 and half == 0 and ch == 0) else []))
                        wfr = y.dma_start(out=wf_row[:], in_=wf_dram[e:e + 1, c * CH:(c + 1) * CH])
                        wfr_tok = ctx.inc(wfr, "qwf", 16)
                        ctx.waits(t, [wfr_tok, wb_free])
                        mm = t.matmul(ps_wb[:], CONST["onesrow"][0:1, 0:128],
                                      wf_row[0:1, :], start=True, stop=True)
                        wbt = ctx.inc(mm, "pe")
                        wf_guard[0] = wbt
                        ctx.waits(s, [wbt, wb_free])
                        wb_ev = s.activation(out=wb_sb[:], in_=ps_wb[:], func=AF.Identity)
                        for ot in range(NDT):
                            pb = (ch * NDT + ot) % 2
                            ctx.waits(t, [evac.get(pb)] + ga_tok + w2tok)
                            for ft in range(NTT):
                                mm = t.matmul(
                                    ps_o[pb][:], w2b[:, ft * 1024 + ot * 128: ft * 1024 + (ot + 1) * 128],
                                    g_all[:, ft * 1024 + ch * CH: ft * 1024 + (ch + 1) * CH],
                                    start=(ft == 0), stop=(ft == NTT - 1))
                            pt = ctx.inc(mm, "pe")
                            b_done = pt
                            w2_users = pt
                            ctx.waits(v, [pt, wb_ev])
                            wb_free = v.tensor_tensor(out=tmpo[:], in0=ps_o[pb][:], in1=wb_sb[:], op=ALU.mult)
                            radd = v.tensor_tensor(
                                out=R[:, ot * T + c * CH: ot * T + (c + 1) * CH],
                                in0=R[:, ot * T + c * CH: ot * T + (c + 1) * CH],
                                in1=tmpo[:], op=ALU.add)
                            evac[pb] = radd
                    b_done = radd
        moe_tok = [radd]
        esA.close()
        esH2.close()

        with contextlib.ExitStack() as esT:
            ST = lambda n, sh, dt: esT.enter_context(nc.sbuf_tensor(n, sh, dt))
            PT = lambda n, sh: esT.enter_context(nc.psum_tensor(n, sh, F32))
            return emit_ln_residual_store(ctx, R, xin, xout, CONST, ST, PT, moe_tok, tag="BT")




def phase_l1attn(ctx, xin, xout, W, CONST, start_toks=()):
    entry = ctx.bar() + list(start_toks)
    for _eng in (ctx.t, ctx.v, ctx.s, ctx.g, ctx.y):
        ctx.waits(_eng, entry)
    nc = ctx.nc
    v, s, t, g, y = ctx.v, ctx.s, ctx.t, ctx.g, ctx.y
    NEG = -30.0
    EPS9 = 1e-9

    with contextlib.ExitStack() as es0:
        S0 = lambda n, sh, dt: es0.enter_context(nc.sbuf_tensor(n, sh, dt))
        Ybf = S0("C_Y", [128, NDT * T], BF16)
        vecs = S0("C_vecs", [128, 6 * NDT], F32)
        vt = emit_dma_waited(ctx, y, vecs[:, 0:5 * NDT], W["vecs_pp"], chan="qv0", pre=ctx.bar())
        ctx.waits(s, [vt])
        ew_tok = s.activation(out=vecs[:, 5 * NDT:6 * NDT], in_=vecs[:, 3 * NDT:4 * NDT],
                              func=AF.Exp)
        TMK, TMV, TMR, DEC, FIR, EW = range(6)

        with contextlib.ExitStack() as es1:
            S1 = lambda n, sh, dt: es1.enter_context(nc.sbuf_tensor(n, sh, dt))
            xk = S1("C_xk", [128, NDT * T], BF16)
            xv = S1("C_xv", [128, NDT * T], BF16)
            xr = S1("C_xr", [128, NDT * T], BF16)

            with contextlib.ExitStack() as es2:
                S2 = lambda n, sh, dt: es2.enter_context(nc.sbuf_tensor(n, sh, dt))
                P2 = lambda n, sh: es2.enter_context(nc.psum_tensor(n, sh, F32))
                U = S2("C_U", [128, NDT * T], BF16)
                xc = S2("C_xc", [128, NDT * CH], F32)
                sq = S2("C_sq", [128, NDT * CH], BF16)
                tmpm = S2("C_tmpm", [128, T], F32)
                rows = LNRows(S2, "C_r1")
                ps_mean = P2("C_psm", [1, CH])
                ps_msq = P2("C_psq", [1, CH])
                ps_bc1 = P2("C_psb1", [128, CH])
                ps_bc2 = P2("C_psb2", [128, CH])
                xld = [None] * NDT
                uc_tok = [None] * NDT
                psum_free = {}
                bc_free = []
                for c in range(NCH):
                    for dt in range(NDT):
                        xld[dt] = emit_dma_waited(
                            ctx, y, xc[:, dt * CH:(dt + 1) * CH],
                            xin[dt * 128:(dt + 1) * 128, c * CH:(c + 1) * CH],
                            chan=f"qx{dt}",
                            pre=[uc_tok[dt]] + ((ctx.bar() + list(start_toks)) if c == 0 else []))
                    xtiles = [xc[:, dt * CH:(dt + 1) * CH] for dt in range(NDT)]
                    st_tok = emit_ln_stats_chunk(ctx, xtiles, ps_mean, ps_msq, sq,
                                                 CONST, [[x] for x in xld], psum_free)
                    rdy = emit_ln_rowmath(ctx, ps_mean, ps_msq, rows, [st_tok])
                    psum_free["mean"] = rdy
                    psum_free["msq"] = rdy
                    bt = emit_bcast2(ctx, CONST, rows.rstd, rows.b, ps_bc1, ps_bc2, rdy, bc_free)
                    for dt in range(NDT):
                        ctx.waits(v, [bt])
                        v.tensor_tensor(out=U[:, dt * T + c * CH:dt * T + (c + 1) * CH],
                                        in0=xc[:, dt * CH:(dt + 1) * CH], in1=ps_bc1[:], op=ALU.mult)
                        uc_tok[dt] = v.tensor_tensor(
                            out=U[:, dt * T + c * CH:dt * T + (c + 1) * CH],
                            in0=U[:, dt * T + c * CH:dt * T + (c + 1) * CH],
                            in1=ps_bc2[:], op=ALU.add)
                    bc_free = [uc_tok[NDT - 1]]
                ctx.waits(v, [uc_tok[NDT - 1], ew_tok])
                mx_src = {}
                for dt in range(NDT):
                    sc = vecs[:, TMK * NDT + dt: TMK * NDT + dt + 1]
                    v.tensor_tensor(out=tmpm[:, 1:T], in0=U[:, dt * T + 1: dt * T + T],
                                    in1=U[:, dt * T: dt * T + T - 1], op=ALU.subtract)
                    v.scalar_tensor_tensor(out=xk[:, dt * T + 1: dt * T + T],
                                           in0=tmpm[:, 1:T], scalar=sc,
                                           in1=U[:, dt * T: dt * T + T - 1],
                                           op0=ALU.mult, op1=ALU.add)
                    mx_src[TMK] = v.tensor_scalar(out=xk[:, dt * T: dt * T + 1],
                                                  in0=U[:, dt * T: dt * T + 1],
                                                  scalar1=sc, scalar2=None, op0=ALU.mult)
                for dt in range(NDT):
                    v.tensor_tensor(out=tmpm[:, 1:T], in0=U[:, dt * T + 1: dt * T + T],
                                    in1=U[:, dt * T: dt * T + T - 1], op=ALU.subtract)
                    for (dst, tmi) in ((xv, TMV), (xr, TMR)):
                        sc = vecs[:, tmi * NDT + dt: tmi * NDT + dt + 1]
                        v.scalar_tensor_tensor(out=dst[:, dt * T + 1: dt * T + T],
                                               in0=tmpm[:, 1:T], scalar=sc,
                                               in1=U[:, dt * T: dt * T + T - 1],
                                               op0=ALU.mult, op1=ALU.add)
                        mx_src[tmi] = v.tensor_scalar(out=dst[:, dt * T: dt * T + 1],
                                                      in0=U[:, dt * T: dt * T + 1],
                                                      scalar1=sc, scalar2=None, op0=ALU.mult)
                mx_tok = [mx_src[TMR]]
                mx_by_wi = [mx_src[TMK], mx_src[TMV], mx_src[TMR]]

            with contextlib.ExitStack() as es3:
                S3 = lambda n, sh, dt: es3.enter_context(nc.sbuf_tensor(n, sh, dt))
                P3 = lambda n, sh: es3.enter_context(nc.psum_tensor(n, sh, F32))
                wk = S3("C_wk", [128, NDT * 128], BF16)
                wv = S3("C_wv", [128, NDT * 128], BF16)
                wr = S3("C_wr", [128, NDT * 128], BF16)
                kbufs = [S3("C_kd0", [128, T], F32), S3("C_kd1", [128, T], F32)]
                vbufs = [S3("C_vd0", [128, T], F32), S3("C_vd1", [128, T], F32)]
                ekv = S3("C_ekv", [128, T], F32)
                pst = S3("C_pst", [128, T + 1], F32)
                nst = S3("C_nst", [128, T + 1], F32)
                A1 = S3("C_A1", [128, T], F32)
                rbuf = S3("C_rd", [128, T], BF16)
                ps_mm = [P3("C_pmm0", [128, CH]), P3("C_pmm1", [128, CH])]

                y_tok = None
                prev_y = None
                prev_nscan = None
                cons_k = [None, None]
                cons_v = [None, None]
                evac = {}
                w_guard = [None, None, None]
                for dt in range(NDT):
                    idx = dt % 2
                    kb_ = kbufs[idx]
                    vb_ = vbufs[idx]
                    wtok = []
                    for wi_, (wb_, wname) in enumerate(((wk, "wkT"), (wv, "wvT"), (wr, "wrT"))):
                        wtok.append(load_wcast(
                            ctx, W[wname][:, dt * 1024:(dt + 1) * 1024], wb_[:], f"qwC{wi_}",
                            pre=[w_guard[wi_]] + (ctx.bar() if dt == 0 else [])))
                    for wi_, (wb_, src_, dstb) in enumerate(
                            ((wk, xk, kb_), (wv, xv, vb_), (wr, xr, None))):
                        for c in range(NCH):
                            pb = c % 2
                            ctx.waits(t, [evac.get(pb), mx_by_wi[wi_], wtok[wi_]])
                            for di in range(NDT):
                                mm = t.matmul(ps_mm[pb][:],
                                              wb_[:, di * 128:(di + 1) * 128],
                                              src_[:, di * T + c * CH:di * T + (c + 1) * CH],
                                              start=(di == 0), stop=(di == NDT - 1))
                            pt = ctx.inc(mm, "pe")
                            w_guard[wi_] = pt
                            if dstb is not None:
                                war = cons_k[idx] if wi_ == 0 else cons_v[idx]
                                ctx.waits(s, [pt, war])
                                evac[pb] = s.activation(out=dstb[:, c * CH:(c + 1) * CH],
                                                        in_=ps_mm[pb][:], func=AF.Identity)
                            else:
                                ctx.waits(s, [pt, prev_y])
                                evac[pb] = s.activation(out=rbuf[:, c * CH:(c + 1) * CH],
                                                        in_=ps_mm[pb][:], func=AF.Sigmoid)
                    kv_tok = [evac.get(0), evac.get(1)]
                    ctx.waits(s, kv_tok + [prev_nscan])
                    ek_tok = s.activation(out=ekv[:], in_=kb_[:], func=AF.Exp)
                    ctx.waits(g, [ek_tok])
                    ekv_tok = g.tensor_tensor(out=ekv[:], in0=ekv[:], in1=vb_[:], op=ALU.mult)
                    ctx.waits(v, [ekv_tok])
                    v.memset(pst[:, 0:1], -1.0e30)
                    v.memset(nst[:, 0:1], 0.0)
                    dec_b = vecs[:, DEC * NDT + dt: DEC * NDT + dt + 1].broadcast_to((128, T))
                    ew_b = vecs[:, EW * NDT + dt: EW * NDT + dt + 1].broadcast_to((128, T))
                    u_sc = vecs[:, FIR * NDT + dt: FIR * NDT + dt + 1]
                    v.tensor_tensor_scan(out=pst[:, 1:T + 1], data0=dec_b, data1=kb_[:],
                                         initial=-1.0e30, op0=ALU.add, op1=ALU.max)
                    prev_nscan = v.tensor_tensor_scan(out=nst[:, 1:T + 1], data0=ew_b, data1=ekv[:],
                                                      initial=0.0, op0=ALU.mult, op1=ALU.add)
                    v.scalar_tensor_tensor(out=A1[:], in0=kb_[:], scalar=u_sc, in1=pst[:, 0:T],
                                           op0=ALU.add, op1=ALU.max)
                    A2 = pst[:, 0:T]
                    v.tensor_tensor(out=A2, in0=pst[:, 0:T], in1=A1[:], op=ALU.subtract)
                    tv = v.tensor_scalar(out=A2, in0=A2, scalar1=-30.0, scalar2=None, op0=ALU.max)
                    ctx.waits(s, [tv])
                    eo_tok = s.activation(out=A2, in_=A2, func=AF.Exp)
                    v.scalar_tensor_tensor(out=kb_[:], in0=kb_[:], scalar=u_sc, in1=A1[:],
                                           op0=ALU.add, op1=ALU.subtract)
                    tv = v.tensor_scalar(out=kb_[:], in0=kb_[:], scalar1=-30.0, scalar2=None, op0=ALU.max)
                    ctx.waits(s, [tv, eo_tok])
                    ec_tok = s.activation(out=kb_[:], in_=kb_[:], func=AF.Exp)
                    em_tok = s.activation(out=A1[:], in_=A1[:], func=AF.Exp, scale=-1.0)
                    ctx.waits(v, [ec_tok, em_tok])
                    tv = v.scalar_tensor_tensor(out=A2, in0=A2, scalar=EPS9, in1=kb_[:],
                                                op0=ALU.add, op1=ALU.add)
                    ctx.waits(g, [tv])
                    ecv_tok = g.tensor_tensor(out=kb_[:], in0=kb_[:], in1=vb_[:], op=ALU.mult)
                    cons_v[idx] = ecv_tok
                    ctx.waits(s, [tv])
                    s.activation(out=A2, in_=A2, func=AF.Ln)
                    rcp = s.activation(out=A2, in_=A2, func=AF.Exp, scale=-1.0)
                    ctx.waits(v, [rcp])
                    v.tensor_tensor(out=A1[:], in0=A1[:], in1=nst[:, 0:T], op=ALU.mult)
                    ctx.waits(v, [ecv_tok])
                    cons_k[idx] = v.tensor_tensor(out=A1[:], in0=A1[:], in1=kb_[:], op=ALU.add)
                    v.tensor_tensor(out=A1[:], in0=A1[:], in1=A2, op=ALU.mult)
                    y_tok = v.tensor_tensor(out=Ybf[:, dt * T:(dt + 1) * T], in0=A1[:],
                                            in1=rbuf[:], op=ALU.mult)
                    prev_y = y_tok
                y_last = [y_tok]

        with contextlib.ExitStack() as es4:
            S4 = lambda n, sh, dt: es4.enter_context(nc.sbuf_tensor(n, sh, dt))
            P4 = lambda n, sh: es4.enter_context(nc.psum_tensor(n, sh, F32))
            wo = S4("C_wo", [128, NDT * D], BF16)
            Yp = S4("C_Yp", [128, NDT * T], F32)
            ps_mm = [P4("C_pmm2", [128, CH]), P4("C_pmm3", [128, CH])]
            wtok = [load_wcast(ctx, W["woT"], wo[:], "qwC0", pre=ctx.bar())]
            evac = {}
            for ot in range(NDT):
                for c in range(NCH):
                    pb = (ot * NCH + c) % 2
                    ctx.waits(t, [evac.get(pb)] + wtok + y_last)
                    for dt in range(NDT):
                        mm = t.matmul(ps_mm[pb][:],
                                      wo[:, dt * D + ot * 128: dt * D + (ot + 1) * 128],
                                      Ybf[:, dt * T + c * CH:dt * T + (c + 1) * CH],
                                      start=(dt == 0), stop=(dt == NDT - 1))
                    pt = ctx.inc(mm, "pe")
                    ctx.wait(s, pt)
                    evac[pb] = s.activation(out=Yp[:, ot * T + c * CH: ot * T + (c + 1) * CH],
                                            in_=ps_mm[pb][:], func=AF.Identity)
            ytok = [evac.get(0), evac.get(1)]
            with contextlib.ExitStack() as es5:
                S5 = lambda n, sh, dt: es5.enter_context(nc.sbuf_tensor(n, sh, dt))
                P5 = lambda n, sh: es5.enter_context(nc.psum_tensor(n, sh, F32))
                return emit_ln_residual_store(ctx, Yp, xin, xout, CONST, S5, P5, ytok, tag="CT")


def host_prep_C(inputs):
    vecs = np.stack([
        np.asarray(inputs["L1_attn_time_mix_k"]),
        np.asarray(inputs["L1_attn_time_mix_v"]),
        np.asarray(inputs["L1_attn_time_mix_r"]),
        np.asarray(inputs["L1_attn_time_decay"]),
        np.asarray(inputs["L1_attn_time_first"]),
    ])
    vecs_pp = np.concatenate([vec_pp(vv) for vv in vecs], axis=1)
    def ilv_dtblocks(w):
        wT = np.asarray(w).T
        return np.concatenate([ilv(wT[:, dt * 128:(dt + 1) * 128]) for dt in range(NDT)], 1)
    return {
        "wkT": ilv_dtblocks(inputs["L1_attn_W_k"]),
        "wvT": ilv_dtblocks(inputs["L1_attn_W_v"]),
        "wrT": ilv_dtblocks(inputs["L1_attn_W_r"]),
        "woT": ilv(np.asarray(inputs["L1_attn_W_o"]).T),
        "vecs_pp": np.ascontiguousarray(vecs_pp),
    }


def build_phaseC_program():
    nc = bass.Bass("TRN2", target_bir_lowering=False)
    es = contextlib.ExitStack()
    ctx = Ctx(nc, es)
    xin = nc.dram_tensor("xin", [D, T], F32, kind="ExternalInput").ap()
    xout = nc.dram_tensor("xout", [D, T], F32, kind="ExternalOutput").ap()
    W = {wn: nc.dram_tensor(wn, [128, NDT * D], F32, kind="ExternalInput").ap()
         for wn in ("wkT", "wvT", "wrT", "woT")}
    W["vecs_pp"] = nc.dram_tensor("vecs_pp", [128, 5 * NDT], F32, kind="ExternalInput").ap()
    with nc.Block():
        CONST = build_consts(ctx)
        for eng in (nc.tensor, nc.vector, nc.scalar, nc.gpsimd):
            ctx.waits(eng, CONST["_ready"])
        st = phase_l1attn(ctx, xin, xout, W, CONST)
        ctx.waits(nc.gpsimd, st)
    es.close()
    return nc



FD4 = 4096


def phase_l1ffn(ctx, xin, xout, W, CONST, start_toks=()):
    entry = ctx.bar() + list(start_toks)
    for _eng in (ctx.t, ctx.v, ctx.s, ctx.g, ctx.y):
        ctx.waits(_eng, entry)
    nc = ctx.nc
    v, s, t, g, y = ctx.v, ctx.s, ctx.t, ctx.g, ctx.y

    with contextlib.ExitStack() as es0:
        S0 = lambda n, sh, dt: es0.enter_context(nc.sbuf_tensor(n, sh, dt))
        U = S0("D_U", [128, NDT * T], BF16)
        Y = S0("D_Y", [128, NDT * T], F32)

        with contextlib.ExitStack() as es1:
            S1 = lambda n, sh, dt: es1.enter_context(nc.sbuf_tensor(n, sh, dt))
            P1 = lambda n, sh: es1.enter_context(nc.psum_tensor(n, sh, F32))
            xc = S1("D_xc", [128, NDT * CH], F32)
            sq = S1("D_sq", [128, NDT * CH], BF16)
            rows = LNRows(S1, "D_r1")
            ps_mean = P1("D_psm", [1, CH])
            ps_msq = P1("D_psq", [1, CH])
            ps_bc1 = P1("D_psb1", [128, CH])
            ps_bc2 = P1("D_psb2", [128, CH])
            xld = [None] * NDT
            uc_tok = [None] * NDT
            psum_free = {}
            bc_free = []
            for c in range(NCH):
                for dt in range(NDT):
                    xld[dt] = emit_dma_waited(
                        ctx, y, xc[:, dt * CH:(dt + 1) * CH],
                        xin[dt * 128:(dt + 1) * 128, c * CH:(c + 1) * CH],
                        chan=f"qx{dt}",
                        pre=[uc_tok[dt]] + ((ctx.bar() + list(start_toks)) if c == 0 else []))
                xtiles = [xc[:, dt * CH:(dt + 1) * CH] for dt in range(NDT)]
                st_tok = emit_ln_stats_chunk(ctx, xtiles, ps_mean, ps_msq, sq,
                                             CONST, [[x] for x in xld], psum_free)
                rdy = emit_ln_rowmath(ctx, ps_mean, ps_msq, rows, [st_tok])
                psum_free["mean"] = rdy
                psum_free["msq"] = rdy
                bt = emit_bcast2(ctx, CONST, rows.rstd, rows.b, ps_bc1, ps_bc2, rdy, bc_free)
                for dt in range(NDT):
                    ctx.waits(v, [bt])
                    v.tensor_tensor(out=U[:, dt * T + c * CH:dt * T + (c + 1) * CH],
                                    in0=xc[:, dt * CH:(dt + 1) * CH], in1=ps_bc1[:], op=ALU.mult)
                    uc_tok[dt] = v.tensor_tensor(
                        out=U[:, dt * T + c * CH:dt * T + (c + 1) * CH],
                        in0=U[:, dt * T + c * CH:dt * T + (c + 1) * CH],
                        in1=ps_bc2[:], op=ALU.add)
                bc_free = [uc_tok[NDT - 1]]
            u_tok = [uc_tok[NDT - 1]]

        with contextlib.ExitStack() as es2:
            S2 = lambda n, sh, dt: es2.enter_context(nc.sbuf_tensor(n, sh, dt))
            P2 = lambda n, sh: es2.enter_context(nc.psum_tensor(n, sh, F32))
            gw = S2("D_gw", [128, NDT * NEXP], BF16)
            gb = S2("D_gb", [NEXP, 1], F32)
            at_ = S2("D_at", [128, NDT * 32], BF16)
            bt_ = S2("D_bt", [32, FD4], BF16)
            e32 = S2("D_e32", [NEXP, 32], F32)
            logitT = S2("D_logT", [NEXP, T], F32)
            maskT = S2("D_maskT", [NEXP, T], F32)
            ltm = S2("D_ltm", [128, NTT * NEXP], F32)
            m1 = S2("D_m1", [128, 1], F32)
            xa = S2("D_xa", [32, T], F32)
            xam = S2("D_xam", [32, T], BF16)
            ps_g = P2("D_psg", [NEXP, CH])
            ps_t = [P2("D_pst0", [128, 128]), P2("D_pst1", [128, 128])]
            ps_xa = P2("D_psxa", [32, CH])

            gt = [load_wcast(ctx, W["gate_WT"], gw[:], "qwD1", pre=ctx.bar())]
            gbt = emit_dma_waited(ctx, y, gb[:], W["gate_b"], chan="qv0", pre=ctx.bar())
            e32t = emit_dma_waited(ctx, y, e32[:], W["e32"], chan="qv1", pre=ctx.bar())
            att = [load_wcast(ctx, W["AT"], at_[:], "qwD2", pre=[])]
            bt_tok = [load_wcast(ctx, W["BT_f32"], bt_[:], "qwD3", pre=[])]

            glast = None
            for c in range(NCH):
                ctx.waits(t, gt + [glast] + u_tok)
                for dt in range(NDT):
                    mm = t.matmul(ps_g[:], gw[:, dt * NEXP:(dt + 1) * NEXP],
                                  U[:, dt * T + c * CH:dt * T + (c + 1) * CH],
                                  start=(dt == 0), stop=(dt == NDT - 1))
                pt = ctx.inc(mm, "pe")
                ctx.waits(s, [pt, gbt])
                glast = s.activation(out=logitT[:, c * CH:(c + 1) * CH], in_=ps_g[:],
                                     func=AF.Identity, bias=gb[:])
            tr_done = [None, None]
            tm_tok = [None] * NTT
            for tt in range(NTT):
                b = tt % 2
                ctx.waits(t, [glast, tr_done[b]])
                tr = t.transpose(ps_t[b][0:128, 0:NEXP], logitT[:, tt * 128:(tt + 1) * 128],
                                 CONST["ident"][0:NEXP, 0:NEXP])
                ptok = ctx.inc(tr, "pe")
                ctx.wait(s, ptok)
                tm_tok[tt] = s.activation(out=ltm[:, tt * NEXP:(tt + 1) * NEXP],
                                          in_=ps_t[b][0:128, 0:NEXP], func=AF.Identity)
                tr_done[b] = tm_tok[tt]
            for tt in range(NTT):
                sl = ltm[:, tt * NEXP:(tt + 1) * NEXP]
                ctx.waits(v, [tm_tok[tt]])
                v.tensor_reduce(out=m1[:], in_=sl, axis=AX.X, op=ALU.max)
                v.tensor_scalar(out=sl, in0=sl, scalar1=m1[:], scalar2=None, op0=ALU.is_ge)
            tr_done = [None, None]
            mk_last = None
            for tt in range(NTT):
                b = tt % 2
                ctx.waits(t, [("dve", ctx.counts["dve"]), tr_done[b]])
                tr = t.transpose(ps_t[b][0:NEXP, 0:128], ltm[:, tt * NEXP:(tt + 1) * NEXP],
                                 CONST["ident"][:])
                ptok = ctx.inc(tr, "pe")
                ctx.wait(s, ptok)
                mk_last = s.activation(out=maskT[:, tt * 128:(tt + 1) * 128],
                                       in_=ps_t[b][0:NEXP, 0:128], func=AF.Identity)
                tr_done[b] = mk_last
            xa_last = None
            for c in range(NCH):
                ctx.waits(t, att + [xa_last] + u_tok)
                for dt in range(NDT):
                    mm = t.matmul(ps_xa[:], at_[:, dt * 32:(dt + 1) * 32],
                                  U[:, dt * T + c * CH:dt * T + (c + 1) * CH],
                                  start=(dt == 0), stop=(dt == NDT - 1))
                pt = ctx.inc(mm, "pe")
                ctx.wait(s, pt)
                xa_last = s.activation(out=xa[:, c * CH:(c + 1) * CH], in_=ps_xa[:],
                                       func=AF.Identity)
            xam_last = None
            for c in range(NCH):
                ctx.waits(t, [e32t, mk_last, xam_last, xa_last])
                mm = t.matmul(ps_xa[:], e32[:], maskT[:, c * CH:(c + 1) * CH],
                              start=True, stop=True)
                pt = ctx.inc(mm, "pe")
                ctx.waits(v, [pt, xa_last])
                xam_last = v.tensor_tensor(out=xam[0:32, c * CH:(c + 1) * CH],
                                           in0=xa[0:32, c * CH:(c + 1) * CH],
                                           in1=ps_xa[0:32, :], op=ALU.mult)
            xam_tok = [xam_last]

            with contextlib.ExitStack() as es3:
                S3 = lambda n, sh, dt: es3.enter_context(nc.sbuf_tensor(n, sh, dt))
                P3 = lambda n, sh: es3.enter_context(nc.psum_tensor(n, sh, F32))
                wib = [S3("D_wib0", [128, NDT * 1024], BF16), S3("D_wib1", [128, NDT * 1024], BF16)]
                wob = [S3("D_wob0", [128, NDT * 1024], BF16), S3("D_wob1", [128, NDT * 1024], BF16)]
                wibias = S3("D_wibias", [128, 32], F32)
                wobias = S3("D_wobias", [128, NDT], F32)
                inter = S3("D_inter", [128, NDT * CH], BF16)
                ps_i = [P3("D_psi0", [128, CH]), P3("D_psi1", [128, CH])]
                ps_o = [P3("D_pso0", [128, CH]), P3("D_pso1", [128, CH])]
                wibt = emit_dma_waited(ctx, y, wibias[:], W["wi_b_pp"], chan="qv0",
                                       pre=ctx.bar())
                wobt = emit_dma_waited(ctx, y, wobias[:], W["wo_b_pp"], chan="qv1",
                                       pre=ctx.bar())
                wi_used = [None, None]
                wo_used = [None, None]
                i_free = {}
                o_free = {}
                y_last = None
                for fsb in range(4):
                    wslot = fsb % 2
                    wtoki = [load_wcast(ctx, W["wiT"][:, fsb * 8192:(fsb + 1) * 8192], wib[wslot][:],
                                        f"qwiD{wslot}",
                                        pre=[wi_used[wslot]] + (ctx.bar() if fsb < 2 else []))]
                    wtoko = [load_wcast(ctx, W["woT2"][:, fsb * 8192:(fsb + 1) * 8192], wob[wslot][:],
                                        f"qwoD{wslot}",
                                        pre=[wo_used[wslot]] + (ctx.bar() if fsb < 2 else []))]
                    for c in range(NCH):
                        for fi in range(NDT):
                            fs = fsb * NDT + fi
                            pb = fi % 2
                            ctx.waits(t, [i_free.get(pb)] + wtoki + u_tok + xam_tok)
                            for dt in range(NDT):
                                t.matmul(ps_i[pb][:],
                                         wib[wslot][:, dt * 1024 + fi * 128: dt * 1024 + (fi + 1) * 128],
                                         U[:, dt * T + c * CH:dt * T + (c + 1) * CH],
                                         start=(dt == 0), stop=False)
                            mm = t.matmul(ps_i[pb][:], bt_[:, fs * 128:(fs + 1) * 128],
                                          xam[:, c * CH:(c + 1) * CH],
                                          start=False, stop=True)
                            pt = ctx.inc(mm, "pe")
                            wi_used[wslot] = pt
                            ctx.waits(s, [pt, wibt, bt_tok[-1] if fsb == 0 else None])
                            i_free[pb] = s.activation(out=inter[:, fi * CH:(fi + 1) * CH],
                                                      in_=ps_i[pb][:], func=AF.Relu,
                                                      bias=wibias[:, fs:fs + 1])
                        it_tok = [i_free.get(0), i_free.get(1)]
                        for ot in range(NDT):
                            pb = ot % 2
                            ctx.waits(t, [o_free.get(pb)] + wtoko + it_tok)
                            for fi in range(NDT):
                                mm = t.matmul(ps_o[pb][:],
                                              wob[wslot][:, fi * 1024 + ot * 128: fi * 1024 + (ot + 1) * 128],
                                              inter[:, fi * CH:(fi + 1) * CH],
                                              start=(fi == 0), stop=(fi == NDT - 1))
                            pt = ctx.inc(mm, "pe")
                            wo_used[wslot] = pt
                            if fsb == 0:
                                ctx.waits(s, [pt, wobt])
                                y_last = s.activation(out=Y[:, ot * T + c * CH: ot * T + (c + 1) * CH],
                                                      in_=ps_o[pb][:], func=AF.Identity,
                                                      bias=wobias[:, ot:ot + 1])
                                o_free[pb] = y_last
                            else:
                                ctx.waits(v, [pt, ("act", ctx.counts["act"])])
                                y_last = v.tensor_tensor(
                                    out=Y[:, ot * T + c * CH: ot * T + (c + 1) * CH],
                                    in0=Y[:, ot * T + c * CH: ot * T + (c + 1) * CH],
                                    in1=ps_o[pb][:], op=ALU.add)
                                o_free[pb] = y_last
                y_toks = [y_last]

        with contextlib.ExitStack() as es4:
            S4 = lambda n, sh, dt: es4.enter_context(nc.sbuf_tensor(n, sh, dt))
            P4 = lambda n, sh: es4.enter_context(nc.psum_tensor(n, sh, F32))
            return emit_ln_residual_store(ctx, Y, xin, xout, CONST, S4, P4, y_toks, tag="DT")


def host_prep_D(inputs):
    la = np.asarray(inputs["L1_ffn_lora_A"])
    lb = np.asarray(inputs["L1_ffn_lora_B"])
    e32 = np.zeros((NEXP, 32), np.float32)
    for e in range(NEXP):
        e32[e, e * 4:(e + 1) * 4] = 1.0
    wiT = np.asarray(inputs["L1_ffn_wi_W"]).T
    woT2 = np.asarray(inputs["L1_ffn_wo_W"]).T
    wiT_i = np.concatenate([ilv(wiT[:, f * 1024:(f + 1) * 1024]) for f in range(4)], 1)
    woT2_i = np.concatenate([ilv(woT2[f * 1024:(f + 1) * 1024, :]) for f in range(4)], 1)
    return {
        "gate_WT": ilv(np.asarray(inputs["L1_ffn_gate_W"]).T),
        "gate_b": np.ascontiguousarray(np.asarray(inputs["L1_ffn_gate_b"]).reshape(NEXP, 1)),
        "wiT": np.ascontiguousarray(wiT_i),
        "woT2": np.ascontiguousarray(woT2_i),
        "AT": ilv(la.transpose(2, 0, 1).reshape(1024, 32)),
        "BT_f32": np.ascontiguousarray(lb.transpose(0, 2, 1).reshape(32, FD4)),
        "wi_b_pp": vec_pp(inputs["L1_ffn_wi_b"]),
        "wo_b_pp": vec_pp(inputs["L1_ffn_wo_b"]),
        "e32": e32,
    }


def build_phaseD_program():
    nc = bass.Bass("TRN2", target_bir_lowering=False)
    es = contextlib.ExitStack()
    ctx = Ctx(nc, es)
    xin = nc.dram_tensor("xin", [D, T], F32, kind="ExternalInput").ap()
    xout = nc.dram_tensor("xout", [D, T], F32, kind="ExternalOutput").ap()
    W = {
        "gate_WT": nc.dram_tensor("gate_WT", [128, NDT * NEXP], F32, kind="ExternalInput").ap(),
        "gate_b": nc.dram_tensor("gate_b", [NEXP, 1], F32, kind="ExternalInput").ap(),
        "wiT": nc.dram_tensor("wiT", [128, 4 * 8192], F32, kind="ExternalInput").ap(),
        "woT2": nc.dram_tensor("woT2", [128, 4 * 8192], F32, kind="ExternalInput").ap(),
        "AT": nc.dram_tensor("AT", [128, NDT * 32], F32, kind="ExternalInput").ap(),
        "BT_f32": nc.dram_tensor("BT_f32", [32, FD4], F32, kind="ExternalInput").ap(),
        "wi_b_pp": nc.dram_tensor("wi_b_pp", [128, 32], F32, kind="ExternalInput").ap(),
        "wo_b_pp": nc.dram_tensor("wo_b_pp", [128, NDT], F32, kind="ExternalInput").ap(),
        "e32": nc.dram_tensor("e32", [NEXP, 32], F32, kind="ExternalInput").ap(),
    }
    with nc.Block():
        CONST = build_consts(ctx)
        for eng in (nc.tensor, nc.vector, nc.scalar, nc.gpsimd):
            ctx.waits(eng, CONST["_ready"])
        st = phase_l1ffn(ctx, xin, xout, W, CONST)
        ctx.waits(nc.gpsimd, st)
    es.close()
    return nc



def make_host_consts():
    con = {}
    con["c_ident"] = np.eye(128, dtype=np.float32)
    con["c_tri"] = np.triu(np.ones((128, 128), np.float32))
    return con


def build_consts(ctx):
    nc = ctx.nc
    C = {}
    C["ident"] = ctx.S("C_ident", [128, 128], F32)
    C["identb"] = ctx.S("C_identb", [128, 128], BF16)
    C["trimask"] = ctx.S("C_trimask", [128, 128], BF16)
    C["ones_scaled"] = ctx.S("C_onessc", [128, 1], F32)
    C["ones_scaled_b"] = ctx.S("C_onesscb", [128, 1], BF16)
    C["onesrow"] = ctx.S("C_onesrow", [1, 128], F32)
    C["onesrow_b"] = ctx.S("C_onesrowb", [1, 128], BF16)
    tri_f = ctx.S("C_trif", [128, 128], F32)
    C["eps_row"] = ctx.S("C_epsrow", [1, 1], F32)
    ctx.v.memset(C["eps_row"][:], LN_EPS)
    ctx.eps_row = C["eps_row"][:]
    ident_h = nc.dram_tensor("c_ident", [128, 128], F32, kind="ExternalInput").ap()
    tri_h = nc.dram_tensor("c_tri", [128, 128], F32, kind="ExternalInput").ap()
    t1 = emit_dma_waited(ctx, ctx.y, C["ident"][:], ident_h, chan="qc1")
    t3 = emit_dma_waited(ctx, ctx.y, tri_f[:], tri_h, chan="qc2")
    ctx.waits(ctx.g, [t1, t3])
    ctx.g.tensor_copy(out=C["identb"][:], in_=C["ident"][:])
    cv = ctx.g.tensor_copy(out=C["trimask"][:], in_=tri_f[:])
    ctx.v.memset(C["ones_scaled"][:], 1.0 / 1024.0)
    ctx.v.memset(C["ones_scaled_b"][:], 1.0 / 1024.0)
    ctx.v.memset(C["onesrow_b"][:], 1.0)
    ms = ctx.v.memset(C["onesrow"][:], 1.0)
    C["_ready"] = [cv, ms]
    return C



def build_phaseA_program():
    nc = bass.Bass("TRN2", target_bir_lowering=False)
    es = contextlib.ExitStack()
    ctx = Ctx(nc, es)
    xin = nc.dram_tensor("xin", [D, T], F32, kind="ExternalInput").ap()
    xout = nc.dram_tensor("xout", [D, T], F32, kind="ExternalOutput").ap()
    W = {wn: nc.dram_tensor(wn, [128, NDT * D], F32, kind="ExternalInput").ap()
         for wn in ("wqT", "wkT", "wvT", "woT")}
    with nc.Block():
        CONST = build_consts(ctx)
        for eng in (nc.tensor, nc.vector, nc.scalar, nc.gpsimd):
            ctx.waits(eng, CONST["_ready"])
        st = phase_l0attn(ctx, xin, xout, W, CONST)
        ctx.waits(nc.gpsimd, st)
    es.close()
    return nc


def host_prep_A(inputs):
    shared = {
        "wqT": ilv(np.asarray(inputs["L0_attn_q_heads"]).T),
        "wkT": ilv(np.asarray(inputs["L0_attn_k_heads"]).T),
        "wvT": ilv(np.asarray(inputs["L0_attn_v_heads"]).T),
        "woT": ilv(np.asarray(inputs["L0_attn_output_proj"]).T),
    }
    shared.update(make_host_consts())
    return shared


_BUILDERS = None


def kernel_multi(**inputs):
    x = np.asarray(inputs["L0_attn_x"])
    consts = make_host_consts()
    phase_w = [host_prep_A(inputs), host_prep_B(inputs),
               host_prep_C(inputs), host_prep_D(inputs)]
    builders = [build_phaseA_program, build_phaseB_program,
                build_phaseC_program, build_phaseD_program]
    xT = np.ascontiguousarray(x.transpose(0, 2, 1)).astype(np.float32, copy=False)
    cur = [np.ascontiguousarray(xT[b]) for b in range(8)]
    for build, Wd in zip(builders, phase_w):
        nc = build()
        in_maps = []
        for b in range(8):
            m = dict(Wd)
            m.update(consts)
            m["xin"] = cur[b]
            in_maps.append(m)
        res = run_bass_kernel_spmd(nc, in_maps, core_ids=list(range(8)))
        cur = [np.asarray(res.results[b]["xout"]) for b in range(8)]
    return np.stack([c.T for c in cur]).astype(np.float32, copy=False)


def ilv(w):
    K, M = w.shape
    return np.ascontiguousarray(np.asarray(w).reshape(K // 128, 128, M).transpose(1, 0, 2).reshape(128, -1))


def qkv_perm():
    perm = np.zeros(1536, dtype=np.int64)
    for h in range(16):
        for d0 in range(64):
            orig = h * 64 + d0
            if d0 < 32:
                new = (h // 4) * 128 + (h % 4) * 32 + d0
            else:
                new = 512 + (h // 4) * 128 + (h % 4) * 32 + (d0 - 32)
            perm[new] = orig
    for kv in range(4):
        for d0 in range(64):
            orig = 1024 + kv * 64 + d0
            if d0 < 32:
                new = 1024 + kv * 32 + d0
            else:
                new = 1152 + kv * 32 + (d0 - 32)
            perm[new] = orig
    for j in range(1280, 1536):
        perm[j] = j
    return perm


def rope_tables():
    inv_freq = 1.0 / (10000.0 ** (np.arange(0, HD, 2, dtype=np.float32) / HD))
    freqs = np.arange(T, dtype=np.float32)[:, None] * inv_freq[None, :]
    cosH = np.cos(freqs).T.astype(np.float32)
    sinH = np.sin(freqs).T.astype(np.float32)
    return np.tile(cosH, (4, 1)).copy(), np.tile(sinH, (4, 1)).copy()


def vec_pp(vec):
    return np.ascontiguousarray(np.asarray(vec).reshape(-1, 128).T)


def host_prep_B(inputs):
    perm = qkv_perm()
    wqkvT = np.asarray(inputs["L0_ffn_wqkv"]).T[:, perm]
    wqkvT_i = np.concatenate([ilv(wqkvT[:, 0:768]), ilv(wqkvT[:, 768:1536])], axis=1)
    cosF, sinF = rope_tables()
    w13 = np.asarray(inputs["L0_ffn_w13"])
    w1T = w13[:, :FFN, :].transpose(0, 2, 1)
    w3T = w13[:, FFN:, :].transpose(0, 2, 1)
    w1T_i = np.stack([np.concatenate([ilv(w1T[e][:, s * CH:(s + 1) * CH]) for s in range(4)], 1)
                      for e in range(NEXP)]).reshape(NEXP * 128, 4 * 4096)
    w3T_i = np.stack([np.concatenate([ilv(w3T[e][:, s * CH:(s + 1) * CH]) for s in range(4)], 1)
                      for e in range(NEXP)]).reshape(NEXP * 128, 4 * 4096)
    w2T = np.asarray(inputs["L0_ffn_w2"]).transpose(0, 2, 1)
    w2T_i = np.stack([ilv(w2T[e]) for e in range(NEXP)]).reshape(NEXP * 128, NTT * 1024)
    return {
        "wqkvT": np.ascontiguousarray(wqkvT_i),
        "woT": ilv(np.asarray(inputs["L0_ffn_wo"]).T),
        "gate_wT": ilv(np.asarray(inputs["L0_ffn_gate_w"]).T),
        "ln1_pp": vec_pp(inputs["L0_ffn_ln1_w"]),
        "ln2_pp": vec_pp(inputs["L0_ffn_ln2_w"]),
        "cosF": cosF, "sinF": sinF,
        "w1T": np.ascontiguousarray(w1T_i), "w3T": np.ascontiguousarray(w3T_i),
        "w2T": np.ascontiguousarray(w2T_i),
    }


def build_phaseB_program():
    nc = bass.Bass("TRN2", target_bir_lowering=False)
    es = contextlib.ExitStack()
    ctx = Ctx(nc, es)
    xin = nc.dram_tensor("xin", [D, T], F32, kind="ExternalInput").ap()
    xout = nc.dram_tensor("xout", [D, T], F32, kind="ExternalOutput").ap()
    W = {
        "wqkvT": nc.dram_tensor("wqkvT", [128, 2 * 6144], F32, kind="ExternalInput").ap(),
        "woT": nc.dram_tensor("woT", [128, NDT * D], F32, kind="ExternalInput").ap(),
        "gate_wT": nc.dram_tensor("gate_wT", [128, NDT * NEXP], F32, kind="ExternalInput").ap(),
        "ln1_pp": nc.dram_tensor("ln1_pp", [128, NDT], F32, kind="ExternalInput").ap(),
        "ln2_pp": nc.dram_tensor("ln2_pp", [128, NDT], F32, kind="ExternalInput").ap(),
        "cosF": nc.dram_tensor("cosF", [128, T], F32, kind="ExternalInput").ap(),
        "sinF": nc.dram_tensor("sinF", [128, T], F32, kind="ExternalInput").ap(),
        "w1T": nc.dram_tensor("w1T", [NEXP * 128, 4 * 4096], F32, kind="ExternalInput").ap(),
        "w3T": nc.dram_tensor("w3T", [NEXP * 128, 4 * 4096], F32, kind="ExternalInput").ap(),
        "w2T": nc.dram_tensor("w2T", [NEXP * 128, NTT * 1024], F32, kind="ExternalInput").ap(),
    }
    with nc.Block():
        CONST = build_consts(ctx)
        for eng in (nc.tensor, nc.vector, nc.scalar, nc.gpsimd):
            ctx.waits(eng, CONST["_ready"])
        st = phase_l0ffn(ctx, xin, xout, W, CONST)
        ctx.waits(nc.gpsimd, st)
    es.close()
    return nc


def build_fused_program():
    nc = bass.Bass("TRN2", target_bir_lowering=False)
    es = contextlib.ExitStack()
    ctx = Ctx(nc, es)
    xin = nc.dram_tensor("xin", [D, T], F32, kind="ExternalInput").ap()
    xout = nc.dram_tensor("xout", [D, T], F32, kind="ExternalOutput").ap()
    import os
    _xmk = "ExternalOutput" if os.environ.get("BASS_DEBUG_XM") else "Internal"
    xm = [nc.dram_tensor(f"xm{i}", [D, T], F32, kind=_xmk).ap() for i in range(3)]
    WA = {wn: nc.dram_tensor(wn, [128, NDT * D], F32, kind="ExternalInput").ap()
          for wn in ("wqT", "wkT", "wvT", "woT")}
    WB = {
        "wqkvT": nc.dram_tensor("b_wqkvT", [128, 2 * 6144], F32, kind="ExternalInput").ap(),
        "woT": nc.dram_tensor("b_woT", [128, NDT * D], F32, kind="ExternalInput").ap(),
        "gate_wT": nc.dram_tensor("b_gate_wT", [128, NDT * NEXP], F32, kind="ExternalInput").ap(),
        "ln1_pp": nc.dram_tensor("b_ln1_pp", [128, NDT], F32, kind="ExternalInput").ap(),
        "ln2_pp": nc.dram_tensor("b_ln2_pp", [128, NDT], F32, kind="ExternalInput").ap(),
        "cosF": nc.dram_tensor("b_cosF", [128, T], F32, kind="ExternalInput").ap(),
        "sinF": nc.dram_tensor("b_sinF", [128, T], F32, kind="ExternalInput").ap(),
        "w1T": nc.dram_tensor("b_w1T", [NEXP * 128, 4 * 4096], F32, kind="ExternalInput").ap(),
        "w3T": nc.dram_tensor("b_w3T", [NEXP * 128, 4 * 4096], F32, kind="ExternalInput").ap(),
        "w2T": nc.dram_tensor("b_w2T", [NEXP * 128, NTT * 1024], F32, kind="ExternalInput").ap(),
    }
    WC = {wn: nc.dram_tensor("c_" + wn, [128, NDT * D], F32, kind="ExternalInput").ap()
          for wn in ("wkT", "wvT", "wrT", "woT")}
    WC["vecs_pp"] = nc.dram_tensor("c_vecs_pp", [128, 5 * NDT], F32, kind="ExternalInput").ap()
    WD = {
        "gate_WT": nc.dram_tensor("d_gate_WT", [128, NDT * NEXP], F32, kind="ExternalInput").ap(),
        "gate_b": nc.dram_tensor("d_gate_b", [NEXP, 1], F32, kind="ExternalInput").ap(),
        "wiT": nc.dram_tensor("d_wiT", [128, 4 * 8192], F32, kind="ExternalInput").ap(),
        "woT2": nc.dram_tensor("d_woT2", [128, 4 * 8192], F32, kind="ExternalInput").ap(),
        "AT": nc.dram_tensor("d_AT", [128, NDT * 32], F32, kind="ExternalInput").ap(),
        "BT_f32": nc.dram_tensor("d_BT_f32", [32, FD4], F32, kind="ExternalInput").ap(),
        "wi_b_pp": nc.dram_tensor("d_wi_b_pp", [128, 32], F32, kind="ExternalInput").ap(),
        "wo_b_pp": nc.dram_tensor("d_wo_b_pp", [128, NDT], F32, kind="ExternalInput").ap(),
        "e32": nc.dram_tensor("d_e32", [NEXP, 32], F32, kind="ExternalInput").ap(),
    }
    with nc.Block():
        CONST = build_consts(ctx)
        for eng in (nc.tensor, nc.vector, nc.scalar, nc.gpsimd):
            ctx.waits(eng, CONST["_ready"])
        st = phase_l0attn(ctx, xin, xm[0], WA, CONST)
        st = phase_l0ffn(ctx, xm[0], xm[1], WB, CONST, start_toks=st)
        st = phase_l1attn(ctx, xm[1], xm[2], WC, CONST, start_toks=st)
        st = phase_l1ffn(ctx, xm[2], xout, WD, CONST, start_toks=st)
        ctx.waits(nc.gpsimd, st)
    es.close()
    return nc


def kernel(**inputs):
    x = np.asarray(inputs["L0_attn_x"])
    consts = make_host_consts()
    WA = host_prep_A(inputs)
    WB = {"b_" + k: v for k, v in host_prep_B(inputs).items()}
    WC = {"c_" + k: v for k, v in host_prep_C(inputs).items()}
    WD = {"d_" + k: v for k, v in host_prep_D(inputs).items()}
    shared = {}
    shared.update(WA); shared.update(WB); shared.update(WC); shared.update(WD)
    shared.update(consts)
    nc = build_fused_program()
    xT = np.ascontiguousarray(x.transpose(0, 2, 1)).astype(np.float32, copy=False)
    in_maps = []
    for b in range(8):
        m = dict(shared)
        m["xin"] = np.ascontiguousarray(xT[b])
        in_maps.append(m)
    res = run_bass_kernel_spmd(nc, in_maps, core_ids=list(range(8)))
    return np.stack([np.asarray(res.results[b]["xout"]).T for b in range(8)]).astype(np.float32, copy=False)


# version 19
# speedup vs baseline: 1.0094x; 1.0094x over previous
import contextlib
import numpy as np
import concourse.bass as bass
import concourse.mybir as mybir
from concourse.bass_utils import run_bass_kernel_spmd

F32 = mybir.dt.float32
BF16 = mybir.dt.bfloat16
AF = mybir.ActivationFunctionType
ALU = mybir.AluOpType
AX = mybir.AxisListType

D = 1024
T = 2048
NDT = 8
CH = 512
NCH = T // CH
NTT = T // 128
HEADS = 16
HD = 64
LN_EPS = 1e-5



class Ctx:

    def __init__(self, nc, es):
        self.nc = nc
        self.es = es
        self._sems = {}
        self.counts = {}

    def sem(self, name):
        if name not in self._sems:
            self._sems[name] = self.es.enter_context(self.nc.semaphore(name))
            self.counts[name] = 0
        return self._sems[name]

    def inc(self, instr, name, step=1):
        s = self.sem(name)
        self.counts[name] += step
        instr.then_inc(s, step)
        return (name, self.counts[name])

    def wait(self, eng, tok):
        if tok is None:
            return
        name, val = tok
        eng.wait_ge(self.sem(name), val)

    def waits(self, eng, toks):
        best = {}
        for t in toks:
            if t is None:
                continue
            n, v = t
            if n not in best or v > best[n]:
                best[n] = v
        for n, v in best.items():
            eng.wait_ge(self.sem(n), v)

    @property
    def t(self):
        return self.nc.tensor

    @property
    def v(self):
        if not hasattr(self, "_vp"):
            self._vp = EngProxy(self, self.nc.vector, "dve")
        return self._vp

    @property
    def s(self):
        if not hasattr(self, "_sp"):
            self._sp = EngProxy(self, self.nc.scalar, "act")
        return self._sp

    @property
    def g(self):
        if not hasattr(self, "_gp"):
            self._gp = EngProxy(self, self.nc.gpsimd, "pool")
        return self._gp

    @property
    def y(self):
        return self.nc.sync

    def S(self, name, shape, dtype):
        return self.es.enter_context(self.nc.sbuf_tensor(name, shape, dtype))

    def bar(self):
        return [(n, c) for n, c in self.counts.items() if c > 0]


class EngProxy:
    CHAINED = {
        "tensor_tensor", "tensor_scalar", "tensor_copy", "memset",
        "reciprocal", "scalar_tensor_tensor", "tensor_tensor_scan",
        "tensor_reduce", "reduce_max", "reduce_sum", "select",
        "transpose", "activation", "copy", "add", "mul", "iota",
        "tensor_scalar_mul", "tensor_scalar_add", "tensor_add", "tensor_mul",
        "tensor_sub", "stream_shuffle",
    }

    def __init__(self, ctx, eng, sem_name):
        self._ctx = ctx
        self._eng = eng
        self._sem = sem_name

    def __getattr__(self, name):
        real = getattr(self._eng, name)
        if name not in EngProxy.CHAINED:
            return real

        def wrap(*a, **kw):
            ctx = self._ctx
            if ctx.counts.get(self._sem, 0) > 0:
                self._eng.wait_ge(ctx.sem(self._sem), ctx.counts[self._sem])
            instr = real(*a, **kw)
            return ctx.inc(instr, self._sem)
        return wrap


def emit_dma_waited(ctx, q, out, in_, chan="qs", pre=()):
    ctx.waits(q, pre)
    d = q.dma_start(out=out, in_=in_)
    return ctx.inc(d, chan, 16)


def load_wcast(ctx, hbm2d, dst_ap, chan, pre=()):
    ctx.waits(ctx.g, list(pre))
    d = ctx.g.dma_start(out=dst_ap, in_=hbm2d)
    return ctx.inc(d, chan, 16)



class LNRows:
    def __init__(self, alloc, tag):
        self.mean = alloc(tag + "_mean", [1, CH], F32)
        self.msq = alloc(tag + "_msq", [1, CH], F32)
        self.rstd = alloc(tag + "_rstd", [1, CH], F32)
        self.b = alloc(tag + "_b", [1, CH], F32)


def emit_ln_rowmath(ctx, ps_mean, ps_msq, rows, done_toks, rms=False, eps_ap=None):
    v, s = ctx.v, ctx.s
    eps_ap = eps_ap if eps_ap is not None else ctx.eps_row
    ctx.waits(s, done_toks)
    if not rms:
        t1 = s.activation(out=rows.mean[:], in_=ps_mean[:], func=AF.Identity)
    t2 = s.activation(out=rows.msq[:], in_=ps_msq[:], func=AF.Identity)
    if rms:
        tl = s.activation(out=rows.rstd[:], in_=rows.msq[:], func=AF.Ln, bias=eps_ap)
        tl = s.activation(out=rows.rstd[:], in_=rows.rstd[:], func=AF.Exp, scale=-0.5)
        return tl
    ctx.waits(v, [t1, t2])
    tv = v.tensor_tensor(out=rows.rstd[:], in0=rows.mean[:], in1=rows.mean[:], op=ALU.mult)
    tv = v.tensor_tensor(out=rows.rstd[:], in0=rows.msq[:], in1=rows.rstd[:], op=ALU.subtract)
    ctx.wait(s, tv)
    tl = s.activation(out=rows.rstd[:], in_=rows.rstd[:], func=AF.Ln, bias=eps_ap)
    tl = s.activation(out=rows.rstd[:], in_=rows.rstd[:], func=AF.Exp, scale=-0.5)
    ctx.wait(v, tl)
    tb = v.scalar_tensor_tensor(
        out=rows.b[:], in0=rows.mean[:], scalar=-1.0, in1=rows.rstd[:],
        op0=ALU.mult, op1=ALU.mult)
    return tb


def emit_ln_stats_chunk(ctx, src_tiles, ps_mean, ps_msq, sq, CONST,
                        pre_toks, psum_free_toks, rms=False):
    t, s = ctx.t, ctx.s
    tokA = None
    if not rms:
        for dt in range(NDT):
            ctx.waits(t, list(pre_toks[dt]) + [psum_free_toks.get("mean")])
            mm = t.matmul(ps_mean[:], CONST["ones_scaled"][:, 0:1], src_tiles[dt],
                          start=(dt == 0), stop=(dt == NDT - 1))
        tokA = ctx.inc(mm, "pe")
    for dt in range(NDT):
        ctx.waits(s, list(pre_toks[dt]) + [psum_free_toks.get("msq")])
        at = s.activation(out=sq[:, dt * CH:(dt + 1) * CH], in_=src_tiles[dt], func=AF.Square)
        ctx.waits(t, [at])
        mm = t.matmul(ps_msq[:], CONST["ones_scaled_b"][:, 0:1], sq[:, dt * CH:(dt + 1) * CH],
                      start=(dt == 0), stop=(dt == NDT - 1))
    tokB = ctx.inc(mm, "pe")
    return tokB


def emit_bcast2(ctx, CONST, row1, row2, ps_bc1, ps_bc2, ready_tok, bc_free_toks):
    t = ctx.t
    ctx.waits(t, [ready_tok] + list(bc_free_toks))
    ctx.inc(t.matmul(ps_bc1[:], CONST["onesrow"][0:1, 0:128], row1[:], start=True, stop=True), "pe")
    mm = t.matmul(ps_bc2[:], CONST["onesrow"][0:1, 0:128], row2[:], start=True, stop=True)
    return ctx.inc(mm, "pe")


def emit_bcast1(ctx, CONST, row1, ps_bc1, ready_tok, bc_free_toks):
    t = ctx.t
    ctx.waits(t, [ready_tok] + list(bc_free_toks))
    mm = t.matmul(ps_bc1[:], CONST["onesrow"][0:1, 0:128], row1[:], start=True, stop=True)
    return ctx.inc(mm, "pe")



def phase_l0attn(ctx, xin, xout, W, CONST, start_toks=()):
    entry = ctx.bar() + list(start_toks)
    for _eng in (ctx.t, ctx.v, ctx.s, ctx.g, ctx.y):
        ctx.waits(_eng, entry)
    nc = ctx.nc
    v, s, t, g, y = ctx.v, ctx.s, ctx.t, ctx.g, ctx.y
    ph = {}

    with contextlib.ExitStack() as es0:
        S0 = lambda n, sh, dt: es0.enter_context(nc.sbuf_tensor(n, sh, dt))
        attn_o = S0("A_attnO", [128, NDT * T], BF16)

        with contextlib.ExitStack() as esq:
            Sq = lambda n, sh, dt: esq.enter_context(nc.sbuf_tensor(n, sh, dt))
            qT = Sq("A_qT", [128, NDT * T], BF16)
            kT = Sq("A_kT", [128, NDT * T], BF16)
            vaug = Sq("A_vaug", [128, NTT * HEADS * (HD + 1)], BF16)
            U = Sq("A_U", [128, NDT * T], BF16)

            with contextlib.ExitStack() as es1:
                S1 = lambda n, sh, dt: es1.enter_context(nc.sbuf_tensor(n, sh, dt))
                P1 = lambda n, sh: es1.enter_context(nc.psum_tensor(n, sh, F32))
                xc = S1("A_xc", [128, NDT * CH], F32)
                sq = S1("A_sq", [128, NDT * CH], BF16)
                rows = LNRows(S1, "A_r1")
                ps_mean = P1("A_psm", [1, CH])
                ps_msq = P1("A_psq", [1, CH])
                ps_bc1 = P1("A_psb1", [128, CH])
                ps_bc2 = P1("A_psb2", [128, CH])

                xld = [None] * NDT
                uc_tok = [None] * NDT
                psum_free = {}
                bc_free = []
                for c in range(NCH):
                    for dt in range(NDT):
                        xld[dt] = emit_dma_waited(
                            ctx, y, xc[:, dt * CH:(dt + 1) * CH],
                            xin[dt * 128:(dt + 1) * 128, c * CH:(c + 1) * CH],
                            chan=f"qx{dt}",
                            pre=[uc_tok[dt]] + (list(start_toks) if c == 0 else []))
                    xtiles = [xc[:, dt * CH:(dt + 1) * CH] for dt in range(NDT)]
                    st_tok = emit_ln_stats_chunk(ctx, xtiles, ps_mean, ps_msq, sq,
                                                 CONST, [[x] for x in xld], psum_free)
                    rdy = emit_ln_rowmath(ctx, ps_mean, ps_msq, rows, [st_tok])
                    psum_free["mean"] = rdy
                    psum_free["msq"] = rdy
                    bt = emit_bcast2(ctx, CONST, rows.rstd, rows.b, ps_bc1, ps_bc2, rdy, bc_free)
                    for dt in range(NDT):
                        ctx.waits(v, [bt])
                        v.tensor_tensor(out=U[:, dt * T + c * CH:dt * T + (c + 1) * CH],
                                        in0=xc[:, dt * CH:(dt + 1) * CH], in1=ps_bc1[:], op=ALU.mult)
                        tv = v.tensor_tensor(
                            out=U[:, dt * T + c * CH:dt * T + (c + 1) * CH],
                            in0=U[:, dt * T + c * CH:dt * T + (c + 1) * CH],
                            in1=ps_bc2[:], op=ALU.add)
                        uc_tok[dt] = tv
                    bc_free = [uc_tok[NDT - 1]]
                u_tok = [uc_tok[NDT - 1]]

            with contextlib.ExitStack() as es2:
                S2 = lambda n, sh, dt: es2.enter_context(nc.sbuf_tensor(n, sh, dt))
                P2 = lambda n, sh, *a: es2.enter_context(nc.psum_tensor(n, sh, *(a or (F32,))))
                wbufs = [S2("A_w0", [128, NDT * D], BF16), S2("A_w1", [128, NDT * D], BF16)]
                vT_s = [S2("A_vts0", [128, CH], BF16), S2("A_vts1", [128, CH], BF16)]
                ps_mm = [P2("A_pmm0", [128, CH]), P2("A_pmm1", [128, CH])]
                ps_tr = [P2("A_ptr0", [128, 128], BF16), P2("A_ptr1", [128, 128], BF16)]

                ones_tok = g.memset(vaug[:].rearrange("p (a b) -> p a b", b=HD + 1)[:, :, HD:HD + 1], 1.0)

                evac_toks = {}
                vts_tok = [None, None]
                tr_cons = [None, None]
                w_users = [[], []]
                scope_bar = ctx.bar()
                for wi, wname in enumerate(("wqT", "wkT", "wvT")):
                    ws = wi % 2
                    wbuf = wbufs[ws]
                    wtok = [load_wcast(ctx, W[wname], wbuf[:], f"qwA{ws}",
                                       pre=w_users[ws] + (scope_bar if wi < 2 else []))]
                    w_users[ws] = []
                    for ot in range(NDT):
                        for c in range(NCH):
                            pb = (ot * NCH + c) % 2
                            ps = ps_mm[pb]
                            ctx.waits(t, [evac_toks.get(pb)] + u_tok + wtok)
                            for dt in range(NDT):
                                mm = t.matmul(ps[:], wbuf[:, dt * D + ot * 128: dt * D + (ot + 1) * 128],
                                              U[:, dt * T + c * CH:dt * T + (c + 1) * CH],
                                              start=(dt == 0), stop=(dt == NDT - 1))
                            pt = ctx.inc(mm, "pe")
                            w_users[ws].append(pt)
                            if wi < 2:
                                outb = qT if wi == 0 else kT
                                ctx.wait(v, pt)
                                evac_toks[pb] = v.tensor_copy(
                                    out=outb[:, ot * T + c * CH: ot * T + (c + 1) * CH],
                                    in_=ps[:])
                            else:
                                ctx.wait(s, pt)
                                vb = pb
                                ctx.waits(s, [vts_tok[vb]])
                                ev_tok = s.activation(out=vT_s[vb][:], in_=ps[:], func=AF.Identity)
                                evac_toks[pb] = ev_tok
                                h0 = 2 * ot
                                for tj in range(4):
                                    tt = c * 4 + tj
                                    trb = tj % 2
                                    ctx.waits(t, [ev_tok, tr_cons[trb], ones_tok])
                                    tr = t.transpose(ps_tr[trb][:], vT_s[vb][:, tj * 128:(tj + 1) * 128],
                                                     CONST["identb"][:])
                                    ptok = ctx.inc(tr, "pe")
                                    w_users.append(ptok)
                                    ctx.wait(s, ptok)
                                    base = tt * HEADS * (HD + 1)
                                    s.activation(out=vaug[:, base + h0 * (HD + 1): base + h0 * (HD + 1) + HD],
                                                 in_=ps_tr[trb][:, 0:HD], func=AF.Identity)
                                    tr_cons[trb] = s.activation(
                                        out=vaug[:, base + (h0 + 1) * (HD + 1): base + (h0 + 1) * (HD + 1) + HD],
                                        in_=ps_tr[trb][:, HD:128], func=AF.Identity)
                                    vts_tok[vb] = tr_cons[trb]
                ph["attn_ready"] = [tr_cons[0], tr_cons[1], ones_tok,
                                    evac_toks.get(0), evac_toks.get(1)]

            with contextlib.ExitStack() as es3:
                S3 = lambda n, sh, dt: es3.enter_context(nc.sbuf_tensor(n, sh, dt))
                P3 = lambda n, sh: es3.enter_context(nc.psum_tensor(n, sh, F32))
                E = [S3("A_E0", [128, NTT * CH], BF16), S3("A_E1", [128, NTT * CH], BF16)]
                un = [S3("A_un0", [HD, CH], BF16), S3("A_un1", [HD, CH], BF16)]
                rd = [S3("A_rd0", [1, CH], F32), S3("A_rd1", [1, CH], F32)]
                ps_s = [[P3("A_pss00", [128, CH]), P3("A_pss01", [128, CH])],
                        [P3("A_pss10", [128, CH]), P3("A_pss11", [128, CH])]]
                ps_av = [P3("A_pav0", [128, CH]), P3("A_pav1", [128, CH])]
                ps_bc = P3("A_pbc", [HD, CH])

                ctx.waits(t, ph["attn_ready"] + ctx.bar())
                e_slot = [[None] * NTT, [None] * NTT]
                s_free = [[None, None], [None, None]]
                av_free = [None, None]
                bc_free = None
                un_free = [None, None]
                for j in range(HEADS // 2):
                    dt_h = j
                    for qc in range(NCH):
                        nkb = qc * 4 + 4
                        e_rdy = [[None] * nkb, [None] * nkb]
                        for kb in range(nkb):
                            sb = kb % 2
                            jj = kb - qc * 4
                            col0 = 0 if jj < 0 else jj * 128
                            ncols = CH - col0
                            sc_tok = [None, None]
                            for hp in range(2):
                                pb_h = 64 * hp
                                ctx.waits(t, [s_free[hp][sb]])
                                mm = t.matmul(
                                    ps_s[hp][sb][:, col0:CH],
                                    kT[pb_h:pb_h + HD, dt_h * T + kb * 128: dt_h * T + (kb + 1) * 128],
                                    qT[pb_h:pb_h + HD, dt_h * T + qc * CH + col0: dt_h * T + (qc + 1) * CH],
                                    start=True, stop=True, tile_position=(pb_h, 0))
                                sc_tok[hp] = ctx.inc(mm, "pe")
                            for hp in range(2):
                                ctx.waits(s, [sc_tok[hp], e_slot[hp][kb]])
                                ex_tok = s.activation(out=E[hp][:, kb * CH: kb * CH + ncols],
                                                      in_=ps_s[hp][sb][:, col0:CH],
                                                      func=AF.Exp, scale=0.125)
                                s_free[hp][sb] = ex_tok
                                e_rdy[hp][kb] = ex_tok
                                if jj >= 0:
                                    ctx.waits(v, [ex_tok])
                                    e_rdy[hp][kb] = v.tensor_tensor(
                                        out=E[hp][:, kb * CH: kb * CH + 128],
                                        in0=E[hp][:, kb * CH: kb * CH + 128],
                                        in1=CONST["trimask"][:], op=ALU.mult)
                        av_tok = [None, None]
                        for hp in range(2):
                            h = 2 * j + hp
                            ctx.waits(t, e_rdy[hp] + [av_free[hp]])
                            for kb in range(nkb):
                                jj = kb - qc * 4
                                col0 = 0 if jj < 0 else jj * 128
                                ncols = CH - col0
                                base = kb * HEADS * (HD + 1)
                                mm = t.matmul(
                                    ps_av[hp][0:HD + 1, col0:CH],
                                    vaug[:, base + h * (HD + 1): base + (h + 1) * (HD + 1)],
                                    E[hp][:, kb * CH: kb * CH + ncols],
                                    start=(kb == 0), stop=(kb == nkb - 1))
                            av_tok[hp] = ctx.inc(mm, "pe")
                            for kb in range(nkb):
                                e_slot[hp][kb] = av_tok[hp]
                        for hp in range(2):
                            h = 2 * j + hp
                            pb_h = 64 * hp
                            ctx.waits(s, [av_tok[hp], un_free[hp]])
                            s.activation(out=rd[hp][:], in_=ps_av[hp][64:65, :], func=AF.Ln)
                            rc_tok = s.activation(out=rd[hp][:], in_=rd[hp][:], func=AF.Exp, scale=-1.0)
                            un_tok = s.activation(out=un[hp][:], in_=ps_av[hp][0:HD, :], func=AF.Identity)
                            av_free[hp] = un_tok
                            ctx.waits(t, [rc_tok, bc_free])
                            mm = t.matmul(ps_bc[:], CONST["onesrow"][0:1, 0:HD], rd[hp][:],
                                          start=True, stop=True)
                            bc_tok = ctx.inc(mm, "pe")
                            ctx.waits(v, [bc_tok, un_tok])
                            nm_tok = v.tensor_tensor(
                                out=attn_o[pb_h:pb_h + HD, dt_h * T + qc * CH:dt_h * T + (qc + 1) * CH],
                                in0=un[hp][:], in1=ps_bc[:], op=ALU.mult)
                            bc_free = nm_tok
                            un_free[hp] = nm_tok
                ph["attn_done"] = [bc_free]

        with contextlib.ExitStack() as es4:
            S4 = lambda n, sh, dt: es4.enter_context(nc.sbuf_tensor(n, sh, dt))
            P4 = lambda n, sh: es4.enter_context(nc.psum_tensor(n, sh, F32))
            wo = S4("A_wo", [128, NDT * D], BF16)
            Y = S4("A_Y", [128, NDT * T], F32)
            sq = S4("A_sq2", [128, NDT * CH], BF16)
            rows = LNRows(S4, "A_r2")
            xr = [S4("A_xr0", [128, CH], F32), S4("A_xr1", [128, CH], F32)]
            ost = [S4("A_ost0", [128, CH], F32), S4("A_ost1", [128, CH], F32)]
            ps_mm = [P4("A_pmm2", [128, CH]), P4("A_pmm3", [128, CH])]
            ps_mean = P4("A_psm2", [1, CH])
            ps_msq = P4("A_psq2", [1, CH])
            ps_bc1 = P4("A_psb3", [128, CH])
            ps_bc2 = P4("A_psb4", [128, CH])

            ph["bar3"] = ctx.bar()
            wtok = [load_wcast(ctx, W["woT"], wo[:], "qwA0", pre=ctx.bar())]

            evac = {}
            for ot in range(NDT):
                for c in range(NCH):
                    pb = (ot * NCH + c) % 2
                    ctx.waits(t, [evac.get(pb)] + wtok + ph["attn_done"] + ph["bar3"])
                    for dt in range(NDT):
                        mm = t.matmul(ps_mm[pb][:],
                                      wo[:, dt * D + ot * 128: dt * D + (ot + 1) * 128],
                                      attn_o[:, dt * T + c * CH:dt * T + (c + 1) * CH],
                                      start=(dt == 0), stop=(dt == NDT - 1))
                    pt = ctx.inc(mm, "pe")
                    ctx.wait(s, pt)
                    evac[pb] = s.activation(out=Y[:, ot * T + c * CH: ot * T + (c + 1) * CH],
                                            in_=ps_mm[pb][:], func=AF.Identity)
            ytok = [evac.get(0), evac.get(1)]

            psum_free = {}
            bc_free = []
            st_done = {0: None, 1: None}
            for c in range(NCH):
                ytiles = [Y[:, dt * T + c * CH:dt * T + (c + 1) * CH] for dt in range(NDT)]
                st_tok = emit_ln_stats_chunk(ctx, ytiles, ps_mean, ps_msq, sq,
                                             CONST, [ytok] * NDT, psum_free)
                rdy = emit_ln_rowmath(ctx, ps_mean, ps_msq, rows, [st_tok])
                psum_free["mean"] = rdy
                psum_free["msq"] = rdy
                bt = emit_bcast2(ctx, CONST, rows.rstd, rows.b, ps_bc1, ps_bc2, rdy, bc_free)
                bc_free = []
                for dt in range(NDT):
                    b = dt % 2
                    xtok = emit_dma_waited(
                        ctx, y, xr[b][:], xin[dt * 128:(dt + 1) * 128, c * CH:(c + 1) * CH],
                        chan=f"qr{b}", pre=[st_done[b]] + (ph["bar3"] if (c == 0 and dt < 2) else []))
                    ctx.waits(v, [bt, st_done[b]])
                    v.tensor_tensor(out=ost[b][:], in0=Y[:, dt * T + c * CH:dt * T + (c + 1) * CH],
                                    in1=ps_bc1[:], op=ALU.mult)
                    v.tensor_tensor(out=ost[b][:], in0=ost[b][:], in1=ps_bc2[:], op=ALU.add)
                    ctx.waits(v, [xtok])
                    vtok = v.tensor_tensor(out=ost[b][:], in0=ost[b][:], in1=xr[b][:], op=ALU.add)
                    if dt == NDT - 1:
                        bc_free = [vtok]
                    ctx.wait(y, vtok)
                    dd = y.dma_start(out=xout[dt * 128:(dt + 1) * 128, c * CH:(c + 1) * CH],
                                     in_=ost[b][:])
                    st_done[b] = ctx.inc(dd, f"qo{b}", 16)
            return [st_done[0], st_done[1]]




NKV = 4
FFN = 2048
NEXP = 8


def emit_rms_apply(ctx, src_full, dst_full, w_pp, CONST, rows, sq, ps_msq,
                   ps_bc1, pre_toks, psum_free, bc_free):
    v = ctx.v
    last = None
    for c in range(NCH):
        tiles = [src_full[:, dt * T + c * CH:dt * T + (c + 1) * CH] for dt in range(NDT)]
        st_tok = emit_ln_stats_chunk(ctx, tiles, None, ps_msq, sq, CONST,
                                     [pre_toks] * NDT, psum_free, rms=True)
        rdy = emit_ln_rowmath(ctx, None, ps_msq, rows, [st_tok], rms=True)
        psum_free["msq"] = rdy
        bt = emit_bcast1(ctx, CONST, rows.rstd, ps_bc1, rdy, list(bc_free))
        del bc_free[:]
        for dt in range(NDT):
            ctx.waits(v, [bt])
            v.tensor_tensor(out=dst_full[:, dt * T + c * CH:dt * T + (c + 1) * CH],
                            in0=src_full[:, dt * T + c * CH:dt * T + (c + 1) * CH],
                            in1=ps_bc1[:], op=ALU.mult)
            last = v.tensor_scalar(out=dst_full[:, dt * T + c * CH:dt * T + (c + 1) * CH],
                                   in0=dst_full[:, dt * T + c * CH:dt * T + (c + 1) * CH],
                                   scalar1=w_pp[:, dt:dt + 1], scalar2=None, op0=ALU.mult)
        bc_free.append(last)
    return [last]


def emit_ln_residual_store(ctx, Yfull, xin, xout, CONST, alloc, palloc, y_toks, tag="T"):
    v, s, t, g, y = ctx.v, ctx.s, ctx.t, ctx.g, ctx.y
    sq = alloc(tag + "_sq", [128, NDT * CH], BF16)
    rows = LNRows(alloc, tag + "_r")
    xr = [alloc(tag + "_xr0", [128, CH], F32), alloc(tag + "_xr1", [128, CH], F32)]
    ost = [alloc(tag + "_ost0", [128, CH], F32), alloc(tag + "_ost1", [128, CH], F32)]
    ps_mean = palloc(tag + "_psm", [1, CH])
    ps_msq = palloc(tag + "_psq", [1, CH])
    ps_bc1 = palloc(tag + "_psb1", [128, CH])
    ps_bc2 = palloc(tag + "_psb2", [128, CH])
    bar = ctx.bar()
    psum_free = {}
    bc_free = []
    st_done = {0: None, 1: None}
    for c in range(NCH):
        ytiles = [Yfull[:, dt * T + c * CH:dt * T + (c + 1) * CH] for dt in range(NDT)]
        st_tok = emit_ln_stats_chunk(ctx, ytiles, ps_mean, ps_msq, sq,
                                     CONST, [y_toks] * NDT, psum_free)
        rdy = emit_ln_rowmath(ctx, ps_mean, ps_msq, rows, [st_tok])
        psum_free["mean"] = rdy
        psum_free["msq"] = rdy
        bt = emit_bcast2(ctx, CONST, rows.rstd, rows.b, ps_bc1, ps_bc2, rdy, bc_free)
        bc_free = []
        for dt in range(NDT):
            b = dt % 2
            xtok = emit_dma_waited(
                ctx, y, xr[b][:], xin[dt * 128:(dt + 1) * 128, c * CH:(c + 1) * CH],
                chan=f"qr{b}", pre=[st_done[b]] + (bar if (c == 0 and dt < 2) else []))
            ctx.waits(v, [bt, st_done[b]])
            v.tensor_tensor(out=ost[b][:], in0=Yfull[:, dt * T + c * CH:dt * T + (c + 1) * CH],
                            in1=ps_bc1[:], op=ALU.mult)
            v.tensor_tensor(out=ost[b][:], in0=ost[b][:], in1=ps_bc2[:], op=ALU.add)
            ctx.waits(v, [xtok])
            vtok = v.tensor_tensor(out=ost[b][:], in0=ost[b][:], in1=xr[b][:], op=ALU.add)
            if dt == NDT - 1:
                bc_free = [vtok]
            ctx.wait(y, vtok)
            dd = y.dma_start(out=xout[dt * 128:(dt + 1) * 128, c * CH:(c + 1) * CH],
                             in_=ost[b][:])
            st_done[b] = ctx.inc(dd, f"qo{b}", 16)
    return [st_done[0], st_done[1]]


def phase_l0ffn(ctx, xin, xout, W, CONST, start_toks=()):
    entry = ctx.bar() + list(start_toks)
    for _eng in (ctx.t, ctx.v, ctx.s, ctx.g, ctx.y):
        ctx.waits(_eng, entry)
    nc = ctx.nc
    v, s, t, g, y = ctx.v, ctx.s, ctx.t, ctx.g, ctx.y

    with contextlib.ExitStack() as es0:
        S0 = lambda n, sh, dt: es0.enter_context(nc.sbuf_tensor(n, sh, dt))
        R = S0("B_R", [128, NDT * T], F32)
        stage_toks_g = {}

        esq_ = contextlib.ExitStack()
        Sq = lambda n, sh, dt: esq_.enter_context(nc.sbuf_tensor(n, sh, dt))
        qT = Sq("B_qT", [128, NDT * T], BF16)
        kT = Sq("B_kT", [128, 2 * T], BF16)
        vaug = Sq("B_vaug", [128, NTT * NKV * (HD + 1)], BF16)
        esH1 = contextlib.ExitStack()
        H = esH1.enter_context(nc.sbuf_tensor("B_H1", [128, NDT * T], BF16))

        with contextlib.ExitStack() as es1:
            S1 = lambda n, sh, dt: es1.enter_context(nc.sbuf_tensor(n, sh, dt))
            P1 = lambda n, sh: es1.enter_context(nc.psum_tensor(n, sh, F32))
            xc = S1("B_xc", [128, NDT * CH], F32)
            sq = S1("B_sq", [128, NDT * CH], BF16)
            rows = LNRows(S1, "B_r1")
            ps_mean = P1("B_psm", [1, CH])
            ps_msq = P1("B_psq", [1, CH])
            ps_bc1 = P1("B_psb1", [128, CH])
            ps_bc2 = P1("B_psb2", [128, CH])
            xld = [None] * NDT
            rc_tok = [None] * NDT
            psum_free = {}
            bc_free = []
            for c in range(NCH):
                for dt in range(NDT):
                    xld[dt] = emit_dma_waited(
                        ctx, y, xc[:, dt * CH:(dt + 1) * CH],
                        xin[dt * 128:(dt + 1) * 128, c * CH:(c + 1) * CH],
                        chan=f"qx{dt}",
                        pre=[rc_tok[dt]] + ((ctx.bar() + list(start_toks)) if c == 0 else []))
                xtiles = [xc[:, dt * CH:(dt + 1) * CH] for dt in range(NDT)]
                st_tok = emit_ln_stats_chunk(ctx, xtiles, ps_mean, ps_msq, sq,
                                             CONST, [[x] for x in xld], psum_free)
                rdy = emit_ln_rowmath(ctx, ps_mean, ps_msq, rows, [st_tok])
                psum_free["mean"] = rdy
                psum_free["msq"] = rdy
                bt = emit_bcast2(ctx, CONST, rows.rstd, rows.b, ps_bc1, ps_bc2, rdy, bc_free)
                for dt in range(NDT):
                    ctx.waits(v, [bt])
                    v.tensor_tensor(out=R[:, dt * T + c * CH:dt * T + (c + 1) * CH],
                                    in0=xc[:, dt * CH:(dt + 1) * CH], in1=ps_bc1[:], op=ALU.mult)
                    rc_tok[dt] = v.tensor_tensor(
                        out=R[:, dt * T + c * CH:dt * T + (c + 1) * CH],
                        in0=R[:, dt * T + c * CH:dt * T + (c + 1) * CH],
                        in1=ps_bc2[:], op=ALU.add)
                bc_free = [rc_tok[NDT - 1]]
            r_tok = [rc_tok[NDT - 1]]

        with contextlib.ExitStack() as es2:
            S2 = lambda n, sh, dt: es2.enter_context(nc.sbuf_tensor(n, sh, dt))
            P2 = lambda n, sh: es2.enter_context(nc.psum_tensor(n, sh, F32))
            sq = S2("B_sq2", [128, NDT * CH], BF16)
            rows = LNRows(S2, "B_r2")
            ln1 = S2("B_ln1", [128, NDT], F32)
            ps_msq = P2("B_psq2", [1, CH])
            ps_bc1 = P2("B_psb3", [128, CH])
            lt = emit_dma_waited(ctx, y, ln1[:], W["ln1_pp"], chan="qv0", pre=ctx.bar())
            ctx.waits(v, [lt])
            h_tok = emit_rms_apply(ctx, R, H, ln1, CONST, rows, sq, ps_msq, ps_bc1,
                                   r_tok, {}, [])

        if True:
            with contextlib.ExitStack() as es3:
                S3 = lambda n, sh, dt: es3.enter_context(nc.sbuf_tensor(n, sh, dt))
                P3 = lambda n, sh, *a: es3.enter_context(nc.psum_tensor(n, sh, *(a or (F32,))))
                wbufs = [S3("B_wqkv0", [128, NDT * 768], BF16), S3("B_wqkv1", [128, NDT * 768], BF16)]
                vT_s = [S3("B_vts0", [128, CH], BF16), S3("B_vts1", [128, CH], BF16)]
                ps_mm = [P3("B_pmm0", [128, CH]), P3("B_pmm1", [128, CH])]
                ps_tr = [P3("B_ptr0", [128, 128], BF16), P3("B_ptr1", [128, 128], BF16)]

                ones_tok = g.memset(
                    vaug[:].rearrange("p (a b) -> p a b", b=HD + 1)[:, :, HD:HD + 1], 1.0)
                evac_toks = {}
                vts_tok = [None, None]
                tr_cons = [None, None]
                wtok = []
                w_users = []
                for ot in range(12):
                    if ot % 6 == 0:
                        half = ot // 6
                        wbuf = wbufs[half]
                        wtok = [load_wcast(
                            ctx, W["wqkvT"][:, half * 6144:(half + 1) * 6144], wbuf[:], f"qwB{half}",
                            pre=(ctx.bar() if ot == 0 else []))]
                    oth = ot % 6
                    for c in range(NCH):
                        pb = (ot * NCH + c) % 2
                        ps = ps_mm[pb]
                        ctx.waits(t, [evac_toks.get(pb)] + h_tok + wtok)
                        for dt in range(NDT):
                            mm = t.matmul(ps[:], wbuf[:, dt * 768 + oth * 128: dt * 768 + (oth + 1) * 128],
                                          H[:, dt * T + c * CH:dt * T + (c + 1) * CH],
                                          start=(dt == 0), stop=(dt == NDT - 1))
                        pt = ctx.inc(mm, "pe")
                        if ot < 8:
                            ctx.wait(v, pt)
                            evac_toks[pb] = v.tensor_copy(
                                out=qT[:, ot * T + c * CH: ot * T + (c + 1) * CH], in_=ps[:])
                        elif ot < 10:
                            ctx.wait(v, pt)
                            evac_toks[pb] = v.tensor_copy(
                                out=kT[:, (ot - 8) * T + c * CH: (ot - 8) * T + (c + 1) * CH], in_=ps[:])
                        else:
                            ctx.wait(s, pt)
                            vb = pb
                            ctx.waits(s, [vts_tok[vb]])
                            ev_tok = s.activation(out=vT_s[vb][:], in_=ps[:], func=AF.Identity)
                            evac_toks[pb] = ev_tok
                            kv0 = 2 * (ot - 10)
                            for tj in range(4):
                                tt = c * 4 + tj
                                trb = tj % 2
                                ctx.waits(t, [ev_tok, tr_cons[trb], ones_tok])
                                tr = t.transpose(ps_tr[trb][:], vT_s[vb][:, tj * 128:(tj + 1) * 128],
                                                 CONST["identb"][:])
                                ptok = ctx.inc(tr, "pe")
                                ctx.wait(s, ptok)
                                base = tt * NKV * (HD + 1)
                                s.activation(out=vaug[:, base + kv0 * (HD + 1): base + kv0 * (HD + 1) + HD],
                                             in_=ps_tr[trb][:, 0:HD], func=AF.Identity)
                                tr_cons[trb] = s.activation(
                                    out=vaug[:, base + (kv0 + 1) * (HD + 1): base + (kv0 + 1) * (HD + 1) + HD],
                                    in_=ps_tr[trb][:, HD:128], func=AF.Identity)
                                vts_tok[vb] = tr_cons[trb]
                qkv_done = [tr_cons[0], tr_cons[1], ones_tok,
                            evac_toks.get(0), evac_toks.get(1)]

            esH1.close()
            attn_o = Sq("B_attnO", [128, NDT * T], BF16)

            with contextlib.ExitStack() as es4:
                S4 = lambda n, sh, dt: es4.enter_context(nc.sbuf_tensor(n, sh, dt))
                cosf = S4("B_cos", [128, T], BF16)
                sinf = S4("B_sin", [128, T], BF16)
                tmp = [S4(f"B_rt{i}", [128, CH], F32) for i in range(3)]
                bar4 = ctx.bar()
                c1 = load_wcast(ctx, W["cosF"], cosf[:], "qwB", pre=bar4)
                c2 = load_wcast(ctx, W["sinF"], sinf[:], "qwB", pre=[])
                ctx.waits(v, [c1, c2] + qkv_done)
                pairs = [(kT, 0, kT, T)] + [(qT, p * T, qT, (4 + p) * T) for p in range(4)]
                rope_pair_tok = []
                for (b1, o1, b2, o2) in pairs:
                    rope_last = None
                    for c in range(NCH):
                        sl = slice(c * CH, (c + 1) * CH)
                        a1 = b1[:, o1 + c * CH: o1 + (c + 1) * CH]
                        a2 = b2[:, o2 + c * CH: o2 + (c + 1) * CH]
                        v.tensor_tensor(out=tmp[0][:], in0=a1, in1=sinf[:, sl], op=ALU.mult)
                        v.tensor_tensor(out=tmp[1][:], in0=a1, in1=cosf[:, sl], op=ALU.mult)
                        v.tensor_tensor(out=tmp[2][:], in0=a2, in1=sinf[:, sl], op=ALU.mult)
                        v.tensor_tensor(out=a1, in0=tmp[1][:], in1=tmp[2][:], op=ALU.subtract)
                        v.tensor_tensor(out=tmp[1][:], in0=a2, in1=cosf[:, sl], op=ALU.mult)
                        rope_last = v.tensor_tensor(out=a2, in0=tmp[1][:], in1=tmp[0][:], op=ALU.add)
                    rope_pair_tok.append(rope_last)
                rope_tok = [rope_pair_tok[-1]]

            with contextlib.ExitStack() as es5:
                S5 = lambda n, sh, dt: es5.enter_context(nc.sbuf_tensor(n, sh, dt))
                P5 = lambda n, sh: es5.enter_context(nc.psum_tensor(n, sh, F32))
                E = [S5("B_E0", [128, NTT * CH], BF16), S5("B_E1", [128, NTT * CH], BF16)]
                unb = S5("B_un", [128, CH], BF16)
                un = [unb[0:64, :], unb[64:128, :]]
                rd = [S5("B_rd0", [1, CH], BF16), S5("B_rd1", [1, CH], BF16)]
                ps_s = [[P5("B_pss00", [128, CH]), P5("B_pss01", [128, CH])],
                        [P5("B_pss10", [128, CH]), P5("B_pss11", [128, CH])]]
                ps_av = [P5("B_pav0", [128, CH]), P5("B_pav1", [128, CH])]
                ps_bc = P5("B_pbc", [HD, CH])

                bar5 = ctx.bar()
                e_slot = [[None] * NTT, [None] * NTT]
                s_free = [[None, None], [None, None]]
                av_free = [None, None]
                bc_free = None
                un_free = [None, None]
                for hpair in range(HEADS // 2):
                    h0 = 2 * hpair
                    kvh = h0 // 4
                    qt1 = h0 // 4
                    ctx.waits(t, [rope_pair_tok[0], rope_pair_tok[1 + qt1]] +
                              (bar5 if hpair == 0 else []))
                    for qc in range(NCH):
                        nkb = qc * 4 + 4
                        e_rdy = [[None] * nkb, [None] * nkb]
                        for kb in range(nkb):
                            sb = kb % 2
                            jj = kb - qc * 4
                            col0 = 0 if jj < 0 else jj * 128
                            ncols = CH - col0
                            sc_tok = [None, None]
                            for hp in range(2):
                                h = h0 + hp
                                pb_h = 32 * (h % 4)
                                tp = (pb_h, 0)
                                ctx.waits(t, [s_free[hp][sb]])
                                t.matmul(
                                    ps_s[hp][sb][:, col0:CH],
                                    kT[pb_h:pb_h + 32, kb * 128:(kb + 1) * 128],
                                    qT[pb_h:pb_h + 32, qt1 * T + qc * CH + col0: qt1 * T + (qc + 1) * CH],
                                    start=True, stop=False, tile_position=tp)
                                mm = t.matmul(
                                    ps_s[hp][sb][:, col0:CH],
                                    kT[pb_h:pb_h + 32, T + kb * 128: T + (kb + 1) * 128],
                                    qT[pb_h:pb_h + 32, (4 + qt1) * T + qc * CH + col0: (4 + qt1) * T + (qc + 1) * CH],
                                    start=False, stop=True, tile_position=tp)
                                sc_tok[hp] = ctx.inc(mm, "pe")
                            for hp in range(2):
                                ctx.waits(s, [sc_tok[hp], e_slot[hp][kb]])
                                ex_tok = s.activation(out=E[hp][:, kb * CH: kb * CH + ncols],
                                                      in_=ps_s[hp][sb][:, col0:CH],
                                                      func=AF.Exp, scale=0.125)
                                s_free[hp][sb] = ex_tok
                                e_rdy[hp][kb] = ex_tok
                                if jj >= 0:
                                    ctx.waits(v, [ex_tok])
                                    e_rdy[hp][kb] = v.tensor_tensor(
                                        out=E[hp][:, kb * CH: kb * CH + 128],
                                        in0=E[hp][:, kb * CH: kb * CH + 128],
                                        in1=CONST["trimask"][:], op=ALU.mult)
                        av_tok = [None, None]
                        for hp in range(2):
                            ctx.waits(t, e_rdy[hp] + [av_free[hp]])
                            for kb in range(nkb):
                                jj = kb - qc * 4
                                col0 = 0 if jj < 0 else jj * 128
                                ncols = CH - col0
                                base = kb * NKV * (HD + 1)
                                mm = t.matmul(
                                    ps_av[hp][0:HD + 1, col0:CH],
                                    vaug[:, base + kvh * (HD + 1): base + (kvh + 1) * (HD + 1)],
                                    E[hp][:, kb * CH: kb * CH + ncols],
                                    start=(kb == 0), stop=(kb == nkb - 1))
                            av_tok[hp] = ctx.inc(mm, "pe")
                            for kb in range(nkb):
                                e_slot[hp][kb] = av_tok[hp]
                        for hp in range(2):
                            h = h0 + hp
                            ctx.waits(s, [av_tok[hp], un_free[hp]])
                            s.activation(out=rd[hp][:], in_=ps_av[hp][64:65, :], func=AF.Ln)
                            rc_tok2 = s.activation(out=rd[hp][:], in_=rd[hp][:], func=AF.Exp, scale=-1.0)
                            un_tok = s.activation(out=un[hp], in_=ps_av[hp][0:HD, :], func=AF.Identity)
                            av_free[hp] = un_tok
                            ctx.waits(t, [rc_tok2, bc_free])
                            mm = t.matmul(ps_bc[:], CONST["onesrow_b"][0:1, 0:HD], rd[hp][:],
                                          start=True, stop=True)
                            bc_tok = ctx.inc(mm, "pe")
                            ctx.waits(v, [bc_tok, un_tok])
                            nm_tok = v.tensor_tensor(
                                out=attn_o[64 * (h % 2):64 * (h % 2) + HD,
                                           (h // 2) * T + qc * CH:(h // 2) * T + (qc + 1) * CH],
                                in0=un[hp], in1=ps_bc[:], op=ALU.mult)
                            bc_free = nm_tok
                            un_free[hp] = nm_tok
                attn_done = [bc_free]

            with contextlib.ExitStack() as es6:
                S6 = lambda n, sh, dt: es6.enter_context(nc.sbuf_tensor(n, sh, dt))
                P6 = lambda n, sh: es6.enter_context(nc.psum_tensor(n, sh, F32))
                wo = S6("B_wo", [128, NDT * D], BF16)
                ps_mm = [P6("B_pmm2", [128, CH]), P6("B_pmm3", [128, CH])]
                wtok = [load_wcast(ctx, W["woT"], wo[:], "qwB", pre=ctx.bar())]
                evac = {}
                bar6 = ctx.bar()
                radd = None
                for ot in range(NDT):
                    for c in range(NCH):
                        pb = (ot * NCH + c) % 2
                        ctx.waits(t, [evac.get(pb)] + wtok + attn_done + bar6)
                        for dt in range(NDT):
                            mm = t.matmul(ps_mm[pb][:],
                                          wo[:, dt * D + ot * 128: dt * D + (ot + 1) * 128],
                                          attn_o[:, dt * T + c * CH:dt * T + (c + 1) * CH],
                                          start=(dt == 0), stop=(dt == NDT - 1))
                        pt = ctx.inc(mm, "pe")
                        ctx.waits(v, [pt])
                        radd = v.tensor_tensor(
                            out=R[:, ot * T + c * CH: ot * T + (c + 1) * CH],
                            in0=R[:, ot * T + c * CH: ot * T + (c + 1) * CH],
                            in1=ps_mm[pb][:], op=ALU.add)
                        evac[pb] = radd
                r2_tok = [radd]

        esq_.close()
        esH2 = contextlib.ExitStack()
        H = esH2.enter_context(nc.sbuf_tensor("B_H2", [128, NDT * T], BF16))

        with contextlib.ExitStack() as es7:
            S7 = lambda n, sh, dt: es7.enter_context(nc.sbuf_tensor(n, sh, dt))
            P7 = lambda n, sh: es7.enter_context(nc.psum_tensor(n, sh, F32))
            sq = S7("B_sq7", [128, NDT * CH], BF16)
            rows = LNRows(S7, "B_r7")
            ln2 = S7("B_ln2", [128, NDT], F32)
            ps_msq = P7("B_psq7", [1, CH])
            ps_bc1 = P7("B_psb7", [128, CH])
            lt = emit_dma_waited(ctx, y, ln2[:], W["ln2_pp"], chan="qv0", pre=ctx.bar())
            ctx.waits(v, [lt])
            h2_tok = emit_rms_apply(ctx, R, H, ln2, CONST, rows, sq, ps_msq, ps_bc1,
                                    r2_tok, {}, [])

        wf_dram = nc.dram_tensor("B_wf_dram", [NEXP, T], F32, kind="Internal").ap()
        esM = contextlib.ExitStack()
        wfT = esM.enter_context(nc.sbuf_tensor("B_wfT", [NEXP, T], F32))

        with contextlib.ExitStack() as es8:
            S8 = lambda n, sh, dt: es8.enter_context(nc.sbuf_tensor(n, sh, dt))
            P8 = lambda n, sh: es8.enter_context(nc.psum_tensor(n, sh, F32))
            gw = S8("B_gw", [128, NDT * NEXP], BF16)
            logitT = S8("B_logT", [NEXP, T], F32)
            ltm = S8("B_ltm", [128, NTT * NEXP], F32)
            prob = S8("B_prob", [128, NEXP], F32)
            eq1 = S8("B_eq1", [128, NEXP], F32)
            m1 = S8("B_m1", [128, 1], F32)
            m2 = S8("B_m2", [128, 1], F32)
            ws = S8("B_ws", [128, 1], F32)
            ps_g = P8("B_psg", [NEXP, CH])
            ps_t = [P8("B_pst0", [128, 128]), P8("B_pst1", [128, 128])]
            gt = [load_wcast(ctx, W["gate_wT"], gw[:], "qwB", pre=ctx.bar())]
            glast = None
            for c in range(NCH):
                ctx.waits(t, gt + [glast] + h2_tok)
                for dt in range(NDT):
                    mm = t.matmul(ps_g[:], gw[:, dt * NEXP:(dt + 1) * NEXP],
                                  H[:, dt * T + c * CH:dt * T + (c + 1) * CH],
                                  start=(dt == 0), stop=(dt == NDT - 1))
                pt = ctx.inc(mm, "pe")
                ctx.wait(s, pt)
                glast = s.activation(out=logitT[:, c * CH:(c + 1) * CH], in_=ps_g[:],
                                     func=AF.Identity)
            tr_done = [None, None]
            tm_tok = [None] * NTT
            for tt in range(NTT):
                b = tt % 2
                ctx.waits(t, [glast, tr_done[b]])
                tr = t.transpose(ps_t[b][0:128, 0:NEXP], logitT[:, tt * 128:(tt + 1) * 128],
                                 CONST["ident"][0:NEXP, 0:NEXP])
                ptok = ctx.inc(tr, "pe")
                ctx.wait(s, ptok)
                tm_tok[tt] = s.activation(out=ltm[:, tt * NEXP:(tt + 1) * NEXP],
                                          in_=ps_t[b][0:128, 0:NEXP], func=AF.Identity)
                tr_done[b] = tm_tok[tt]
            for tt in range(NTT):
                sl = ltm[:, tt * NEXP:(tt + 1) * NEXP]
                ctx.waits(v, [tm_tok[tt]])
                v.tensor_reduce(out=m1[:], in_=sl, axis=AX.X, op=ALU.max)
                tv = v.tensor_scalar(out=prob[:], in0=sl, scalar1=m1[:], scalar2=None,
                                     op0=ALU.subtract)
                ctx.waits(s, [tv])
                et = s.activation(out=prob[:], in_=prob[:], func=AF.Exp)
                ctx.waits(v, [et])
                v.tensor_reduce(out=ws[:], in_=prob[:], axis=AX.X, op=ALU.add)
                v.reciprocal(out=ws[:], in_=ws[:])
                v.tensor_scalar(out=prob[:], in0=prob[:], scalar1=ws[:], scalar2=None, op0=ALU.mult)
                v.tensor_reduce(out=m1[:], in_=prob[:], axis=AX.X, op=ALU.max)
                v.tensor_scalar(out=eq1[:], in0=prob[:], scalar1=m1[:], scalar2=None, op0=ALU.is_ge)
                v.scalar_tensor_tensor(out=prob[:], in0=eq1[:], scalar=-1.0e30, in1=prob[:],
                                       op0=ALU.mult, op1=ALU.add)
                v.tensor_reduce(out=m2[:], in_=prob[:], axis=AX.X, op=ALU.max)
                v.tensor_scalar(out=prob[:], in0=prob[:], scalar1=m2[:], scalar2=None, op0=ALU.is_ge)
                v.tensor_tensor(out=ws[:], in0=m1[:], in1=m2[:], op=ALU.add)
                v.reciprocal(out=ws[:], in_=ws[:])
                v.tensor_tensor(out=m1[:], in0=m1[:], in1=ws[:], op=ALU.mult)
                v.tensor_tensor(out=m2[:], in0=m2[:], in1=ws[:], op=ALU.mult)
                v.tensor_scalar(out=eq1[:], in0=eq1[:], scalar1=m1[:], scalar2=None, op0=ALU.mult)
                v.scalar_tensor_tensor(out=sl, in0=prob[:], scalar=m2[:], in1=eq1[:],
                                       op0=ALU.mult, op1=ALU.add)
            tr_done = [None, None]
            wf_last = None
            for tt in range(NTT):
                b = tt % 2
                ctx.waits(t, [("dve", ctx.counts["dve"]), tr_done[b]])
                tr = t.transpose(ps_t[b][0:NEXP, 0:128], ltm[:, tt * NEXP:(tt + 1) * NEXP],
                                 CONST["ident"][:])
                ptok = ctx.inc(tr, "pe")
                ctx.wait(s, ptok)
                wf_last = s.activation(out=wfT[:, tt * 128:(tt + 1) * 128],
                                       in_=ps_t[b][0:NEXP, 0:128], func=AF.Identity)
                tr_done[b] = wf_last
            ctx.waits(y, [wf_last])
            wfd = y.dma_start(out=wf_dram, in_=wfT[:])
            wf_tok = [ctx.inc(wfd, "qwf", 16)]

        esM.close()

        esA = contextlib.ExitStack()
        SA = lambda n, sh, dt: esA.enter_context(nc.sbuf_tensor(n, sh, dt))
        PA = lambda n, sh: esA.enter_context(nc.psum_tensor(n, sh, F32))
        g_all = SA("B_gall", [128, NTT * 1024], BF16)
        w1s = [SA("B_w1s0", [128, NDT * CH], BF16), SA("B_w1s1", [128, NDT * CH], BF16)]
        w3s = [SA("B_w3s0", [128, NDT * CH], BF16), SA("B_w3s1", [128, NDT * CH], BF16)]
        _gs = SA("B_gsil0", [128, CH], F32)
        gsil = [_gs, _gs]
        w2b = SA("B_w2b", [128, NTT * 1024], BF16)
        wf_row = SA("B_wfrow", [1, CH], F32)
        wb_sb = SA("B_wb", [128, CH], F32)
        ps_g1 = PA("B_psg1", [128, CH])
        ps_g3 = PA("B_psg3", [128, CH])
        ps_o = [PA("B_pso0", [128, CH]), PA("B_pso1", [128, CH])]
        ps_wb = PA("B_pswb", [128, CH])
        tmpo = gsil[0]
        wf_guard = [None]
        radd = None
        b_done = None
        slab_guard = [None, None]
        gs_tok = [None, None]
        g1_free = None
        g3_free = None
        evac = {}
        wb_free = None
        w2_users = None
        bar9 = ctx.bar()
        for e in range(NEXP):
            if True:
                pre_toks = {}
                for slab in range(2):
                    sb = slab % 2
                    pre_toks[(1, slab)] = load_wcast(
                        ctx, W["w1T"][e * 128:(e + 1) * 128, slab * 4096:(slab + 1) * 4096],
                        w1s[sb][:], f"qs1{sb}",
                        pre=[slab_guard[sb]] + (bar9 if e == 0 else []))
                    pre_toks[(3, slab)] = load_wcast(
                        ctx, W["w3T"][e * 128:(e + 1) * 128, slab * 4096:(slab + 1) * 4096],
                        w3s[sb][:], f"qs3{sb}",
                        pre=[slab_guard[sb]] + (bar9 if e == 0 else []))
                w2tok = [load_wcast(ctx, W["w2T"][e * 128:(e + 1) * 128, :], w2b[:], "qw2B",
                                    pre=[b_done, w2_users] + (bar9 if e == 0 else []))]
                for half in range(2):
                    for slab in range(4):
                        sb = slab % 2
                        if (1, slab) in pre_toks:
                            t1 = pre_toks.pop((1, slab))
                            t3 = pre_toks.pop((3, slab))
                        else:
                            t1 = load_wcast(
                                ctx, W["w1T"][e * 128:(e + 1) * 128, slab * 4096:(slab + 1) * 4096],
                                w1s[sb][:], f"qs1{sb}", pre=[slab_guard[sb]])
                            t3 = load_wcast(
                                ctx, W["w3T"][e * 128:(e + 1) * 128, slab * 4096:(slab + 1) * 4096],
                                w3s[sb][:], f"qs3{sb}", pre=[slab_guard[sb]])
                        for fs in range(4):
                            fsg = slab * 4 + fs
                            for ch in range(2):
                                c = half * 2 + ch
                                ctx.waits(t, [t1, t3, g1_free, b_done] + h2_tok)
                                for dt in range(NDT):
                                    mm = t.matmul(
                                        ps_g1[:], w1s[sb][:, dt * CH + fs * 128: dt * CH + (fs + 1) * 128],
                                        H[:, dt * T + c * CH:dt * T + (c + 1) * CH],
                                        start=(dt == 0), stop=(dt == NDT - 1))
                                g1_tok = ctx.inc(mm, "pe")
                                ctx.waits(t, [g3_free])
                                for dt in range(NDT):
                                    mm = t.matmul(
                                        ps_g3[:], w3s[sb][:, dt * CH + fs * 128: dt * CH + (fs + 1) * 128],
                                        H[:, dt * T + c * CH:dt * T + (c + 1) * CH],
                                        start=(dt == 0), stop=(dt == NDT - 1))
                                g3_tok = ctx.inc(mm, "pe")
                                slab_guard[sb] = g3_tok
                                ctx.waits(s, [g1_tok, gs_tok[0]])
                                sil = s.activation(out=gsil[0][:], in_=ps_g1[:], func=AF.Silu)
                                g1_free = sil
                                ctx.waits(v, [sil, g3_tok, b_done])
                                gt2 = v.tensor_tensor(
                                    out=g_all[:, fsg * 1024 + ch * CH: fsg * 1024 + (ch + 1) * CH],
                                    in0=gsil[0][:], in1=ps_g3[:], op=ALU.mult)
                                gs_tok[0] = gt2
                                g3_free = gt2
                    ga_tok = [gs_tok[0], gs_tok[1]]
                    for ch in range(2):
                        c = half * 2 + ch
                        ctx.waits(y, wf_tok + [wf_guard[0]] + (bar9 if (e == 0 and half == 0 and ch == 0) else []))
                        wfr = y.dma_start(out=wf_row[:], in_=wf_dram[e:e + 1, c * CH:(c + 1) * CH])
                        wfr_tok = ctx.inc(wfr, "qwf", 16)
                        ctx.waits(t, [wfr_tok, wb_free])
                        mm = t.matmul(ps_wb[:], CONST["onesrow"][0:1, 0:128],
                                      wf_row[0:1, :], start=True, stop=True)
                        wbt = ctx.inc(mm, "pe")
                        wf_guard[0] = wbt
                        ctx.waits(s, [wbt, wb_free])
                        wb_ev = s.activation(out=wb_sb[:], in_=ps_wb[:], func=AF.Identity)
                        for ot in range(NDT):
                            pb = (ch * NDT + ot) % 2
                            ctx.waits(t, [evac.get(pb)] + ga_tok + w2tok)
                            for ft in range(NTT):
                                mm = t.matmul(
                                    ps_o[pb][:], w2b[:, ft * 1024 + ot * 128: ft * 1024 + (ot + 1) * 128],
                                    g_all[:, ft * 1024 + ch * CH: ft * 1024 + (ch + 1) * CH],
                                    start=(ft == 0), stop=(ft == NTT - 1))
                            pt = ctx.inc(mm, "pe")
                            b_done = pt
                            w2_users = pt
                            ctx.waits(v, [pt, wb_ev])
                            wb_free = v.tensor_tensor(out=tmpo[:], in0=ps_o[pb][:], in1=wb_sb[:], op=ALU.mult)
                            radd = v.tensor_tensor(
                                out=R[:, ot * T + c * CH: ot * T + (c + 1) * CH],
                                in0=R[:, ot * T + c * CH: ot * T + (c + 1) * CH],
                                in1=tmpo[:], op=ALU.add)
                            evac[pb] = radd
                    b_done = radd
        moe_tok = [radd]
        esA.close()
        esH2.close()

        with contextlib.ExitStack() as esT:
            ST = lambda n, sh, dt: esT.enter_context(nc.sbuf_tensor(n, sh, dt))
            PT = lambda n, sh: esT.enter_context(nc.psum_tensor(n, sh, F32))
            return emit_ln_residual_store(ctx, R, xin, xout, CONST, ST, PT, moe_tok, tag="BT")




def phase_l1attn(ctx, xin, xout, W, CONST, start_toks=()):
    entry = ctx.bar() + list(start_toks)
    for _eng in (ctx.t, ctx.v, ctx.s, ctx.g, ctx.y):
        ctx.waits(_eng, entry)
    nc = ctx.nc
    v, s, t, g, y = ctx.v, ctx.s, ctx.t, ctx.g, ctx.y
    NEG = -30.0
    EPS9 = 1e-9

    with contextlib.ExitStack() as es0:
        S0 = lambda n, sh, dt: es0.enter_context(nc.sbuf_tensor(n, sh, dt))
        Ybf = S0("C_Y", [128, NDT * T], BF16)
        vecs = S0("C_vecs", [128, 6 * NDT], F32)
        vt = emit_dma_waited(ctx, y, vecs[:, 0:5 * NDT], W["vecs_pp"], chan="qv0", pre=ctx.bar())
        ctx.waits(s, [vt])
        ew_tok = s.activation(out=vecs[:, 5 * NDT:6 * NDT], in_=vecs[:, 3 * NDT:4 * NDT],
                              func=AF.Exp)
        TMK, TMV, TMR, DEC, FIR, EW = range(6)

        with contextlib.ExitStack() as es1:
            S1 = lambda n, sh, dt: es1.enter_context(nc.sbuf_tensor(n, sh, dt))
            xk = S1("C_xk", [128, NDT * T], BF16)
            xv = S1("C_xv", [128, NDT * T], BF16)
            xr = S1("C_xr", [128, NDT * T], BF16)

            with contextlib.ExitStack() as es2:
                S2 = lambda n, sh, dt: es2.enter_context(nc.sbuf_tensor(n, sh, dt))
                P2 = lambda n, sh: es2.enter_context(nc.psum_tensor(n, sh, F32))
                U = S2("C_U", [128, NDT * T], BF16)
                xc = S2("C_xc", [128, NDT * CH], F32)
                sq = S2("C_sq", [128, NDT * CH], BF16)
                tmpm = S2("C_tmpm", [128, T], F32)
                rows = LNRows(S2, "C_r1")
                ps_mean = P2("C_psm", [1, CH])
                ps_msq = P2("C_psq", [1, CH])
                ps_bc1 = P2("C_psb1", [128, CH])
                ps_bc2 = P2("C_psb2", [128, CH])
                xld = [None] * NDT
                uc_tok = [None] * NDT
                psum_free = {}
                bc_free = []
                for c in range(NCH):
                    for dt in range(NDT):
                        xld[dt] = emit_dma_waited(
                            ctx, y, xc[:, dt * CH:(dt + 1) * CH],
                            xin[dt * 128:(dt + 1) * 128, c * CH:(c + 1) * CH],
                            chan=f"qx{dt}",
                            pre=[uc_tok[dt]] + ((ctx.bar() + list(start_toks)) if c == 0 else []))
                    xtiles = [xc[:, dt * CH:(dt + 1) * CH] for dt in range(NDT)]
                    st_tok = emit_ln_stats_chunk(ctx, xtiles, ps_mean, ps_msq, sq,
                                                 CONST, [[x] for x in xld], psum_free)
                    rdy = emit_ln_rowmath(ctx, ps_mean, ps_msq, rows, [st_tok])
                    psum_free["mean"] = rdy
                    psum_free["msq"] = rdy
                    bt = emit_bcast2(ctx, CONST, rows.rstd, rows.b, ps_bc1, ps_bc2, rdy, bc_free)
                    for dt in range(NDT):
                        ctx.waits(v, [bt])
                        v.tensor_tensor(out=U[:, dt * T + c * CH:dt * T + (c + 1) * CH],
                                        in0=xc[:, dt * CH:(dt + 1) * CH], in1=ps_bc1[:], op=ALU.mult)
                        uc_tok[dt] = v.tensor_tensor(
                            out=U[:, dt * T + c * CH:dt * T + (c + 1) * CH],
                            in0=U[:, dt * T + c * CH:dt * T + (c + 1) * CH],
                            in1=ps_bc2[:], op=ALU.add)
                    bc_free = [uc_tok[NDT - 1]]
                ctx.waits(v, [uc_tok[NDT - 1], ew_tok])
                mx_src = {}
                for dt in range(NDT):
                    sc = vecs[:, TMK * NDT + dt: TMK * NDT + dt + 1]
                    v.tensor_tensor(out=tmpm[:, 1:T], in0=U[:, dt * T + 1: dt * T + T],
                                    in1=U[:, dt * T: dt * T + T - 1], op=ALU.subtract)
                    v.scalar_tensor_tensor(out=xk[:, dt * T + 1: dt * T + T],
                                           in0=tmpm[:, 1:T], scalar=sc,
                                           in1=U[:, dt * T: dt * T + T - 1],
                                           op0=ALU.mult, op1=ALU.add)
                    mx_src[TMK] = v.tensor_scalar(out=xk[:, dt * T: dt * T + 1],
                                                  in0=U[:, dt * T: dt * T + 1],
                                                  scalar1=sc, scalar2=None, op0=ALU.mult)
                for dt in range(NDT):
                    v.tensor_tensor(out=tmpm[:, 1:T], in0=U[:, dt * T + 1: dt * T + T],
                                    in1=U[:, dt * T: dt * T + T - 1], op=ALU.subtract)
                    for (dst, tmi) in ((xv, TMV), (xr, TMR)):
                        sc = vecs[:, tmi * NDT + dt: tmi * NDT + dt + 1]
                        v.scalar_tensor_tensor(out=dst[:, dt * T + 1: dt * T + T],
                                               in0=tmpm[:, 1:T], scalar=sc,
                                               in1=U[:, dt * T: dt * T + T - 1],
                                               op0=ALU.mult, op1=ALU.add)
                        mx_src[tmi] = v.tensor_scalar(out=dst[:, dt * T: dt * T + 1],
                                                      in0=U[:, dt * T: dt * T + 1],
                                                      scalar1=sc, scalar2=None, op0=ALU.mult)
                mx_tok = [mx_src[TMR]]
                mx_by_wi = [mx_src[TMK], mx_src[TMV], mx_src[TMR]]

            with contextlib.ExitStack() as es3:
                S3 = lambda n, sh, dt: es3.enter_context(nc.sbuf_tensor(n, sh, dt))
                P3 = lambda n, sh: es3.enter_context(nc.psum_tensor(n, sh, F32))
                wk = S3("C_wk", [128, NDT * 128], BF16)
                wv = S3("C_wv", [128, NDT * 128], BF16)
                wr = S3("C_wr", [128, NDT * 128], BF16)
                kbufs = [S3("C_kd0", [128, T], F32), S3("C_kd1", [128, T], F32)]
                vbufs = [S3("C_vd0", [128, T], F32), S3("C_vd1", [128, T], F32)]
                ekv = S3("C_ekv", [128, T], F32)
                pst = S3("C_pst", [128, T + 1], F32)
                nst = S3("C_nst", [128, T + 1], F32)
                A1 = S3("C_A1", [128, T], F32)
                rbuf = S3("C_rd", [128, T], BF16)
                ps_mm = [P3("C_pmm0", [128, CH]), P3("C_pmm1", [128, CH])]

                y_tok = None
                prev_y = None
                prev_ecv = None
                kv_consumed = [None, None]
                evac = {}
                w_guard = [None, None, None]
                for dt in range(NDT):
                    idx = dt % 2
                    kb_ = kbufs[idx]
                    vb_ = vbufs[idx]
                    wtok = []
                    for wi_, (wb_, wname) in enumerate(((wk, "wkT"), (wv, "wvT"), (wr, "wrT"))):
                        wtok.append(load_wcast(
                            ctx, W[wname][:, dt * 1024:(dt + 1) * 1024], wb_[:], f"qwC{wi_}",
                            pre=[w_guard[wi_]] + (ctx.bar() if dt == 0 else [])))
                    for wi_, (wb_, src_, dstb) in enumerate(
                            ((wk, xk, kb_), (wv, xv, vb_), (wr, xr, None))):
                        for c in range(NCH):
                            pb = c % 2
                            ctx.waits(t, [evac.get(pb), mx_by_wi[wi_], wtok[wi_]])
                            for di in range(NDT):
                                mm = t.matmul(ps_mm[pb][:],
                                              wb_[:, di * 128:(di + 1) * 128],
                                              src_[:, di * T + c * CH:di * T + (c + 1) * CH],
                                              start=(di == 0), stop=(di == NDT - 1))
                            pt = ctx.inc(mm, "pe")
                            w_guard[wi_] = pt
                            if dstb is not None:
                                ctx.waits(s, [pt, kv_consumed[idx]])
                                evac[pb] = s.activation(out=dstb[:, c * CH:(c + 1) * CH],
                                                        in_=ps_mm[pb][:], func=AF.Identity)
                            else:
                                ctx.waits(s, [pt, prev_y])
                                evac[pb] = s.activation(out=rbuf[:, c * CH:(c + 1) * CH],
                                                        in_=ps_mm[pb][:], func=AF.Sigmoid)
                    kv_tok = [evac.get(0), evac.get(1)]
                    ctx.waits(s, kv_tok + [prev_y, prev_ecv])
                    ek_tok = s.activation(out=ekv[:], in_=kb_[:], func=AF.Exp)
                    ctx.waits(g, [ek_tok])
                    ekv_tok = g.tensor_tensor(out=ekv[:], in0=ekv[:], in1=vb_[:], op=ALU.mult)
                    ctx.waits(v, [ekv_tok])
                    v.memset(pst[:, 0:1], -1.0e30)
                    v.memset(nst[:, 0:1], 0.0)
                    dec_b = vecs[:, DEC * NDT + dt: DEC * NDT + dt + 1].broadcast_to((128, T))
                    ew_b = vecs[:, EW * NDT + dt: EW * NDT + dt + 1].broadcast_to((128, T))
                    u_sc = vecs[:, FIR * NDT + dt: FIR * NDT + dt + 1]
                    v.tensor_tensor_scan(out=pst[:, 1:T + 1], data0=dec_b, data1=kb_[:],
                                         initial=-1.0e30, op0=ALU.add, op1=ALU.max)
                    v.tensor_tensor_scan(out=nst[:, 1:T + 1], data0=ew_b, data1=ekv[:],
                                         initial=0.0, op0=ALU.mult, op1=ALU.add)
                    v.scalar_tensor_tensor(out=A1[:], in0=kb_[:], scalar=u_sc, in1=pst[:, 0:T],
                                           op0=ALU.add, op1=ALU.max)
                    A2 = pst[:, 0:T]
                    v.tensor_tensor(out=A2, in0=pst[:, 0:T], in1=A1[:], op=ALU.subtract)
                    tv = v.tensor_scalar(out=A2, in0=A2, scalar1=-30.0, scalar2=None, op0=ALU.max)
                    ctx.waits(s, [tv])
                    eo_tok = s.activation(out=A2, in_=A2, func=AF.Exp)
                    v.scalar_tensor_tensor(out=ekv[:], in0=kb_[:], scalar=u_sc, in1=A1[:],
                                           op0=ALU.add, op1=ALU.subtract)
                    tv = v.tensor_scalar(out=ekv[:], in0=ekv[:], scalar1=-30.0, scalar2=None, op0=ALU.max)
                    ctx.waits(s, [tv, eo_tok])
                    ec_tok = s.activation(out=ekv[:], in_=ekv[:], func=AF.Exp)
                    em_tok = s.activation(out=A1[:], in_=A1[:], func=AF.Exp, scale=-1.0)
                    ctx.waits(v, [ec_tok, em_tok])
                    tv = v.scalar_tensor_tensor(out=A2, in0=A2, scalar=EPS9, in1=ekv[:],
                                                op0=ALU.add, op1=ALU.add)
                    ctx.waits(g, [tv])
                    ecv_tok = g.tensor_tensor(out=ekv[:], in0=ekv[:], in1=vb_[:], op=ALU.mult)
                    prev_ecv = ecv_tok
                    ctx.waits(s, [tv])
                    s.activation(out=A2, in_=A2, func=AF.Ln)
                    rcp = s.activation(out=A2, in_=A2, func=AF.Exp, scale=-1.0)
                    ctx.waits(v, [rcp])
                    v.tensor_tensor(out=A1[:], in0=A1[:], in1=nst[:, 0:T], op=ALU.mult)
                    ctx.waits(v, [ecv_tok])
                    v.tensor_tensor(out=A1[:], in0=A1[:], in1=ekv[:], op=ALU.add)
                    v.tensor_tensor(out=A1[:], in0=A1[:], in1=A2, op=ALU.mult)
                    y_tok = v.tensor_tensor(out=Ybf[:, dt * T:(dt + 1) * T], in0=A1[:],
                                            in1=rbuf[:], op=ALU.mult)
                    kv_consumed[idx] = y_tok
                    prev_y = y_tok
                y_last = [y_tok]

        with contextlib.ExitStack() as es4:
            S4 = lambda n, sh, dt: es4.enter_context(nc.sbuf_tensor(n, sh, dt))
            P4 = lambda n, sh: es4.enter_context(nc.psum_tensor(n, sh, F32))
            wo = S4("C_wo", [128, NDT * D], BF16)
            Yp = S4("C_Yp", [128, NDT * T], F32)
            ps_mm = [P4("C_pmm2", [128, CH]), P4("C_pmm3", [128, CH])]
            wtok = [load_wcast(ctx, W["woT"], wo[:], "qwC0", pre=ctx.bar())]
            evac = {}
            for ot in range(NDT):
                for c in range(NCH):
                    pb = (ot * NCH + c) % 2
                    ctx.waits(t, [evac.get(pb)] + wtok + y_last)
                    for dt in range(NDT):
                        mm = t.matmul(ps_mm[pb][:],
                                      wo[:, dt * D + ot * 128: dt * D + (ot + 1) * 128],
                                      Ybf[:, dt * T + c * CH:dt * T + (c + 1) * CH],
                                      start=(dt == 0), stop=(dt == NDT - 1))
                    pt = ctx.inc(mm, "pe")
                    ctx.wait(s, pt)
                    evac[pb] = s.activation(out=Yp[:, ot * T + c * CH: ot * T + (c + 1) * CH],
                                            in_=ps_mm[pb][:], func=AF.Identity)
            ytok = [evac.get(0), evac.get(1)]
            with contextlib.ExitStack() as es5:
                S5 = lambda n, sh, dt: es5.enter_context(nc.sbuf_tensor(n, sh, dt))
                P5 = lambda n, sh: es5.enter_context(nc.psum_tensor(n, sh, F32))
                return emit_ln_residual_store(ctx, Yp, xin, xout, CONST, S5, P5, ytok, tag="CT")


def host_prep_C(inputs):
    vecs = np.stack([
        np.asarray(inputs["L1_attn_time_mix_k"]),
        np.asarray(inputs["L1_attn_time_mix_v"]),
        np.asarray(inputs["L1_attn_time_mix_r"]),
        np.asarray(inputs["L1_attn_time_decay"]),
        np.asarray(inputs["L1_attn_time_first"]),
    ])
    vecs_pp = np.concatenate([vec_pp(vv) for vv in vecs], axis=1)
    def ilv_dtblocks(w):
        wT = np.asarray(w).T
        return np.concatenate([ilv(wT[:, dt * 128:(dt + 1) * 128]) for dt in range(NDT)], 1)
    return {
        "wkT": ilv_dtblocks(inputs["L1_attn_W_k"]),
        "wvT": ilv_dtblocks(inputs["L1_attn_W_v"]),
        "wrT": ilv_dtblocks(inputs["L1_attn_W_r"]),
        "woT": ilv(np.asarray(inputs["L1_attn_W_o"]).T),
        "vecs_pp": np.ascontiguousarray(vecs_pp),
    }


def build_phaseC_program():
    nc = bass.Bass("TRN2", target_bir_lowering=False)
    es = contextlib.ExitStack()
    ctx = Ctx(nc, es)
    xin = nc.dram_tensor("xin", [D, T], F32, kind="ExternalInput").ap()
    xout = nc.dram_tensor("xout", [D, T], F32, kind="ExternalOutput").ap()
    W = {wn: nc.dram_tensor(wn, [128, NDT * D], F32, kind="ExternalInput").ap()
         for wn in ("wkT", "wvT", "wrT", "woT")}
    W["vecs_pp"] = nc.dram_tensor("vecs_pp", [128, 5 * NDT], F32, kind="ExternalInput").ap()
    with nc.Block():
        CONST = build_consts(ctx)
        for eng in (nc.tensor, nc.vector, nc.scalar, nc.gpsimd):
            ctx.waits(eng, CONST["_ready"])
        st = phase_l1attn(ctx, xin, xout, W, CONST)
        ctx.waits(nc.gpsimd, st)
    es.close()
    return nc



FD4 = 4096


def phase_l1ffn(ctx, xin, xout, W, CONST, start_toks=()):
    entry = ctx.bar() + list(start_toks)
    for _eng in (ctx.t, ctx.v, ctx.s, ctx.g, ctx.y):
        ctx.waits(_eng, entry)
    nc = ctx.nc
    v, s, t, g, y = ctx.v, ctx.s, ctx.t, ctx.g, ctx.y

    with contextlib.ExitStack() as es0:
        S0 = lambda n, sh, dt: es0.enter_context(nc.sbuf_tensor(n, sh, dt))
        U = S0("D_U", [128, NDT * T], BF16)
        Y = S0("D_Y", [128, NDT * T], F32)

        with contextlib.ExitStack() as es1:
            S1 = lambda n, sh, dt: es1.enter_context(nc.sbuf_tensor(n, sh, dt))
            P1 = lambda n, sh: es1.enter_context(nc.psum_tensor(n, sh, F32))
            xc = S1("D_xc", [128, NDT * CH], F32)
            sq = S1("D_sq", [128, NDT * CH], BF16)
            rows = LNRows(S1, "D_r1")
            ps_mean = P1("D_psm", [1, CH])
            ps_msq = P1("D_psq", [1, CH])
            ps_bc1 = P1("D_psb1", [128, CH])
            ps_bc2 = P1("D_psb2", [128, CH])
            xld = [None] * NDT
            uc_tok = [None] * NDT
            psum_free = {}
            bc_free = []
            for c in range(NCH):
                for dt in range(NDT):
                    xld[dt] = emit_dma_waited(
                        ctx, y, xc[:, dt * CH:(dt + 1) * CH],
                        xin[dt * 128:(dt + 1) * 128, c * CH:(c + 1) * CH],
                        chan=f"qx{dt}",
                        pre=[uc_tok[dt]] + ((ctx.bar() + list(start_toks)) if c == 0 else []))
                xtiles = [xc[:, dt * CH:(dt + 1) * CH] for dt in range(NDT)]
                st_tok = emit_ln_stats_chunk(ctx, xtiles, ps_mean, ps_msq, sq,
                                             CONST, [[x] for x in xld], psum_free)
                rdy = emit_ln_rowmath(ctx, ps_mean, ps_msq, rows, [st_tok])
                psum_free["mean"] = rdy
                psum_free["msq"] = rdy
                bt = emit_bcast2(ctx, CONST, rows.rstd, rows.b, ps_bc1, ps_bc2, rdy, bc_free)
                for dt in range(NDT):
                    ctx.waits(v, [bt])
                    v.tensor_tensor(out=U[:, dt * T + c * CH:dt * T + (c + 1) * CH],
                                    in0=xc[:, dt * CH:(dt + 1) * CH], in1=ps_bc1[:], op=ALU.mult)
                    uc_tok[dt] = v.tensor_tensor(
                        out=U[:, dt * T + c * CH:dt * T + (c + 1) * CH],
                        in0=U[:, dt * T + c * CH:dt * T + (c + 1) * CH],
                        in1=ps_bc2[:], op=ALU.add)
                bc_free = [uc_tok[NDT - 1]]
            u_tok = [uc_tok[NDT - 1]]

        with contextlib.ExitStack() as es2:
            S2 = lambda n, sh, dt: es2.enter_context(nc.sbuf_tensor(n, sh, dt))
            P2 = lambda n, sh: es2.enter_context(nc.psum_tensor(n, sh, F32))
            gw = S2("D_gw", [128, NDT * NEXP], BF16)
            gb = S2("D_gb", [NEXP, 1], F32)
            at_ = S2("D_at", [128, NDT * 32], BF16)
            bt_ = S2("D_bt", [32, FD4], BF16)
            e32 = S2("D_e32", [NEXP, 32], F32)
            logitT = S2("D_logT", [NEXP, T], F32)
            maskT = S2("D_maskT", [NEXP, T], F32)
            ltm = S2("D_ltm", [128, NTT * NEXP], F32)
            m1 = S2("D_m1", [128, 1], F32)
            xa = S2("D_xa", [32, T], F32)
            xam = S2("D_xam", [32, T], BF16)
            ps_g = P2("D_psg", [NEXP, CH])
            ps_t = [P2("D_pst0", [128, 128]), P2("D_pst1", [128, 128])]
            ps_xa = P2("D_psxa", [32, CH])

            gt = [load_wcast(ctx, W["gate_WT"], gw[:], "qwD1", pre=ctx.bar())]
            gbt = emit_dma_waited(ctx, y, gb[:], W["gate_b"], chan="qv0", pre=ctx.bar())
            e32t = emit_dma_waited(ctx, y, e32[:], W["e32"], chan="qv1", pre=ctx.bar())
            att = [load_wcast(ctx, W["AT"], at_[:], "qwD2", pre=[])]
            bt_tok = [load_wcast(ctx, W["BT_f32"], bt_[:], "qwD3", pre=[])]

            glast = None
            for c in range(NCH):
                ctx.waits(t, gt + [glast] + u_tok)
                for dt in range(NDT):
                    mm = t.matmul(ps_g[:], gw[:, dt * NEXP:(dt + 1) * NEXP],
                                  U[:, dt * T + c * CH:dt * T + (c + 1) * CH],
                                  start=(dt == 0), stop=(dt == NDT - 1))
                pt = ctx.inc(mm, "pe")
                ctx.waits(s, [pt, gbt])
                glast = s.activation(out=logitT[:, c * CH:(c + 1) * CH], in_=ps_g[:],
                                     func=AF.Identity, bias=gb[:])
            tr_done = [None, None]
            tm_tok = [None] * NTT
            for tt in range(NTT):
                b = tt % 2
                ctx.waits(t, [glast, tr_done[b]])
                tr = t.transpose(ps_t[b][0:128, 0:NEXP], logitT[:, tt * 128:(tt + 1) * 128],
                                 CONST["ident"][0:NEXP, 0:NEXP])
                ptok = ctx.inc(tr, "pe")
                ctx.wait(s, ptok)
                tm_tok[tt] = s.activation(out=ltm[:, tt * NEXP:(tt + 1) * NEXP],
                                          in_=ps_t[b][0:128, 0:NEXP], func=AF.Identity)
                tr_done[b] = tm_tok[tt]
            for tt in range(NTT):
                sl = ltm[:, tt * NEXP:(tt + 1) * NEXP]
                ctx.waits(v, [tm_tok[tt]])
                v.tensor_reduce(out=m1[:], in_=sl, axis=AX.X, op=ALU.max)
                v.tensor_scalar(out=sl, in0=sl, scalar1=m1[:], scalar2=None, op0=ALU.is_ge)
            tr_done = [None, None]
            mk_last = None
            for tt in range(NTT):
                b = tt % 2
                ctx.waits(t, [("dve", ctx.counts["dve"]), tr_done[b]])
                tr = t.transpose(ps_t[b][0:NEXP, 0:128], ltm[:, tt * NEXP:(tt + 1) * NEXP],
                                 CONST["ident"][:])
                ptok = ctx.inc(tr, "pe")
                ctx.wait(s, ptok)
                mk_last = s.activation(out=maskT[:, tt * 128:(tt + 1) * 128],
                                       in_=ps_t[b][0:NEXP, 0:128], func=AF.Identity)
                tr_done[b] = mk_last
            xa_last = None
            for c in range(NCH):
                ctx.waits(t, att + [xa_last] + u_tok)
                for dt in range(NDT):
                    mm = t.matmul(ps_xa[:], at_[:, dt * 32:(dt + 1) * 32],
                                  U[:, dt * T + c * CH:dt * T + (c + 1) * CH],
                                  start=(dt == 0), stop=(dt == NDT - 1))
                pt = ctx.inc(mm, "pe")
                ctx.wait(s, pt)
                xa_last = s.activation(out=xa[:, c * CH:(c + 1) * CH], in_=ps_xa[:],
                                       func=AF.Identity)
            xam_last = None
            for c in range(NCH):
                ctx.waits(t, [e32t, mk_last, xam_last, xa_last])
                mm = t.matmul(ps_xa[:], e32[:], maskT[:, c * CH:(c + 1) * CH],
                              start=True, stop=True)
                pt = ctx.inc(mm, "pe")
                ctx.waits(v, [pt, xa_last])
                xam_last = v.tensor_tensor(out=xam[0:32, c * CH:(c + 1) * CH],
                                           in0=xa[0:32, c * CH:(c + 1) * CH],
                                           in1=ps_xa[0:32, :], op=ALU.mult)
            xam_tok = [xam_last]

            with contextlib.ExitStack() as es3:
                S3 = lambda n, sh, dt: es3.enter_context(nc.sbuf_tensor(n, sh, dt))
                P3 = lambda n, sh: es3.enter_context(nc.psum_tensor(n, sh, F32))
                wib = [S3("D_wib0", [128, NDT * 1024], BF16), S3("D_wib1", [128, NDT * 1024], BF16)]
                wob = [S3("D_wob0", [128, NDT * 1024], BF16), S3("D_wob1", [128, NDT * 1024], BF16)]
                wibias = S3("D_wibias", [128, 32], F32)
                wobias = S3("D_wobias", [128, NDT], F32)
                inter = S3("D_inter", [128, NDT * CH], BF16)
                ps_i = [P3("D_psi0", [128, CH]), P3("D_psi1", [128, CH])]
                ps_o = [P3("D_pso0", [128, CH]), P3("D_pso1", [128, CH])]
                wibt = emit_dma_waited(ctx, y, wibias[:], W["wi_b_pp"], chan="qv0",
                                       pre=ctx.bar())
                wobt = emit_dma_waited(ctx, y, wobias[:], W["wo_b_pp"], chan="qv1",
                                       pre=ctx.bar())
                wi_used = [None, None]
                wo_used = [None, None]
                i_free = {}
                o_free = {}
                y_last = None
                for fsb in range(4):
                    wslot = fsb % 2
                    wtoki = [load_wcast(ctx, W["wiT"][:, fsb * 8192:(fsb + 1) * 8192], wib[wslot][:],
                                        f"qwiD{wslot}",
                                        pre=[wi_used[wslot]] + (ctx.bar() if fsb < 2 else []))]
                    wtoko = [load_wcast(ctx, W["woT2"][:, fsb * 8192:(fsb + 1) * 8192], wob[wslot][:],
                                        f"qwoD{wslot}",
                                        pre=[wo_used[wslot]] + (ctx.bar() if fsb < 2 else []))]
                    for c in range(NCH):
                        for fi in range(NDT):
                            fs = fsb * NDT + fi
                            pb = fi % 2
                            ctx.waits(t, [i_free.get(pb)] + wtoki + u_tok + xam_tok)
                            for dt in range(NDT):
                                t.matmul(ps_i[pb][:],
                                         wib[wslot][:, dt * 1024 + fi * 128: dt * 1024 + (fi + 1) * 128],
                                         U[:, dt * T + c * CH:dt * T + (c + 1) * CH],
                                         start=(dt == 0), stop=False)
                            mm = t.matmul(ps_i[pb][:], bt_[:, fs * 128:(fs + 1) * 128],
                                          xam[:, c * CH:(c + 1) * CH],
                                          start=False, stop=True)
                            pt = ctx.inc(mm, "pe")
                            wi_used[wslot] = pt
                            ctx.waits(s, [pt, wibt, bt_tok[-1] if fsb == 0 else None])
                            i_free[pb] = s.activation(out=inter[:, fi * CH:(fi + 1) * CH],
                                                      in_=ps_i[pb][:], func=AF.Relu,
                                                      bias=wibias[:, fs:fs + 1])
                        it_tok = [i_free.get(0), i_free.get(1)]
                        for ot in range(NDT):
                            pb = ot % 2
                            ctx.waits(t, [o_free.get(pb)] + wtoko + it_tok)
                            for fi in range(NDT):
                                mm = t.matmul(ps_o[pb][:],
                                              wob[wslot][:, fi * 1024 + ot * 128: fi * 1024 + (ot + 1) * 128],
                                              inter[:, fi * CH:(fi + 1) * CH],
                                              start=(fi == 0), stop=(fi == NDT - 1))
                            pt = ctx.inc(mm, "pe")
                            wo_used[wslot] = pt
                            if fsb == 0:
                                ctx.waits(s, [pt, wobt])
                                y_last = s.activation(out=Y[:, ot * T + c * CH: ot * T + (c + 1) * CH],
                                                      in_=ps_o[pb][:], func=AF.Identity,
                                                      bias=wobias[:, ot:ot + 1])
                                o_free[pb] = y_last
                            else:
                                ctx.waits(v, [pt, ("act", ctx.counts["act"])])
                                y_last = v.tensor_tensor(
                                    out=Y[:, ot * T + c * CH: ot * T + (c + 1) * CH],
                                    in0=Y[:, ot * T + c * CH: ot * T + (c + 1) * CH],
                                    in1=ps_o[pb][:], op=ALU.add)
                                o_free[pb] = y_last
                y_toks = [y_last]

        with contextlib.ExitStack() as es4:
            S4 = lambda n, sh, dt: es4.enter_context(nc.sbuf_tensor(n, sh, dt))
            P4 = lambda n, sh: es4.enter_context(nc.psum_tensor(n, sh, F32))
            return emit_ln_residual_store(ctx, Y, xin, xout, CONST, S4, P4, y_toks, tag="DT")


def host_prep_D(inputs):
    la = np.asarray(inputs["L1_ffn_lora_A"])
    lb = np.asarray(inputs["L1_ffn_lora_B"])
    e32 = np.zeros((NEXP, 32), np.float32)
    for e in range(NEXP):
        e32[e, e * 4:(e + 1) * 4] = 1.0
    wiT = np.asarray(inputs["L1_ffn_wi_W"]).T
    woT2 = np.asarray(inputs["L1_ffn_wo_W"]).T
    wiT_i = np.concatenate([ilv(wiT[:, f * 1024:(f + 1) * 1024]) for f in range(4)], 1)
    woT2_i = np.concatenate([ilv(woT2[f * 1024:(f + 1) * 1024, :]) for f in range(4)], 1)
    return {
        "gate_WT": ilv(np.asarray(inputs["L1_ffn_gate_W"]).T),
        "gate_b": np.ascontiguousarray(np.asarray(inputs["L1_ffn_gate_b"]).reshape(NEXP, 1)),
        "wiT": np.ascontiguousarray(wiT_i),
        "woT2": np.ascontiguousarray(woT2_i),
        "AT": ilv(la.transpose(2, 0, 1).reshape(1024, 32)),
        "BT_f32": np.ascontiguousarray(lb.transpose(0, 2, 1).reshape(32, FD4)),
        "wi_b_pp": vec_pp(inputs["L1_ffn_wi_b"]),
        "wo_b_pp": vec_pp(inputs["L1_ffn_wo_b"]),
        "e32": e32,
    }


def build_phaseD_program():
    nc = bass.Bass("TRN2", target_bir_lowering=False)
    es = contextlib.ExitStack()
    ctx = Ctx(nc, es)
    xin = nc.dram_tensor("xin", [D, T], F32, kind="ExternalInput").ap()
    xout = nc.dram_tensor("xout", [D, T], F32, kind="ExternalOutput").ap()
    W = {
        "gate_WT": nc.dram_tensor("gate_WT", [128, NDT * NEXP], F32, kind="ExternalInput").ap(),
        "gate_b": nc.dram_tensor("gate_b", [NEXP, 1], F32, kind="ExternalInput").ap(),
        "wiT": nc.dram_tensor("wiT", [128, 4 * 8192], F32, kind="ExternalInput").ap(),
        "woT2": nc.dram_tensor("woT2", [128, 4 * 8192], F32, kind="ExternalInput").ap(),
        "AT": nc.dram_tensor("AT", [128, NDT * 32], F32, kind="ExternalInput").ap(),
        "BT_f32": nc.dram_tensor("BT_f32", [32, FD4], F32, kind="ExternalInput").ap(),
        "wi_b_pp": nc.dram_tensor("wi_b_pp", [128, 32], F32, kind="ExternalInput").ap(),
        "wo_b_pp": nc.dram_tensor("wo_b_pp", [128, NDT], F32, kind="ExternalInput").ap(),
        "e32": nc.dram_tensor("e32", [NEXP, 32], F32, kind="ExternalInput").ap(),
    }
    with nc.Block():
        CONST = build_consts(ctx)
        for eng in (nc.tensor, nc.vector, nc.scalar, nc.gpsimd):
            ctx.waits(eng, CONST["_ready"])
        st = phase_l1ffn(ctx, xin, xout, W, CONST)
        ctx.waits(nc.gpsimd, st)
    es.close()
    return nc



def make_host_consts():
    con = {}
    con["c_ident"] = np.eye(128, dtype=np.float32)
    con["c_tri"] = np.triu(np.ones((128, 128), np.float32))
    return con


def build_consts(ctx):
    nc = ctx.nc
    C = {}
    C["ident"] = ctx.S("C_ident", [128, 128], F32)
    C["identb"] = ctx.S("C_identb", [128, 128], BF16)
    C["trimask"] = ctx.S("C_trimask", [128, 128], BF16)
    C["ones_scaled"] = ctx.S("C_onessc", [128, 1], F32)
    C["ones_scaled_b"] = ctx.S("C_onesscb", [128, 1], BF16)
    C["onesrow"] = ctx.S("C_onesrow", [1, 128], F32)
    C["onesrow_b"] = ctx.S("C_onesrowb", [1, 128], BF16)
    tri_f = ctx.S("C_trif", [128, 128], F32)
    C["eps_row"] = ctx.S("C_epsrow", [1, 1], F32)
    ctx.v.memset(C["eps_row"][:], LN_EPS)
    ctx.eps_row = C["eps_row"][:]
    ident_h = nc.dram_tensor("c_ident", [128, 128], F32, kind="ExternalInput").ap()
    tri_h = nc.dram_tensor("c_tri", [128, 128], F32, kind="ExternalInput").ap()
    t1 = emit_dma_waited(ctx, ctx.y, C["ident"][:], ident_h, chan="qc1")
    t3 = emit_dma_waited(ctx, ctx.y, tri_f[:], tri_h, chan="qc2")
    ctx.waits(ctx.g, [t1, t3])
    ctx.g.tensor_copy(out=C["identb"][:], in_=C["ident"][:])
    cv = ctx.g.tensor_copy(out=C["trimask"][:], in_=tri_f[:])
    ctx.v.memset(C["ones_scaled"][:], 1.0 / 1024.0)
    ctx.v.memset(C["ones_scaled_b"][:], 1.0 / 1024.0)
    ctx.v.memset(C["onesrow_b"][:], 1.0)
    ms = ctx.v.memset(C["onesrow"][:], 1.0)
    C["_ready"] = [cv, ms]
    return C



def build_phaseA_program():
    nc = bass.Bass("TRN2", target_bir_lowering=False)
    es = contextlib.ExitStack()
    ctx = Ctx(nc, es)
    xin = nc.dram_tensor("xin", [D, T], F32, kind="ExternalInput").ap()
    xout = nc.dram_tensor("xout", [D, T], F32, kind="ExternalOutput").ap()
    W = {wn: nc.dram_tensor(wn, [128, NDT * D], F32, kind="ExternalInput").ap()
         for wn in ("wqT", "wkT", "wvT", "woT")}
    with nc.Block():
        CONST = build_consts(ctx)
        for eng in (nc.tensor, nc.vector, nc.scalar, nc.gpsimd):
            ctx.waits(eng, CONST["_ready"])
        st = phase_l0attn(ctx, xin, xout, W, CONST)
        ctx.waits(nc.gpsimd, st)
    es.close()
    return nc


def host_prep_A(inputs):
    shared = {
        "wqT": ilv(np.asarray(inputs["L0_attn_q_heads"]).T),
        "wkT": ilv(np.asarray(inputs["L0_attn_k_heads"]).T),
        "wvT": ilv(np.asarray(inputs["L0_attn_v_heads"]).T),
        "woT": ilv(np.asarray(inputs["L0_attn_output_proj"]).T),
    }
    shared.update(make_host_consts())
    return shared


_BUILDERS = None


def kernel_multi(**inputs):
    x = np.asarray(inputs["L0_attn_x"])
    consts = make_host_consts()
    phase_w = [host_prep_A(inputs), host_prep_B(inputs),
               host_prep_C(inputs), host_prep_D(inputs)]
    builders = [build_phaseA_program, build_phaseB_program,
                build_phaseC_program, build_phaseD_program]
    xT = np.ascontiguousarray(x.transpose(0, 2, 1)).astype(np.float32, copy=False)
    cur = [np.ascontiguousarray(xT[b]) for b in range(8)]
    for build, Wd in zip(builders, phase_w):
        nc = build()
        in_maps = []
        for b in range(8):
            m = dict(Wd)
            m.update(consts)
            m["xin"] = cur[b]
            in_maps.append(m)
        res = run_bass_kernel_spmd(nc, in_maps, core_ids=list(range(8)))
        cur = [np.asarray(res.results[b]["xout"]) for b in range(8)]
    return np.stack([c.T for c in cur]).astype(np.float32, copy=False)


def ilv(w):
    K, M = w.shape
    return np.ascontiguousarray(np.asarray(w).reshape(K // 128, 128, M).transpose(1, 0, 2).reshape(128, -1))


def qkv_perm():
    perm = np.zeros(1536, dtype=np.int64)
    for h in range(16):
        for d0 in range(64):
            orig = h * 64 + d0
            if d0 < 32:
                new = (h // 4) * 128 + (h % 4) * 32 + d0
            else:
                new = 512 + (h // 4) * 128 + (h % 4) * 32 + (d0 - 32)
            perm[new] = orig
    for kv in range(4):
        for d0 in range(64):
            orig = 1024 + kv * 64 + d0
            if d0 < 32:
                new = 1024 + kv * 32 + d0
            else:
                new = 1152 + kv * 32 + (d0 - 32)
            perm[new] = orig
    for j in range(1280, 1536):
        perm[j] = j
    return perm


def rope_tables():
    inv_freq = 1.0 / (10000.0 ** (np.arange(0, HD, 2, dtype=np.float32) / HD))
    freqs = np.arange(T, dtype=np.float32)[:, None] * inv_freq[None, :]
    cosH = np.cos(freqs).T.astype(np.float32)
    sinH = np.sin(freqs).T.astype(np.float32)
    return np.tile(cosH, (4, 1)).copy(), np.tile(sinH, (4, 1)).copy()


def vec_pp(vec):
    return np.ascontiguousarray(np.asarray(vec).reshape(-1, 128).T)


def host_prep_B(inputs):
    perm = qkv_perm()
    wqkvT = np.asarray(inputs["L0_ffn_wqkv"]).T[:, perm]
    wqkvT_i = np.concatenate([ilv(wqkvT[:, 0:768]), ilv(wqkvT[:, 768:1536])], axis=1)
    cosF, sinF = rope_tables()
    w13 = np.asarray(inputs["L0_ffn_w13"])
    w1T = w13[:, :FFN, :].transpose(0, 2, 1)
    w3T = w13[:, FFN:, :].transpose(0, 2, 1)
    w1T_i = np.stack([np.concatenate([ilv(w1T[e][:, s * CH:(s + 1) * CH]) for s in range(4)], 1)
                      for e in range(NEXP)]).reshape(NEXP * 128, 4 * 4096)
    w3T_i = np.stack([np.concatenate([ilv(w3T[e][:, s * CH:(s + 1) * CH]) for s in range(4)], 1)
                      for e in range(NEXP)]).reshape(NEXP * 128, 4 * 4096)
    w2T = np.asarray(inputs["L0_ffn_w2"]).transpose(0, 2, 1)
    w2T_i = np.stack([ilv(w2T[e]) for e in range(NEXP)]).reshape(NEXP * 128, NTT * 1024)
    return {
        "wqkvT": np.ascontiguousarray(wqkvT_i),
        "woT": ilv(np.asarray(inputs["L0_ffn_wo"]).T),
        "gate_wT": ilv(np.asarray(inputs["L0_ffn_gate_w"]).T),
        "ln1_pp": vec_pp(inputs["L0_ffn_ln1_w"]),
        "ln2_pp": vec_pp(inputs["L0_ffn_ln2_w"]),
        "cosF": cosF, "sinF": sinF,
        "w1T": np.ascontiguousarray(w1T_i), "w3T": np.ascontiguousarray(w3T_i),
        "w2T": np.ascontiguousarray(w2T_i),
    }


def build_phaseB_program():
    nc = bass.Bass("TRN2", target_bir_lowering=False)
    es = contextlib.ExitStack()
    ctx = Ctx(nc, es)
    xin = nc.dram_tensor("xin", [D, T], F32, kind="ExternalInput").ap()
    xout = nc.dram_tensor("xout", [D, T], F32, kind="ExternalOutput").ap()
    W = {
        "wqkvT": nc.dram_tensor("wqkvT", [128, 2 * 6144], F32, kind="ExternalInput").ap(),
        "woT": nc.dram_tensor("woT", [128, NDT * D], F32, kind="ExternalInput").ap(),
        "gate_wT": nc.dram_tensor("gate_wT", [128, NDT * NEXP], F32, kind="ExternalInput").ap(),
        "ln1_pp": nc.dram_tensor("ln1_pp", [128, NDT], F32, kind="ExternalInput").ap(),
        "ln2_pp": nc.dram_tensor("ln2_pp", [128, NDT], F32, kind="ExternalInput").ap(),
        "cosF": nc.dram_tensor("cosF", [128, T], F32, kind="ExternalInput").ap(),
        "sinF": nc.dram_tensor("sinF", [128, T], F32, kind="ExternalInput").ap(),
        "w1T": nc.dram_tensor("w1T", [NEXP * 128, 4 * 4096], F32, kind="ExternalInput").ap(),
        "w3T": nc.dram_tensor("w3T", [NEXP * 128, 4 * 4096], F32, kind="ExternalInput").ap(),
        "w2T": nc.dram_tensor("w2T", [NEXP * 128, NTT * 1024], F32, kind="ExternalInput").ap(),
    }
    with nc.Block():
        CONST = build_consts(ctx)
        for eng in (nc.tensor, nc.vector, nc.scalar, nc.gpsimd):
            ctx.waits(eng, CONST["_ready"])
        st = phase_l0ffn(ctx, xin, xout, W, CONST)
        ctx.waits(nc.gpsimd, st)
    es.close()
    return nc


def build_fused_program():
    nc = bass.Bass("TRN2", target_bir_lowering=False)
    es = contextlib.ExitStack()
    ctx = Ctx(nc, es)
    xin = nc.dram_tensor("xin", [D, T], F32, kind="ExternalInput").ap()
    xout = nc.dram_tensor("xout", [D, T], F32, kind="ExternalOutput").ap()
    import os
    _xmk = "ExternalOutput" if os.environ.get("BASS_DEBUG_XM") else "Internal"
    xm = [nc.dram_tensor(f"xm{i}", [D, T], F32, kind=_xmk).ap() for i in range(3)]
    WA = {wn: nc.dram_tensor(wn, [128, NDT * D], F32, kind="ExternalInput").ap()
          for wn in ("wqT", "wkT", "wvT", "woT")}
    WB = {
        "wqkvT": nc.dram_tensor("b_wqkvT", [128, 2 * 6144], F32, kind="ExternalInput").ap(),
        "woT": nc.dram_tensor("b_woT", [128, NDT * D], F32, kind="ExternalInput").ap(),
        "gate_wT": nc.dram_tensor("b_gate_wT", [128, NDT * NEXP], F32, kind="ExternalInput").ap(),
        "ln1_pp": nc.dram_tensor("b_ln1_pp", [128, NDT], F32, kind="ExternalInput").ap(),
        "ln2_pp": nc.dram_tensor("b_ln2_pp", [128, NDT], F32, kind="ExternalInput").ap(),
        "cosF": nc.dram_tensor("b_cosF", [128, T], F32, kind="ExternalInput").ap(),
        "sinF": nc.dram_tensor("b_sinF", [128, T], F32, kind="ExternalInput").ap(),
        "w1T": nc.dram_tensor("b_w1T", [NEXP * 128, 4 * 4096], F32, kind="ExternalInput").ap(),
        "w3T": nc.dram_tensor("b_w3T", [NEXP * 128, 4 * 4096], F32, kind="ExternalInput").ap(),
        "w2T": nc.dram_tensor("b_w2T", [NEXP * 128, NTT * 1024], F32, kind="ExternalInput").ap(),
    }
    WC = {wn: nc.dram_tensor("c_" + wn, [128, NDT * D], F32, kind="ExternalInput").ap()
          for wn in ("wkT", "wvT", "wrT", "woT")}
    WC["vecs_pp"] = nc.dram_tensor("c_vecs_pp", [128, 5 * NDT], F32, kind="ExternalInput").ap()
    WD = {
        "gate_WT": nc.dram_tensor("d_gate_WT", [128, NDT * NEXP], F32, kind="ExternalInput").ap(),
        "gate_b": nc.dram_tensor("d_gate_b", [NEXP, 1], F32, kind="ExternalInput").ap(),
        "wiT": nc.dram_tensor("d_wiT", [128, 4 * 8192], F32, kind="ExternalInput").ap(),
        "woT2": nc.dram_tensor("d_woT2", [128, 4 * 8192], F32, kind="ExternalInput").ap(),
        "AT": nc.dram_tensor("d_AT", [128, NDT * 32], F32, kind="ExternalInput").ap(),
        "BT_f32": nc.dram_tensor("d_BT_f32", [32, FD4], F32, kind="ExternalInput").ap(),
        "wi_b_pp": nc.dram_tensor("d_wi_b_pp", [128, 32], F32, kind="ExternalInput").ap(),
        "wo_b_pp": nc.dram_tensor("d_wo_b_pp", [128, NDT], F32, kind="ExternalInput").ap(),
        "e32": nc.dram_tensor("d_e32", [NEXP, 32], F32, kind="ExternalInput").ap(),
    }
    with nc.Block():
        CONST = build_consts(ctx)
        for eng in (nc.tensor, nc.vector, nc.scalar, nc.gpsimd):
            ctx.waits(eng, CONST["_ready"])
        st = phase_l0attn(ctx, xin, xm[0], WA, CONST)
        st = phase_l0ffn(ctx, xm[0], xm[1], WB, CONST, start_toks=st)
        st = phase_l1attn(ctx, xm[1], xm[2], WC, CONST, start_toks=st)
        st = phase_l1ffn(ctx, xm[2], xout, WD, CONST, start_toks=st)
        ctx.waits(nc.gpsimd, st)
    es.close()
    return nc


def kernel(**inputs):
    x = np.asarray(inputs["L0_attn_x"])
    consts = make_host_consts()
    WA = host_prep_A(inputs)
    WB = {"b_" + k: v for k, v in host_prep_B(inputs).items()}
    WC = {"c_" + k: v for k, v in host_prep_C(inputs).items()}
    WD = {"d_" + k: v for k, v in host_prep_D(inputs).items()}
    shared = {}
    shared.update(WA); shared.update(WB); shared.update(WC); shared.update(WD)
    shared.update(consts)
    nc = build_fused_program()
    xT = np.ascontiguousarray(x.transpose(0, 2, 1)).astype(np.float32, copy=False)
    in_maps = []
    for b in range(8):
        m = dict(shared)
        m["xin"] = np.ascontiguousarray(xT[b])
        in_maps.append(m)
    res = run_bass_kernel_spmd(nc, in_maps, core_ids=list(range(8)))
    return np.stack([np.asarray(res.results[b]["xout"]).T for b in range(8)]).astype(np.float32, copy=False)


# version 20 (final)
# speedup vs baseline: 1.0178x; 1.0083x over previous
import contextlib
import numpy as np
import concourse.bass as bass
import concourse.mybir as mybir
from concourse.bass_utils import run_bass_kernel_spmd

F32 = mybir.dt.float32
BF16 = mybir.dt.bfloat16
AF = mybir.ActivationFunctionType
ALU = mybir.AluOpType
AX = mybir.AxisListType

D = 1024
T = 2048
NDT = 8
CH = 512
NCH = T // CH
NTT = T // 128
HEADS = 16
HD = 64
LN_EPS = 1e-5



class Ctx:

    def __init__(self, nc, es):
        self.nc = nc
        self.es = es
        self._sems = {}
        self.counts = {}

    def sem(self, name):
        if name not in self._sems:
            self._sems[name] = self.es.enter_context(self.nc.semaphore(name))
            self.counts[name] = 0
        return self._sems[name]

    def inc(self, instr, name, step=1):
        s = self.sem(name)
        self.counts[name] += step
        instr.then_inc(s, step)
        return (name, self.counts[name])

    def wait(self, eng, tok):
        if tok is None:
            return
        name, val = tok
        eng.wait_ge(self.sem(name), val)

    def waits(self, eng, toks):
        best = {}
        for t in toks:
            if t is None:
                continue
            n, v = t
            if n not in best or v > best[n]:
                best[n] = v
        for n, v in best.items():
            eng.wait_ge(self.sem(n), v)

    @property
    def t(self):
        return self.nc.tensor

    @property
    def v(self):
        if not hasattr(self, "_vp"):
            self._vp = EngProxy(self, self.nc.vector, "dve")
        return self._vp

    @property
    def s(self):
        if not hasattr(self, "_sp"):
            self._sp = EngProxy(self, self.nc.scalar, "act")
        return self._sp

    @property
    def g(self):
        if not hasattr(self, "_gp"):
            self._gp = EngProxy(self, self.nc.gpsimd, "pool")
        return self._gp

    @property
    def y(self):
        return self.nc.sync

    def S(self, name, shape, dtype):
        return self.es.enter_context(self.nc.sbuf_tensor(name, shape, dtype))

    def bar(self):
        return [(n, c) for n, c in self.counts.items() if c > 0]


class EngProxy:
    CHAINED = {
        "tensor_tensor", "tensor_scalar", "tensor_copy", "memset",
        "reciprocal", "scalar_tensor_tensor", "tensor_tensor_scan",
        "tensor_reduce", "reduce_max", "reduce_sum", "select",
        "transpose", "activation", "copy", "add", "mul", "iota",
        "tensor_scalar_mul", "tensor_scalar_add", "tensor_add", "tensor_mul",
        "tensor_sub", "stream_shuffle",
    }

    def __init__(self, ctx, eng, sem_name):
        self._ctx = ctx
        self._eng = eng
        self._sem = sem_name

    def __getattr__(self, name):
        real = getattr(self._eng, name)
        if name not in EngProxy.CHAINED:
            return real

        def wrap(*a, **kw):
            ctx = self._ctx
            if ctx.counts.get(self._sem, 0) > 0:
                self._eng.wait_ge(ctx.sem(self._sem), ctx.counts[self._sem])
            instr = real(*a, **kw)
            return ctx.inc(instr, self._sem)
        return wrap


def emit_dma_waited(ctx, q, out, in_, chan="qs", pre=()):
    ctx.waits(q, pre)
    d = q.dma_start(out=out, in_=in_)
    return ctx.inc(d, chan, 16)


def load_wcast(ctx, hbm2d, dst_ap, chan, pre=()):
    ctx.waits(ctx.g, list(pre))
    d = ctx.g.dma_start(out=dst_ap, in_=hbm2d)
    return ctx.inc(d, chan, 16)



class LNRows:
    def __init__(self, alloc, tag):
        self.mean = alloc(tag + "_mean", [1, CH], F32)
        self.msq = alloc(tag + "_msq", [1, CH], F32)
        self.rstd = alloc(tag + "_rstd", [1, CH], F32)
        self.b = alloc(tag + "_b", [1, CH], F32)


def emit_ln_rowmath(ctx, ps_mean, ps_msq, rows, done_toks, rms=False, eps_ap=None):
    v, s = ctx.v, ctx.s
    eps_ap = eps_ap if eps_ap is not None else ctx.eps_row
    ctx.waits(s, done_toks)
    if not rms:
        t1 = s.activation(out=rows.mean[:], in_=ps_mean[:], func=AF.Identity)
    t2 = s.activation(out=rows.msq[:], in_=ps_msq[:], func=AF.Identity)
    if rms:
        tl = s.activation(out=rows.rstd[:], in_=rows.msq[:], func=AF.Ln, bias=eps_ap)
        tl = s.activation(out=rows.rstd[:], in_=rows.rstd[:], func=AF.Exp, scale=-0.5)
        return tl
    ctx.waits(v, [t1, t2])
    tv = v.tensor_tensor(out=rows.rstd[:], in0=rows.mean[:], in1=rows.mean[:], op=ALU.mult)
    tv = v.tensor_tensor(out=rows.rstd[:], in0=rows.msq[:], in1=rows.rstd[:], op=ALU.subtract)
    ctx.wait(s, tv)
    tl = s.activation(out=rows.rstd[:], in_=rows.rstd[:], func=AF.Ln, bias=eps_ap)
    tl = s.activation(out=rows.rstd[:], in_=rows.rstd[:], func=AF.Exp, scale=-0.5)
    ctx.wait(v, tl)
    tb = v.scalar_tensor_tensor(
        out=rows.b[:], in0=rows.mean[:], scalar=-1.0, in1=rows.rstd[:],
        op0=ALU.mult, op1=ALU.mult)
    return tb


def emit_ln_stats_chunk(ctx, src_tiles, ps_mean, ps_msq, sq, CONST,
                        pre_toks, psum_free_toks, rms=False):
    t, s = ctx.t, ctx.s
    tokA = None
    if not rms:
        for dt in range(NDT):
            ctx.waits(t, list(pre_toks[dt]) + [psum_free_toks.get("mean")])
            mm = t.matmul(ps_mean[:], CONST["ones_scaled"][:, 0:1], src_tiles[dt],
                          start=(dt == 0), stop=(dt == NDT - 1))
        tokA = ctx.inc(mm, "pe")
    for dt in range(NDT):
        ctx.waits(s, list(pre_toks[dt]) + [psum_free_toks.get("msq")])
        at = s.activation(out=sq[:, dt * CH:(dt + 1) * CH], in_=src_tiles[dt], func=AF.Square)
        ctx.waits(t, [at])
        mm = t.matmul(ps_msq[:], CONST["ones_scaled_b"][:, 0:1], sq[:, dt * CH:(dt + 1) * CH],
                      start=(dt == 0), stop=(dt == NDT - 1))
    tokB = ctx.inc(mm, "pe")
    return tokB


def emit_bcast2(ctx, CONST, row1, row2, ps_bc1, ps_bc2, ready_tok, bc_free_toks):
    t = ctx.t
    ctx.waits(t, [ready_tok] + list(bc_free_toks))
    ctx.inc(t.matmul(ps_bc1[:], CONST["onesrow"][0:1, 0:128], row1[:], start=True, stop=True), "pe")
    mm = t.matmul(ps_bc2[:], CONST["onesrow"][0:1, 0:128], row2[:], start=True, stop=True)
    return ctx.inc(mm, "pe")


def emit_bcast1(ctx, CONST, row1, ps_bc1, ready_tok, bc_free_toks):
    t = ctx.t
    ctx.waits(t, [ready_tok] + list(bc_free_toks))
    mm = t.matmul(ps_bc1[:], CONST["onesrow"][0:1, 0:128], row1[:], start=True, stop=True)
    return ctx.inc(mm, "pe")



def phase_l0attn(ctx, xin, xout, W, CONST, start_toks=()):
    entry = ctx.bar() + list(start_toks)
    for _eng in (ctx.t, ctx.v, ctx.s, ctx.g, ctx.y):
        ctx.waits(_eng, entry)
    nc = ctx.nc
    v, s, t, g, y = ctx.v, ctx.s, ctx.t, ctx.g, ctx.y
    ph = {}

    with contextlib.ExitStack() as es0:
        S0 = lambda n, sh, dt: es0.enter_context(nc.sbuf_tensor(n, sh, dt))
        attn_o = S0("A_attnO", [128, NDT * T], BF16)

        with contextlib.ExitStack() as esq:
            Sq = lambda n, sh, dt: esq.enter_context(nc.sbuf_tensor(n, sh, dt))
            qT = Sq("A_qT", [128, NDT * T], BF16)
            kT = Sq("A_kT", [128, NDT * T], BF16)
            vaug = Sq("A_vaug", [128, NTT * HEADS * (HD + 1)], BF16)
            U = Sq("A_U", [128, NDT * T], BF16)

            with contextlib.ExitStack() as es1:
                S1 = lambda n, sh, dt: es1.enter_context(nc.sbuf_tensor(n, sh, dt))
                P1 = lambda n, sh: es1.enter_context(nc.psum_tensor(n, sh, F32))
                xc = S1("A_xc", [128, NDT * CH], F32)
                sq = S1("A_sq", [128, NDT * CH], BF16)
                rows = LNRows(S1, "A_r1")
                ps_mean = P1("A_psm", [1, CH])
                ps_msq = P1("A_psq", [1, CH])
                ps_bc1 = P1("A_psb1", [128, CH])
                ps_bc2 = P1("A_psb2", [128, CH])

                xld = [None] * NDT
                uc_tok = [None] * NDT
                psum_free = {}
                bc_free = []
                for c in range(NCH):
                    for dt in range(NDT):
                        xld[dt] = emit_dma_waited(
                            ctx, y, xc[:, dt * CH:(dt + 1) * CH],
                            xin[dt * 128:(dt + 1) * 128, c * CH:(c + 1) * CH],
                            chan=f"qx{dt}",
                            pre=[uc_tok[dt]] + (list(start_toks) if c == 0 else []))
                    xtiles = [xc[:, dt * CH:(dt + 1) * CH] for dt in range(NDT)]
                    st_tok = emit_ln_stats_chunk(ctx, xtiles, ps_mean, ps_msq, sq,
                                                 CONST, [[x] for x in xld], psum_free)
                    rdy = emit_ln_rowmath(ctx, ps_mean, ps_msq, rows, [st_tok])
                    psum_free["mean"] = rdy
                    psum_free["msq"] = rdy
                    bt = emit_bcast2(ctx, CONST, rows.rstd, rows.b, ps_bc1, ps_bc2, rdy, bc_free)
                    for dt in range(NDT):
                        ctx.waits(v, [bt])
                        v.tensor_tensor(out=U[:, dt * T + c * CH:dt * T + (c + 1) * CH],
                                        in0=xc[:, dt * CH:(dt + 1) * CH], in1=ps_bc1[:], op=ALU.mult)
                        tv = v.tensor_tensor(
                            out=U[:, dt * T + c * CH:dt * T + (c + 1) * CH],
                            in0=U[:, dt * T + c * CH:dt * T + (c + 1) * CH],
                            in1=ps_bc2[:], op=ALU.add)
                        uc_tok[dt] = tv
                    bc_free = [uc_tok[NDT - 1]]
                u_tok = [uc_tok[NDT - 1]]

            with contextlib.ExitStack() as es2:
                S2 = lambda n, sh, dt: es2.enter_context(nc.sbuf_tensor(n, sh, dt))
                P2 = lambda n, sh, *a: es2.enter_context(nc.psum_tensor(n, sh, *(a or (F32,))))
                wbufs = [S2("A_w0", [128, NDT * D], BF16), S2("A_w1", [128, NDT * D], BF16)]
                vT_s = [S2("A_vts0", [128, CH], BF16), S2("A_vts1", [128, CH], BF16)]
                ps_mm = [P2("A_pmm0", [128, CH]), P2("A_pmm1", [128, CH])]
                ps_tr = [P2("A_ptr0", [128, 128], BF16), P2("A_ptr1", [128, 128], BF16)]

                ones_tok = g.memset(vaug[:].rearrange("p (a b) -> p a b", b=HD + 1)[:, :, HD:HD + 1], 1.0)

                evac_toks = {}
                vts_tok = [None, None]
                tr_cons = [None, None]
                w_users = [[], []]
                scope_bar = ctx.bar()
                for wi, wname in enumerate(("wqT", "wkT", "wvT")):
                    ws = wi % 2
                    wbuf = wbufs[ws]
                    wtok = [load_wcast(ctx, W[wname], wbuf[:], f"qwA{ws}",
                                       pre=w_users[ws] + (scope_bar if wi < 2 else []))]
                    w_users[ws] = []
                    for ot in range(NDT):
                        for c in range(NCH):
                            pb = (ot * NCH + c) % 2
                            ps = ps_mm[pb]
                            ctx.waits(t, [evac_toks.get(pb)] + u_tok + wtok)
                            for dt in range(NDT):
                                mm = t.matmul(ps[:], wbuf[:, dt * D + ot * 128: dt * D + (ot + 1) * 128],
                                              U[:, dt * T + c * CH:dt * T + (c + 1) * CH],
                                              start=(dt == 0), stop=(dt == NDT - 1))
                            pt = ctx.inc(mm, "pe")
                            w_users[ws].append(pt)
                            if wi < 2:
                                outb = qT if wi == 0 else kT
                                ctx.wait(v, pt)
                                evac_toks[pb] = v.tensor_copy(
                                    out=outb[:, ot * T + c * CH: ot * T + (c + 1) * CH],
                                    in_=ps[:])
                            else:
                                ctx.wait(s, pt)
                                vb = pb
                                ctx.waits(s, [vts_tok[vb]])
                                ev_tok = s.activation(out=vT_s[vb][:], in_=ps[:], func=AF.Identity)
                                evac_toks[pb] = ev_tok
                                h0 = 2 * ot
                                for tj in range(4):
                                    tt = c * 4 + tj
                                    trb = tj % 2
                                    ctx.waits(t, [ev_tok, tr_cons[trb], ones_tok])
                                    tr = t.transpose(ps_tr[trb][:], vT_s[vb][:, tj * 128:(tj + 1) * 128],
                                                     CONST["identb"][:])
                                    ptok = ctx.inc(tr, "pe")
                                    w_users.append(ptok)
                                    ctx.wait(s, ptok)
                                    base = tt * HEADS * (HD + 1)
                                    s.activation(out=vaug[:, base + h0 * (HD + 1): base + h0 * (HD + 1) + HD],
                                                 in_=ps_tr[trb][:, 0:HD], func=AF.Identity)
                                    tr_cons[trb] = s.activation(
                                        out=vaug[:, base + (h0 + 1) * (HD + 1): base + (h0 + 1) * (HD + 1) + HD],
                                        in_=ps_tr[trb][:, HD:128], func=AF.Identity)
                                    vts_tok[vb] = tr_cons[trb]
                ph["attn_ready"] = [tr_cons[0], tr_cons[1], ones_tok,
                                    evac_toks.get(0), evac_toks.get(1)]

            with contextlib.ExitStack() as es3:
                S3 = lambda n, sh, dt: es3.enter_context(nc.sbuf_tensor(n, sh, dt))
                P3 = lambda n, sh: es3.enter_context(nc.psum_tensor(n, sh, F32))
                E = [S3("A_E0", [128, NTT * CH], BF16), S3("A_E1", [128, NTT * CH], BF16)]
                un = [S3("A_un0", [HD, CH], BF16), S3("A_un1", [HD, CH], BF16)]
                rd = [S3("A_rd0", [1, CH], F32), S3("A_rd1", [1, CH], F32)]
                ps_s = [[P3("A_pss00", [128, CH]), P3("A_pss01", [128, CH])],
                        [P3("A_pss10", [128, CH]), P3("A_pss11", [128, CH])]]
                ps_av = [P3("A_pav0", [128, CH]), P3("A_pav1", [128, CH])]
                ps_bc = P3("A_pbc", [HD, CH])

                ctx.waits(t, ph["attn_ready"] + ctx.bar())
                e_slot = [[None] * NTT, [None] * NTT]
                s_free = [[None, None], [None, None]]
                av_free = [None, None]
                bc_free = None
                un_free = [None, None]
                for j in range(HEADS // 2):
                    dt_h = j
                    for qc in range(NCH):
                        nkb = qc * 4 + 4
                        e_rdy = [[None] * nkb, [None] * nkb]
                        for kb in range(nkb):
                            sb = kb % 2
                            jj = kb - qc * 4
                            col0 = 0 if jj < 0 else jj * 128
                            ncols = CH - col0
                            sc_tok = [None, None]
                            for hp in range(2):
                                pb_h = 64 * hp
                                ctx.waits(t, [s_free[hp][sb]])
                                mm = t.matmul(
                                    ps_s[hp][sb][:, col0:CH],
                                    kT[pb_h:pb_h + HD, dt_h * T + kb * 128: dt_h * T + (kb + 1) * 128],
                                    qT[pb_h:pb_h + HD, dt_h * T + qc * CH + col0: dt_h * T + (qc + 1) * CH],
                                    start=True, stop=True, tile_position=(pb_h, 0))
                                sc_tok[hp] = ctx.inc(mm, "pe")
                            for hp in range(2):
                                ctx.waits(s, [sc_tok[hp], e_slot[hp][kb]])
                                ex_tok = s.activation(out=E[hp][:, kb * CH: kb * CH + ncols],
                                                      in_=ps_s[hp][sb][:, col0:CH],
                                                      func=AF.Exp, scale=0.125)
                                s_free[hp][sb] = ex_tok
                                e_rdy[hp][kb] = ex_tok
                                if jj >= 0:
                                    ctx.waits(v, [ex_tok])
                                    e_rdy[hp][kb] = v.tensor_tensor(
                                        out=E[hp][:, kb * CH: kb * CH + 128],
                                        in0=E[hp][:, kb * CH: kb * CH + 128],
                                        in1=CONST["trimask"][:], op=ALU.mult)
                        av_tok = [None, None]
                        for hp in range(2):
                            h = 2 * j + hp
                            ctx.waits(t, e_rdy[hp] + [av_free[hp]])
                            for kb in range(nkb):
                                jj = kb - qc * 4
                                col0 = 0 if jj < 0 else jj * 128
                                ncols = CH - col0
                                base = kb * HEADS * (HD + 1)
                                mm = t.matmul(
                                    ps_av[hp][0:HD + 1, col0:CH],
                                    vaug[:, base + h * (HD + 1): base + (h + 1) * (HD + 1)],
                                    E[hp][:, kb * CH: kb * CH + ncols],
                                    start=(kb == 0), stop=(kb == nkb - 1))
                            av_tok[hp] = ctx.inc(mm, "pe")
                            for kb in range(nkb):
                                e_slot[hp][kb] = av_tok[hp]
                        for hp in range(2):
                            h = 2 * j + hp
                            pb_h = 64 * hp
                            ctx.waits(s, [av_tok[hp], un_free[hp]])
                            s.activation(out=rd[hp][:], in_=ps_av[hp][64:65, :], func=AF.Ln)
                            rc_tok = s.activation(out=rd[hp][:], in_=rd[hp][:], func=AF.Exp, scale=-1.0)
                            un_tok = s.activation(out=un[hp][:], in_=ps_av[hp][0:HD, :], func=AF.Identity)
                            av_free[hp] = un_tok
                            ctx.waits(t, [rc_tok, bc_free])
                            mm = t.matmul(ps_bc[:], CONST["onesrow"][0:1, 0:HD], rd[hp][:],
                                          start=True, stop=True)
                            bc_tok = ctx.inc(mm, "pe")
                            ctx.waits(v, [bc_tok, un_tok])
                            nm_tok = v.tensor_tensor(
                                out=attn_o[pb_h:pb_h + HD, dt_h * T + qc * CH:dt_h * T + (qc + 1) * CH],
                                in0=un[hp][:], in1=ps_bc[:], op=ALU.mult)
                            bc_free = nm_tok
                            un_free[hp] = nm_tok
                ph["attn_done"] = [bc_free]

        with contextlib.ExitStack() as es4:
            S4 = lambda n, sh, dt: es4.enter_context(nc.sbuf_tensor(n, sh, dt))
            P4 = lambda n, sh: es4.enter_context(nc.psum_tensor(n, sh, F32))
            wo = S4("A_wo", [128, NDT * D], BF16)
            Y = S4("A_Y", [128, NDT * T], F32)
            sq = S4("A_sq2", [128, NDT * CH], BF16)
            rows = LNRows(S4, "A_r2")
            xr = [S4("A_xr0", [128, CH], F32), S4("A_xr1", [128, CH], F32)]
            ost = [S4("A_ost0", [128, CH], F32), S4("A_ost1", [128, CH], F32)]
            ps_mm = [P4("A_pmm2", [128, CH]), P4("A_pmm3", [128, CH])]
            ps_mean = P4("A_psm2", [1, CH])
            ps_msq = P4("A_psq2", [1, CH])
            ps_bc1 = P4("A_psb3", [128, CH])
            ps_bc2 = P4("A_psb4", [128, CH])

            ph["bar3"] = ctx.bar()
            wtok = [load_wcast(ctx, W["woT"], wo[:], "qwA0", pre=ctx.bar())]

            evac = {}
            for ot in range(NDT):
                for c in range(NCH):
                    pb = (ot * NCH + c) % 2
                    ctx.waits(t, [evac.get(pb)] + wtok + ph["attn_done"] + ph["bar3"])
                    for dt in range(NDT):
                        mm = t.matmul(ps_mm[pb][:],
                                      wo[:, dt * D + ot * 128: dt * D + (ot + 1) * 128],
                                      attn_o[:, dt * T + c * CH:dt * T + (c + 1) * CH],
                                      start=(dt == 0), stop=(dt == NDT - 1))
                    pt = ctx.inc(mm, "pe")
                    ctx.wait(s, pt)
                    evac[pb] = s.activation(out=Y[:, ot * T + c * CH: ot * T + (c + 1) * CH],
                                            in_=ps_mm[pb][:], func=AF.Identity)
            ytok = [evac.get(0), evac.get(1)]

            psum_free = {}
            bc_free = []
            st_done = {0: None, 1: None}
            for c in range(NCH):
                ytiles = [Y[:, dt * T + c * CH:dt * T + (c + 1) * CH] for dt in range(NDT)]
                st_tok = emit_ln_stats_chunk(ctx, ytiles, ps_mean, ps_msq, sq,
                                             CONST, [ytok] * NDT, psum_free)
                rdy = emit_ln_rowmath(ctx, ps_mean, ps_msq, rows, [st_tok])
                psum_free["mean"] = rdy
                psum_free["msq"] = rdy
                bt = emit_bcast2(ctx, CONST, rows.rstd, rows.b, ps_bc1, ps_bc2, rdy, bc_free)
                bc_free = []
                for dt in range(NDT):
                    b = dt % 2
                    xtok = emit_dma_waited(
                        ctx, y, xr[b][:], xin[dt * 128:(dt + 1) * 128, c * CH:(c + 1) * CH],
                        chan=f"qr{b}", pre=[st_done[b]] + (ph["bar3"] if (c == 0 and dt < 2) else []))
                    ctx.waits(v, [bt, st_done[b]])
                    v.tensor_tensor(out=ost[b][:], in0=Y[:, dt * T + c * CH:dt * T + (c + 1) * CH],
                                    in1=ps_bc1[:], op=ALU.mult)
                    v.tensor_tensor(out=ost[b][:], in0=ost[b][:], in1=ps_bc2[:], op=ALU.add)
                    ctx.waits(v, [xtok])
                    vtok = v.tensor_tensor(out=ost[b][:], in0=ost[b][:], in1=xr[b][:], op=ALU.add)
                    if dt == NDT - 1:
                        bc_free = [vtok]
                    ctx.wait(y, vtok)
                    dd = y.dma_start(out=xout[dt * 128:(dt + 1) * 128, c * CH:(c + 1) * CH],
                                     in_=ost[b][:])
                    st_done[b] = ctx.inc(dd, f"qo{b}", 16)
            return [st_done[0], st_done[1]]




NKV = 4
FFN = 2048
NEXP = 8


def emit_rms_apply(ctx, src_full, dst_full, w_pp, CONST, rows, sq, ps_msq,
                   ps_bc1, pre_toks, psum_free, bc_free):
    v = ctx.v
    last = None
    for c in range(NCH):
        tiles = [src_full[:, dt * T + c * CH:dt * T + (c + 1) * CH] for dt in range(NDT)]
        st_tok = emit_ln_stats_chunk(ctx, tiles, None, ps_msq, sq, CONST,
                                     [pre_toks] * NDT, psum_free, rms=True)
        rdy = emit_ln_rowmath(ctx, None, ps_msq, rows, [st_tok], rms=True)
        psum_free["msq"] = rdy
        bt = emit_bcast1(ctx, CONST, rows.rstd, ps_bc1, rdy, list(bc_free))
        del bc_free[:]
        for dt in range(NDT):
            ctx.waits(v, [bt])
            v.tensor_tensor(out=dst_full[:, dt * T + c * CH:dt * T + (c + 1) * CH],
                            in0=src_full[:, dt * T + c * CH:dt * T + (c + 1) * CH],
                            in1=ps_bc1[:], op=ALU.mult)
            last = v.tensor_scalar(out=dst_full[:, dt * T + c * CH:dt * T + (c + 1) * CH],
                                   in0=dst_full[:, dt * T + c * CH:dt * T + (c + 1) * CH],
                                   scalar1=w_pp[:, dt:dt + 1], scalar2=None, op0=ALU.mult)
        bc_free.append(last)
    return [last]


def emit_ln_residual_store(ctx, Yfull, xin, xout, CONST, alloc, palloc, y_toks, tag="T"):
    v, s, t, g, y = ctx.v, ctx.s, ctx.t, ctx.g, ctx.y
    sq = alloc(tag + "_sq", [128, NDT * CH], BF16)
    rows = LNRows(alloc, tag + "_r")
    xr = [alloc(tag + "_xr0", [128, CH], F32), alloc(tag + "_xr1", [128, CH], F32)]
    ost = [alloc(tag + "_ost0", [128, CH], F32), alloc(tag + "_ost1", [128, CH], F32)]
    ps_mean = palloc(tag + "_psm", [1, CH])
    ps_msq = palloc(tag + "_psq", [1, CH])
    ps_bc1 = palloc(tag + "_psb1", [128, CH])
    ps_bc2 = palloc(tag + "_psb2", [128, CH])
    bar = ctx.bar()
    psum_free = {}
    bc_free = []
    st_done = {0: None, 1: None}
    for c in range(NCH):
        ytiles = [Yfull[:, dt * T + c * CH:dt * T + (c + 1) * CH] for dt in range(NDT)]
        st_tok = emit_ln_stats_chunk(ctx, ytiles, ps_mean, ps_msq, sq,
                                     CONST, [y_toks] * NDT, psum_free)
        rdy = emit_ln_rowmath(ctx, ps_mean, ps_msq, rows, [st_tok])
        psum_free["mean"] = rdy
        psum_free["msq"] = rdy
        bt = emit_bcast2(ctx, CONST, rows.rstd, rows.b, ps_bc1, ps_bc2, rdy, bc_free)
        bc_free = []
        for dt in range(NDT):
            b = dt % 2
            xtok = emit_dma_waited(
                ctx, y, xr[b][:], xin[dt * 128:(dt + 1) * 128, c * CH:(c + 1) * CH],
                chan=f"qr{b}", pre=[st_done[b]] + (bar if (c == 0 and dt < 2) else []))
            ctx.waits(v, [bt, st_done[b]])
            v.tensor_tensor(out=ost[b][:], in0=Yfull[:, dt * T + c * CH:dt * T + (c + 1) * CH],
                            in1=ps_bc1[:], op=ALU.mult)
            v.tensor_tensor(out=ost[b][:], in0=ost[b][:], in1=ps_bc2[:], op=ALU.add)
            ctx.waits(v, [xtok])
            vtok = v.tensor_tensor(out=ost[b][:], in0=ost[b][:], in1=xr[b][:], op=ALU.add)
            if dt == NDT - 1:
                bc_free = [vtok]
            ctx.wait(y, vtok)
            dd = y.dma_start(out=xout[dt * 128:(dt + 1) * 128, c * CH:(c + 1) * CH],
                             in_=ost[b][:])
            st_done[b] = ctx.inc(dd, f"qo{b}", 16)
    return [st_done[0], st_done[1]]


def phase_l0ffn(ctx, xin, xout, W, CONST, start_toks=()):
    entry = ctx.bar() + list(start_toks)
    for _eng in (ctx.t, ctx.v, ctx.s, ctx.g, ctx.y):
        ctx.waits(_eng, entry)
    nc = ctx.nc
    v, s, t, g, y = ctx.v, ctx.s, ctx.t, ctx.g, ctx.y

    with contextlib.ExitStack() as es0:
        S0 = lambda n, sh, dt: es0.enter_context(nc.sbuf_tensor(n, sh, dt))
        R = S0("B_R", [128, NDT * T], F32)
        stage_toks_g = {}

        esq_ = contextlib.ExitStack()
        Sq = lambda n, sh, dt: esq_.enter_context(nc.sbuf_tensor(n, sh, dt))
        qT = Sq("B_qT", [128, NDT * T], BF16)
        kT = Sq("B_kT", [128, 2 * T], BF16)
        vaug = Sq("B_vaug", [128, NTT * NKV * (HD + 1)], BF16)
        esH1 = contextlib.ExitStack()
        H = esH1.enter_context(nc.sbuf_tensor("B_H1", [128, NDT * T], BF16))

        with contextlib.ExitStack() as es1:
            S1 = lambda n, sh, dt: es1.enter_context(nc.sbuf_tensor(n, sh, dt))
            P1 = lambda n, sh: es1.enter_context(nc.psum_tensor(n, sh, F32))
            xc = S1("B_xc", [128, NDT * CH], F32)
            sq = S1("B_sq", [128, NDT * CH], BF16)
            rows = LNRows(S1, "B_r1")
            ps_mean = P1("B_psm", [1, CH])
            ps_msq = P1("B_psq", [1, CH])
            ps_bc1 = P1("B_psb1", [128, CH])
            ps_bc2 = P1("B_psb2", [128, CH])
            xld = [None] * NDT
            rc_tok = [None] * NDT
            psum_free = {}
            bc_free = []
            for c in range(NCH):
                for dt in range(NDT):
                    xld[dt] = emit_dma_waited(
                        ctx, y, xc[:, dt * CH:(dt + 1) * CH],
                        xin[dt * 128:(dt + 1) * 128, c * CH:(c + 1) * CH],
                        chan=f"qx{dt}",
                        pre=[rc_tok[dt]] + ((ctx.bar() + list(start_toks)) if c == 0 else []))
                xtiles = [xc[:, dt * CH:(dt + 1) * CH] for dt in range(NDT)]
                st_tok = emit_ln_stats_chunk(ctx, xtiles, ps_mean, ps_msq, sq,
                                             CONST, [[x] for x in xld], psum_free)
                rdy = emit_ln_rowmath(ctx, ps_mean, ps_msq, rows, [st_tok])
                psum_free["mean"] = rdy
                psum_free["msq"] = rdy
                bt = emit_bcast2(ctx, CONST, rows.rstd, rows.b, ps_bc1, ps_bc2, rdy, bc_free)
                for dt in range(NDT):
                    ctx.waits(v, [bt])
                    v.tensor_tensor(out=R[:, dt * T + c * CH:dt * T + (c + 1) * CH],
                                    in0=xc[:, dt * CH:(dt + 1) * CH], in1=ps_bc1[:], op=ALU.mult)
                    rc_tok[dt] = v.tensor_tensor(
                        out=R[:, dt * T + c * CH:dt * T + (c + 1) * CH],
                        in0=R[:, dt * T + c * CH:dt * T + (c + 1) * CH],
                        in1=ps_bc2[:], op=ALU.add)
                bc_free = [rc_tok[NDT - 1]]
            r_tok = [rc_tok[NDT - 1]]

        with contextlib.ExitStack() as es2:
            S2 = lambda n, sh, dt: es2.enter_context(nc.sbuf_tensor(n, sh, dt))
            P2 = lambda n, sh: es2.enter_context(nc.psum_tensor(n, sh, F32))
            sq = S2("B_sq2", [128, NDT * CH], BF16)
            rows = LNRows(S2, "B_r2")
            ln1 = S2("B_ln1", [128, NDT], F32)
            ps_msq = P2("B_psq2", [1, CH])
            ps_bc1 = P2("B_psb3", [128, CH])
            lt = emit_dma_waited(ctx, y, ln1[:], W["ln1_pp"], chan="qv0", pre=ctx.bar())
            ctx.waits(v, [lt])
            h_tok = emit_rms_apply(ctx, R, H, ln1, CONST, rows, sq, ps_msq, ps_bc1,
                                   r_tok, {}, [])

        if True:
            with contextlib.ExitStack() as es3:
                S3 = lambda n, sh, dt: es3.enter_context(nc.sbuf_tensor(n, sh, dt))
                P3 = lambda n, sh, *a: es3.enter_context(nc.psum_tensor(n, sh, *(a or (F32,))))
                wbufs = [S3("B_wqkv0", [128, NDT * 768], BF16), S3("B_wqkv1", [128, NDT * 768], BF16)]
                vT_s = [S3("B_vts0", [128, CH], BF16), S3("B_vts1", [128, CH], BF16)]
                ps_mm = [P3("B_pmm0", [128, CH]), P3("B_pmm1", [128, CH])]
                ps_tr = [P3("B_ptr0", [128, 128], BF16), P3("B_ptr1", [128, 128], BF16)]

                ones_tok = g.memset(
                    vaug[:].rearrange("p (a b) -> p a b", b=HD + 1)[:, :, HD:HD + 1], 1.0)
                evac_toks = {}
                vts_tok = [None, None]
                tr_cons = [None, None]
                wtok = []
                w_users = []
                for ot in range(12):
                    if ot % 6 == 0:
                        half = ot // 6
                        wbuf = wbufs[half]
                        wtok = [load_wcast(
                            ctx, W["wqkvT"][:, half * 6144:(half + 1) * 6144], wbuf[:], f"qwB{half}",
                            pre=(ctx.bar() if ot == 0 else []))]
                    oth = ot % 6
                    for c in range(NCH):
                        pb = (ot * NCH + c) % 2
                        ps = ps_mm[pb]
                        ctx.waits(t, [evac_toks.get(pb)] + h_tok + wtok)
                        for dt in range(NDT):
                            mm = t.matmul(ps[:], wbuf[:, dt * 768 + oth * 128: dt * 768 + (oth + 1) * 128],
                                          H[:, dt * T + c * CH:dt * T + (c + 1) * CH],
                                          start=(dt == 0), stop=(dt == NDT - 1))
                        pt = ctx.inc(mm, "pe")
                        if ot < 8:
                            ctx.wait(v, pt)
                            evac_toks[pb] = v.tensor_copy(
                                out=qT[:, ot * T + c * CH: ot * T + (c + 1) * CH], in_=ps[:])
                        elif ot < 10:
                            ctx.wait(v, pt)
                            evac_toks[pb] = v.tensor_copy(
                                out=kT[:, (ot - 8) * T + c * CH: (ot - 8) * T + (c + 1) * CH], in_=ps[:])
                        else:
                            ctx.wait(s, pt)
                            vb = pb
                            ctx.waits(s, [vts_tok[vb]])
                            ev_tok = s.activation(out=vT_s[vb][:], in_=ps[:], func=AF.Identity)
                            evac_toks[pb] = ev_tok
                            kv0 = 2 * (ot - 10)
                            for tj in range(4):
                                tt = c * 4 + tj
                                trb = tj % 2
                                ctx.waits(t, [ev_tok, tr_cons[trb], ones_tok])
                                tr = t.transpose(ps_tr[trb][:], vT_s[vb][:, tj * 128:(tj + 1) * 128],
                                                 CONST["identb"][:])
                                ptok = ctx.inc(tr, "pe")
                                ctx.wait(s, ptok)
                                base = tt * NKV * (HD + 1)
                                s.activation(out=vaug[:, base + kv0 * (HD + 1): base + kv0 * (HD + 1) + HD],
                                             in_=ps_tr[trb][:, 0:HD], func=AF.Identity)
                                tr_cons[trb] = s.activation(
                                    out=vaug[:, base + (kv0 + 1) * (HD + 1): base + (kv0 + 1) * (HD + 1) + HD],
                                    in_=ps_tr[trb][:, HD:128], func=AF.Identity)
                                vts_tok[vb] = tr_cons[trb]
                qkv_done = [tr_cons[0], tr_cons[1], ones_tok,
                            evac_toks.get(0), evac_toks.get(1)]

            esH1.close()
            attn_o = Sq("B_attnO", [128, NDT * T], BF16)

            with contextlib.ExitStack() as es4:
                S4 = lambda n, sh, dt: es4.enter_context(nc.sbuf_tensor(n, sh, dt))
                cosf = S4("B_cos", [128, T], BF16)
                sinf = S4("B_sin", [128, T], BF16)
                tmp = [S4(f"B_rt{i}", [128, CH], F32) for i in range(3)]
                bar4 = ctx.bar()
                c1 = load_wcast(ctx, W["cosF"], cosf[:], "qwB", pre=bar4)
                c2 = load_wcast(ctx, W["sinF"], sinf[:], "qwB", pre=[])
                ctx.waits(v, [c1, c2] + qkv_done)
                pairs = [(kT, 0, kT, T)] + [(qT, p * T, qT, (4 + p) * T) for p in range(4)]
                rope_pair_tok = []
                for (b1, o1, b2, o2) in pairs:
                    rope_last = None
                    for c in range(NCH):
                        sl = slice(c * CH, (c + 1) * CH)
                        a1 = b1[:, o1 + c * CH: o1 + (c + 1) * CH]
                        a2 = b2[:, o2 + c * CH: o2 + (c + 1) * CH]
                        v.tensor_tensor(out=tmp[0][:], in0=a1, in1=sinf[:, sl], op=ALU.mult)
                        v.tensor_tensor(out=tmp[1][:], in0=a1, in1=cosf[:, sl], op=ALU.mult)
                        v.tensor_tensor(out=tmp[2][:], in0=a2, in1=sinf[:, sl], op=ALU.mult)
                        v.tensor_tensor(out=a1, in0=tmp[1][:], in1=tmp[2][:], op=ALU.subtract)
                        v.tensor_tensor(out=tmp[1][:], in0=a2, in1=cosf[:, sl], op=ALU.mult)
                        rope_last = v.tensor_tensor(out=a2, in0=tmp[1][:], in1=tmp[0][:], op=ALU.add)
                    rope_pair_tok.append(rope_last)
                rope_tok = [rope_pair_tok[-1]]

            with contextlib.ExitStack() as es5:
                S5 = lambda n, sh, dt: es5.enter_context(nc.sbuf_tensor(n, sh, dt))
                P5 = lambda n, sh: es5.enter_context(nc.psum_tensor(n, sh, F32))
                E = [S5("B_E0", [128, NTT * CH], BF16), S5("B_E1", [128, NTT * CH], BF16)]
                unb = S5("B_un", [128, CH], BF16)
                un = [unb[0:64, :], unb[64:128, :]]
                rd = [S5("B_rd0", [1, CH], BF16), S5("B_rd1", [1, CH], BF16)]
                ps_s = [[P5("B_pss00", [128, CH]), P5("B_pss01", [128, CH])],
                        [P5("B_pss10", [128, CH]), P5("B_pss11", [128, CH])]]
                ps_av = [P5("B_pav0", [128, CH]), P5("B_pav1", [128, CH])]
                ps_bc = P5("B_pbc", [HD, CH])

                bar5 = ctx.bar()
                e_slot = [[None] * NTT, [None] * NTT]
                s_free = [[None, None], [None, None]]
                av_free = [None, None]
                bc_free = None
                un_free = [None, None]
                for hpair in range(HEADS // 2):
                    h0 = 2 * hpair
                    kvh = h0 // 4
                    qt1 = h0 // 4
                    ctx.waits(t, [rope_pair_tok[0], rope_pair_tok[1 + qt1]] +
                              (bar5 if hpair == 0 else []))
                    for qc in range(NCH):
                        nkb = qc * 4 + 4
                        e_rdy = [[None] * nkb, [None] * nkb]
                        for kb in range(nkb):
                            sb = kb % 2
                            jj = kb - qc * 4
                            col0 = 0 if jj < 0 else jj * 128
                            ncols = CH - col0
                            sc_tok = [None, None]
                            for hp in range(2):
                                h = h0 + hp
                                pb_h = 32 * (h % 4)
                                tp = (pb_h, 0)
                                ctx.waits(t, [s_free[hp][sb]])
                                t.matmul(
                                    ps_s[hp][sb][:, col0:CH],
                                    kT[pb_h:pb_h + 32, kb * 128:(kb + 1) * 128],
                                    qT[pb_h:pb_h + 32, qt1 * T + qc * CH + col0: qt1 * T + (qc + 1) * CH],
                                    start=True, stop=False, tile_position=tp)
                                mm = t.matmul(
                                    ps_s[hp][sb][:, col0:CH],
                                    kT[pb_h:pb_h + 32, T + kb * 128: T + (kb + 1) * 128],
                                    qT[pb_h:pb_h + 32, (4 + qt1) * T + qc * CH + col0: (4 + qt1) * T + (qc + 1) * CH],
                                    start=False, stop=True, tile_position=tp)
                                sc_tok[hp] = ctx.inc(mm, "pe")
                            for hp in range(2):
                                ctx.waits(s, [sc_tok[hp], e_slot[hp][kb]])
                                ex_tok = s.activation(out=E[hp][:, kb * CH: kb * CH + ncols],
                                                      in_=ps_s[hp][sb][:, col0:CH],
                                                      func=AF.Exp, scale=0.125)
                                s_free[hp][sb] = ex_tok
                                e_rdy[hp][kb] = ex_tok
                                if jj >= 0:
                                    ctx.waits(v, [ex_tok])
                                    e_rdy[hp][kb] = v.tensor_tensor(
                                        out=E[hp][:, kb * CH: kb * CH + 128],
                                        in0=E[hp][:, kb * CH: kb * CH + 128],
                                        in1=CONST["trimask"][:], op=ALU.mult)
                        av_tok = [None, None]
                        for hp in range(2):
                            ctx.waits(t, e_rdy[hp] + [av_free[hp]])
                            for kb in range(nkb):
                                jj = kb - qc * 4
                                col0 = 0 if jj < 0 else jj * 128
                                ncols = CH - col0
                                base = kb * NKV * (HD + 1)
                                mm = t.matmul(
                                    ps_av[hp][0:HD + 1, col0:CH],
                                    vaug[:, base + kvh * (HD + 1): base + (kvh + 1) * (HD + 1)],
                                    E[hp][:, kb * CH: kb * CH + ncols],
                                    start=(kb == 0), stop=(kb == nkb - 1))
                            av_tok[hp] = ctx.inc(mm, "pe")
                            for kb in range(nkb):
                                e_slot[hp][kb] = av_tok[hp]
                        for hp in range(2):
                            h = h0 + hp
                            ctx.waits(s, [av_tok[hp], un_free[hp]])
                            s.activation(out=rd[hp][:], in_=ps_av[hp][64:65, :], func=AF.Ln)
                            rc_tok2 = s.activation(out=rd[hp][:], in_=rd[hp][:], func=AF.Exp, scale=-1.0)
                            un_tok = s.activation(out=un[hp], in_=ps_av[hp][0:HD, :], func=AF.Identity)
                            av_free[hp] = un_tok
                            ctx.waits(t, [rc_tok2, bc_free])
                            mm = t.matmul(ps_bc[:], CONST["onesrow_b"][0:1, 0:HD], rd[hp][:],
                                          start=True, stop=True)
                            bc_tok = ctx.inc(mm, "pe")
                            ctx.waits(v, [bc_tok, un_tok])
                            nm_tok = v.tensor_tensor(
                                out=attn_o[64 * (h % 2):64 * (h % 2) + HD,
                                           (h // 2) * T + qc * CH:(h // 2) * T + (qc + 1) * CH],
                                in0=un[hp], in1=ps_bc[:], op=ALU.mult)
                            bc_free = nm_tok
                            un_free[hp] = nm_tok
                attn_done = [bc_free]

            with contextlib.ExitStack() as es6:
                S6 = lambda n, sh, dt: es6.enter_context(nc.sbuf_tensor(n, sh, dt))
                P6 = lambda n, sh: es6.enter_context(nc.psum_tensor(n, sh, F32))
                wo = S6("B_wo", [128, NDT * D], BF16)
                ps_mm = [P6("B_pmm2", [128, CH]), P6("B_pmm3", [128, CH])]
                wtok = [load_wcast(ctx, W["woT"], wo[:], "qwB", pre=ctx.bar())]
                evac = {}
                bar6 = ctx.bar()
                radd = None
                for ot in range(NDT):
                    for c in range(NCH):
                        pb = (ot * NCH + c) % 2
                        ctx.waits(t, [evac.get(pb)] + wtok + attn_done + bar6)
                        for dt in range(NDT):
                            mm = t.matmul(ps_mm[pb][:],
                                          wo[:, dt * D + ot * 128: dt * D + (ot + 1) * 128],
                                          attn_o[:, dt * T + c * CH:dt * T + (c + 1) * CH],
                                          start=(dt == 0), stop=(dt == NDT - 1))
                        pt = ctx.inc(mm, "pe")
                        ctx.waits(v, [pt])
                        radd = v.tensor_tensor(
                            out=R[:, ot * T + c * CH: ot * T + (c + 1) * CH],
                            in0=R[:, ot * T + c * CH: ot * T + (c + 1) * CH],
                            in1=ps_mm[pb][:], op=ALU.add)
                        evac[pb] = radd
                r2_tok = [radd]

        esq_.close()
        esH2 = contextlib.ExitStack()
        H = esH2.enter_context(nc.sbuf_tensor("B_H2", [128, NDT * T], BF16))

        with contextlib.ExitStack() as es7:
            S7 = lambda n, sh, dt: es7.enter_context(nc.sbuf_tensor(n, sh, dt))
            P7 = lambda n, sh: es7.enter_context(nc.psum_tensor(n, sh, F32))
            sq = S7("B_sq7", [128, NDT * CH], BF16)
            rows = LNRows(S7, "B_r7")
            ln2 = S7("B_ln2", [128, NDT], F32)
            ps_msq = P7("B_psq7", [1, CH])
            ps_bc1 = P7("B_psb7", [128, CH])
            lt = emit_dma_waited(ctx, y, ln2[:], W["ln2_pp"], chan="qv0", pre=ctx.bar())
            ctx.waits(v, [lt])
            h2_tok = emit_rms_apply(ctx, R, H, ln2, CONST, rows, sq, ps_msq, ps_bc1,
                                    r2_tok, {}, [])

        esA = contextlib.ExitStack()
        SA = lambda n, sh, dt: esA.enter_context(nc.sbuf_tensor(n, sh, dt))
        g_all = SA("B_gall", [128, NTT * 1024], BF16)
        w1s = [SA("B_w1s0", [128, NDT * CH], BF16), SA("B_w1s1", [128, NDT * CH], BF16)]
        w3s = [SA("B_w3s0", [128, NDT * CH], BF16), SA("B_w3s1", [128, NDT * CH], BF16)]
        _gs = SA("B_gsil0", [128, CH], F32)
        gsil = [_gs, _gs]
        esP = contextlib.ExitStack()
        PA = lambda n, sh: esP.enter_context(nc.psum_tensor(n, sh, F32))
        ps_g1 = PA("B_psg1", [128, CH])
        ps_g3 = PA("B_psg3", [128, CH])
        ps_o = [PA("B_pso0", [128, CH]), PA("B_pso1", [128, CH])]
        ps_wb = PA("B_pswb", [128, CH])
        bar9 = ctx.bar()

        wf_dram = nc.dram_tensor("B_wf_dram", [NEXP, T], F32, kind="Internal").ap()
        esM = contextlib.ExitStack()
        wfT = esM.enter_context(nc.sbuf_tensor("B_wfT", [NEXP, T], F32))

        with contextlib.ExitStack() as es8:
            S8 = lambda n, sh, dt: es8.enter_context(nc.sbuf_tensor(n, sh, dt))
            P8 = lambda n, sh: es8.enter_context(nc.psum_tensor(n, sh, F32))
            gw = S8("B_gw", [128, NDT * NEXP], BF16)
            logitT = S8("B_logT", [NEXP, T], F32)
            ltm = S8("B_ltm", [128, NTT * NEXP], F32)
            prob = S8("B_prob", [128, NEXP], F32)
            eq1 = S8("B_eq1", [128, NEXP], F32)
            m1 = S8("B_m1", [128, 1], F32)
            m2 = S8("B_m2", [128, 1], F32)
            ws = S8("B_ws", [128, 1], F32)
            ps_g = P8("B_psg", [NEXP, CH])
            ps_t = [P8("B_pst0", [128, 128]), P8("B_pst1", [128, 128])]
            gt = [load_wcast(ctx, W["gate_wT"], gw[:], "qwB", pre=ctx.bar())]
            glast = None
            for c in range(NCH):
                ctx.waits(t, gt + [glast] + h2_tok)
                for dt in range(NDT):
                    mm = t.matmul(ps_g[:], gw[:, dt * NEXP:(dt + 1) * NEXP],
                                  H[:, dt * T + c * CH:dt * T + (c + 1) * CH],
                                  start=(dt == 0), stop=(dt == NDT - 1))
                pt = ctx.inc(mm, "pe")
                ctx.wait(s, pt)
                glast = s.activation(out=logitT[:, c * CH:(c + 1) * CH], in_=ps_g[:],
                                     func=AF.Identity)
            tr_done = [None, None]
            tm_tok = [None] * NTT
            for tt in range(NTT):
                b = tt % 2
                ctx.waits(t, [glast, tr_done[b]])
                tr = t.transpose(ps_t[b][0:128, 0:NEXP], logitT[:, tt * 128:(tt + 1) * 128],
                                 CONST["ident"][0:NEXP, 0:NEXP])
                ptok = ctx.inc(tr, "pe")
                ctx.wait(s, ptok)
                tm_tok[tt] = s.activation(out=ltm[:, tt * NEXP:(tt + 1) * NEXP],
                                          in_=ps_t[b][0:128, 0:NEXP], func=AF.Identity)
                tr_done[b] = tm_tok[tt]
            for tt in range(NTT):
                sl = ltm[:, tt * NEXP:(tt + 1) * NEXP]
                ctx.waits(v, [tm_tok[tt]])
                v.tensor_reduce(out=m1[:], in_=sl, axis=AX.X, op=ALU.max)
                tv = v.tensor_scalar(out=prob[:], in0=sl, scalar1=m1[:], scalar2=None,
                                     op0=ALU.subtract)
                ctx.waits(s, [tv])
                et = s.activation(out=prob[:], in_=prob[:], func=AF.Exp)
                ctx.waits(v, [et])
                v.tensor_reduce(out=ws[:], in_=prob[:], axis=AX.X, op=ALU.add)
                v.reciprocal(out=ws[:], in_=ws[:])
                v.tensor_scalar(out=prob[:], in0=prob[:], scalar1=ws[:], scalar2=None, op0=ALU.mult)
                v.tensor_reduce(out=m1[:], in_=prob[:], axis=AX.X, op=ALU.max)
                v.tensor_scalar(out=eq1[:], in0=prob[:], scalar1=m1[:], scalar2=None, op0=ALU.is_ge)
                v.scalar_tensor_tensor(out=prob[:], in0=eq1[:], scalar=-1.0e30, in1=prob[:],
                                       op0=ALU.mult, op1=ALU.add)
                v.tensor_reduce(out=m2[:], in_=prob[:], axis=AX.X, op=ALU.max)
                v.tensor_scalar(out=prob[:], in0=prob[:], scalar1=m2[:], scalar2=None, op0=ALU.is_ge)
                v.tensor_tensor(out=ws[:], in0=m1[:], in1=m2[:], op=ALU.add)
                v.reciprocal(out=ws[:], in_=ws[:])
                v.tensor_tensor(out=m1[:], in0=m1[:], in1=ws[:], op=ALU.mult)
                v.tensor_tensor(out=m2[:], in0=m2[:], in1=ws[:], op=ALU.mult)
                v.tensor_scalar(out=eq1[:], in0=eq1[:], scalar1=m1[:], scalar2=None, op0=ALU.mult)
                v.scalar_tensor_tensor(out=sl, in0=prob[:], scalar=m2[:], in1=eq1[:],
                                       op0=ALU.mult, op1=ALU.add)
            tr_done = [None, None]
            wf_last = None
            for tt in range(NTT):
                b = tt % 2
                ctx.waits(t, [("dve", ctx.counts["dve"]), tr_done[b]])
                tr = t.transpose(ps_t[b][0:NEXP, 0:128], ltm[:, tt * NEXP:(tt + 1) * NEXP],
                                 CONST["ident"][:])
                ptok = ctx.inc(tr, "pe")
                ctx.wait(s, ptok)
                wf_last = s.activation(out=wfT[:, tt * 128:(tt + 1) * 128],
                                       in_=ps_t[b][0:NEXP, 0:128], func=AF.Identity)
                tr_done[b] = wf_last
            ctx.waits(y, [wf_last])
            wfd = y.dma_start(out=wf_dram, in_=wfT[:])
            wf_tok = [ctx.inc(wfd, "qwf", 16)]

        esM.close()

        esW = contextlib.ExitStack()
        w2b = esW.enter_context(nc.sbuf_tensor("B_w2b", [128, NTT * 1024], BF16))
        wf_row = esW.enter_context(nc.sbuf_tensor("B_wfrow", [1, CH], F32))
        wb_sb = esW.enter_context(nc.sbuf_tensor("B_wb", [128, CH], F32))
        barA = ctx.bar()
        tmpo = gsil[0]
        wf_guard = [None]
        radd = None
        b_done = None
        slab_guard = [None, None]
        gs_tok = [None, None]
        g1_free = None
        g3_free = None
        evac = {}
        wb_free = None
        w2_users = None
        for e in range(NEXP):
            if True:
                pre_toks = {}
                for slab in range(2):
                    sb = slab % 2
                    pre_toks[(1, slab)] = load_wcast(
                        ctx, W["w1T"][e * 128:(e + 1) * 128, slab * 4096:(slab + 1) * 4096],
                        w1s[sb][:], f"qs1{sb}",
                        pre=[slab_guard[sb]] + (bar9 if e == 0 else []))
                    pre_toks[(3, slab)] = load_wcast(
                        ctx, W["w3T"][e * 128:(e + 1) * 128, slab * 4096:(slab + 1) * 4096],
                        w3s[sb][:], f"qs3{sb}",
                        pre=[slab_guard[sb]] + (bar9 if e == 0 else []))
                w2tok = [load_wcast(ctx, W["w2T"][e * 128:(e + 1) * 128, :], w2b[:], "qw2B",
                                    pre=[b_done, w2_users] + (barA if e == 0 else []))]
                for half in range(2):
                    for slab in range(4):
                        sb = slab % 2
                        if (1, slab) in pre_toks:
                            t1 = pre_toks.pop((1, slab))
                            t3 = pre_toks.pop((3, slab))
                        else:
                            t1 = load_wcast(
                                ctx, W["w1T"][e * 128:(e + 1) * 128, slab * 4096:(slab + 1) * 4096],
                                w1s[sb][:], f"qs1{sb}", pre=[slab_guard[sb]])
                            t3 = load_wcast(
                                ctx, W["w3T"][e * 128:(e + 1) * 128, slab * 4096:(slab + 1) * 4096],
                                w3s[sb][:], f"qs3{sb}", pre=[slab_guard[sb]])
                        for fs in range(4):
                            fsg = slab * 4 + fs
                            for ch in range(2):
                                c = half * 2 + ch
                                ctx.waits(t, [t1, t3, g1_free, b_done] + h2_tok)
                                for dt in range(NDT):
                                    mm = t.matmul(
                                        ps_g1[:], w1s[sb][:, dt * CH + fs * 128: dt * CH + (fs + 1) * 128],
                                        H[:, dt * T + c * CH:dt * T + (c + 1) * CH],
                                        start=(dt == 0), stop=(dt == NDT - 1))
                                g1_tok = ctx.inc(mm, "pe")
                                ctx.waits(t, [g3_free])
                                for dt in range(NDT):
                                    mm = t.matmul(
                                        ps_g3[:], w3s[sb][:, dt * CH + fs * 128: dt * CH + (fs + 1) * 128],
                                        H[:, dt * T + c * CH:dt * T + (c + 1) * CH],
                                        start=(dt == 0), stop=(dt == NDT - 1))
                                g3_tok = ctx.inc(mm, "pe")
                                slab_guard[sb] = g3_tok
                                ctx.waits(s, [g1_tok, gs_tok[0]])
                                sil = s.activation(out=gsil[0][:], in_=ps_g1[:], func=AF.Silu)
                                g1_free = sil
                                ctx.waits(v, [sil, g3_tok, b_done])
                                gt2 = v.tensor_tensor(
                                    out=g_all[:, fsg * 1024 + ch * CH: fsg * 1024 + (ch + 1) * CH],
                                    in0=gsil[0][:], in1=ps_g3[:], op=ALU.mult)
                                gs_tok[0] = gt2
                                g3_free = gt2
                    ga_tok = [gs_tok[0], gs_tok[1]]
                    for ch in range(2):
                        c = half * 2 + ch
                        ctx.waits(y, wf_tok + [wf_guard[0]] + (bar9 if (e == 0 and half == 0 and ch == 0) else []))
                        wfr = y.dma_start(out=wf_row[:], in_=wf_dram[e:e + 1, c * CH:(c + 1) * CH])
                        wfr_tok = ctx.inc(wfr, "qwf", 16)
                        ctx.waits(t, [wfr_tok, wb_free])
                        mm = t.matmul(ps_wb[:], CONST["onesrow"][0:1, 0:128],
                                      wf_row[0:1, :], start=True, stop=True)
                        wbt = ctx.inc(mm, "pe")
                        wf_guard[0] = wbt
                        ctx.waits(s, [wbt, wb_free])
                        wb_ev = s.activation(out=wb_sb[:], in_=ps_wb[:], func=AF.Identity)
                        for ot in range(NDT):
                            pb = (ch * NDT + ot) % 2
                            ctx.waits(t, [evac.get(pb)] + ga_tok + w2tok)
                            for ft in range(NTT):
                                mm = t.matmul(
                                    ps_o[pb][:], w2b[:, ft * 1024 + ot * 128: ft * 1024 + (ot + 1) * 128],
                                    g_all[:, ft * 1024 + ch * CH: ft * 1024 + (ch + 1) * CH],
                                    start=(ft == 0), stop=(ft == NTT - 1))
                            pt = ctx.inc(mm, "pe")
                            b_done = pt
                            w2_users = pt
                            ctx.waits(v, [pt, wb_ev])
                            wb_free = v.tensor_tensor(out=tmpo[:], in0=ps_o[pb][:], in1=wb_sb[:], op=ALU.mult)
                            radd = v.tensor_tensor(
                                out=R[:, ot * T + c * CH: ot * T + (c + 1) * CH],
                                in0=R[:, ot * T + c * CH: ot * T + (c + 1) * CH],
                                in1=tmpo[:], op=ALU.add)
                            evac[pb] = radd
                    b_done = radd
        moe_tok = [radd]
        esW.close()
        esP.close()
        esA.close()
        esH2.close()

        with contextlib.ExitStack() as esT:
            ST = lambda n, sh, dt: esT.enter_context(nc.sbuf_tensor(n, sh, dt))
            PT = lambda n, sh: esT.enter_context(nc.psum_tensor(n, sh, F32))
            return emit_ln_residual_store(ctx, R, xin, xout, CONST, ST, PT, moe_tok, tag="BT")




def phase_l1attn(ctx, xin, xout, W, CONST, start_toks=()):
    entry = ctx.bar() + list(start_toks)
    for _eng in (ctx.t, ctx.v, ctx.s, ctx.g, ctx.y):
        ctx.waits(_eng, entry)
    nc = ctx.nc
    v, s, t, g, y = ctx.v, ctx.s, ctx.t, ctx.g, ctx.y
    NEG = -30.0
    EPS9 = 1e-9

    with contextlib.ExitStack() as es0:
        S0 = lambda n, sh, dt: es0.enter_context(nc.sbuf_tensor(n, sh, dt))
        Ybf = S0("C_Y", [128, NDT * T], BF16)
        vecs = S0("C_vecs", [128, 6 * NDT], F32)
        vt = emit_dma_waited(ctx, y, vecs[:, 0:5 * NDT], W["vecs_pp"], chan="qv0", pre=ctx.bar())
        ctx.waits(s, [vt])
        ew_tok = s.activation(out=vecs[:, 5 * NDT:6 * NDT], in_=vecs[:, 3 * NDT:4 * NDT],
                              func=AF.Exp)
        TMK, TMV, TMR, DEC, FIR, EW = range(6)

        with contextlib.ExitStack() as es1:
            S1 = lambda n, sh, dt: es1.enter_context(nc.sbuf_tensor(n, sh, dt))
            xk = S1("C_xk", [128, NDT * T], BF16)
            xv = S1("C_xv", [128, NDT * T], BF16)
            xr = S1("C_xr", [128, NDT * T], BF16)

            with contextlib.ExitStack() as es2:
                S2 = lambda n, sh, dt: es2.enter_context(nc.sbuf_tensor(n, sh, dt))
                P2 = lambda n, sh: es2.enter_context(nc.psum_tensor(n, sh, F32))
                U = S2("C_U", [128, NDT * T], BF16)
                xc = S2("C_xc", [128, NDT * CH], F32)
                sq = S2("C_sq", [128, NDT * CH], BF16)
                tmpm = S2("C_tmpm", [128, T], F32)
                rows = LNRows(S2, "C_r1")
                ps_mean = P2("C_psm", [1, CH])
                ps_msq = P2("C_psq", [1, CH])
                ps_bc1 = P2("C_psb1", [128, CH])
                ps_bc2 = P2("C_psb2", [128, CH])
                xld = [None] * NDT
                uc_tok = [None] * NDT
                psum_free = {}
                bc_free = []
                for c in range(NCH):
                    for dt in range(NDT):
                        xld[dt] = emit_dma_waited(
                            ctx, y, xc[:, dt * CH:(dt + 1) * CH],
                            xin[dt * 128:(dt + 1) * 128, c * CH:(c + 1) * CH],
                            chan=f"qx{dt}",
                            pre=[uc_tok[dt]] + ((ctx.bar() + list(start_toks)) if c == 0 else []))
                    xtiles = [xc[:, dt * CH:(dt + 1) * CH] for dt in range(NDT)]
                    st_tok = emit_ln_stats_chunk(ctx, xtiles, ps_mean, ps_msq, sq,
                                                 CONST, [[x] for x in xld], psum_free)
                    rdy = emit_ln_rowmath(ctx, ps_mean, ps_msq, rows, [st_tok])
                    psum_free["mean"] = rdy
                    psum_free["msq"] = rdy
                    bt = emit_bcast2(ctx, CONST, rows.rstd, rows.b, ps_bc1, ps_bc2, rdy, bc_free)
                    for dt in range(NDT):
                        ctx.waits(v, [bt])
                        v.tensor_tensor(out=U[:, dt * T + c * CH:dt * T + (c + 1) * CH],
                                        in0=xc[:, dt * CH:(dt + 1) * CH], in1=ps_bc1[:], op=ALU.mult)
                        uc_tok[dt] = v.tensor_tensor(
                            out=U[:, dt * T + c * CH:dt * T + (c + 1) * CH],
                            in0=U[:, dt * T + c * CH:dt * T + (c + 1) * CH],
                            in1=ps_bc2[:], op=ALU.add)
                    bc_free = [uc_tok[NDT - 1]]
                ctx.waits(v, [uc_tok[NDT - 1], ew_tok])
                mx_src = {}
                for dt in range(NDT):
                    sc = vecs[:, TMK * NDT + dt: TMK * NDT + dt + 1]
                    v.tensor_tensor(out=tmpm[:, 1:T], in0=U[:, dt * T + 1: dt * T + T],
                                    in1=U[:, dt * T: dt * T + T - 1], op=ALU.subtract)
                    v.scalar_tensor_tensor(out=xk[:, dt * T + 1: dt * T + T],
                                           in0=tmpm[:, 1:T], scalar=sc,
                                           in1=U[:, dt * T: dt * T + T - 1],
                                           op0=ALU.mult, op1=ALU.add)
                    mx_src[TMK] = v.tensor_scalar(out=xk[:, dt * T: dt * T + 1],
                                                  in0=U[:, dt * T: dt * T + 1],
                                                  scalar1=sc, scalar2=None, op0=ALU.mult)
                for dt in range(NDT):
                    v.tensor_tensor(out=tmpm[:, 1:T], in0=U[:, dt * T + 1: dt * T + T],
                                    in1=U[:, dt * T: dt * T + T - 1], op=ALU.subtract)
                    for (dst, tmi) in ((xv, TMV), (xr, TMR)):
                        sc = vecs[:, tmi * NDT + dt: tmi * NDT + dt + 1]
                        v.scalar_tensor_tensor(out=dst[:, dt * T + 1: dt * T + T],
                                               in0=tmpm[:, 1:T], scalar=sc,
                                               in1=U[:, dt * T: dt * T + T - 1],
                                               op0=ALU.mult, op1=ALU.add)
                        mx_src[tmi] = v.tensor_scalar(out=dst[:, dt * T: dt * T + 1],
                                                      in0=U[:, dt * T: dt * T + 1],
                                                      scalar1=sc, scalar2=None, op0=ALU.mult)
                mx_tok = [mx_src[TMR]]
                mx_by_wi = [mx_src[TMK], mx_src[TMV], mx_src[TMR]]

            with contextlib.ExitStack() as es3:
                S3 = lambda n, sh, dt: es3.enter_context(nc.sbuf_tensor(n, sh, dt))
                P3 = lambda n, sh: es3.enter_context(nc.psum_tensor(n, sh, F32))
                wk = S3("C_wk", [128, NDT * 128], BF16)
                wv = S3("C_wv", [128, NDT * 128], BF16)
                wr = S3("C_wr", [128, NDT * 128], BF16)
                kbufs = [S3("C_kd0", [128, T], F32), S3("C_kd1", [128, T], F32)]
                vbufs = [S3("C_vd0", [128, T], F32), S3("C_vd1", [128, T], F32)]
                ekv = S3("C_ekv", [128, T], F32)
                pst = S3("C_pst", [128, T + 1], F32)
                nst = S3("C_nst", [128, T + 1], F32)
                A1 = S3("C_A1", [128, T], F32)
                rbuf = S3("C_rd", [128, T], BF16)
                ps_mm = [P3("C_pmm0", [128, CH]), P3("C_pmm1", [128, CH])]

                y_tok = None
                prev_y = None
                prev_ecv = None
                kv_consumed = [None, None]
                evac = {}
                w_guard = [None, None, None]
                for dt in range(NDT):
                    idx = dt % 2
                    kb_ = kbufs[idx]
                    vb_ = vbufs[idx]
                    wtok = []
                    for wi_, (wb_, wname) in enumerate(((wk, "wkT"), (wv, "wvT"), (wr, "wrT"))):
                        wtok.append(load_wcast(
                            ctx, W[wname][:, dt * 1024:(dt + 1) * 1024], wb_[:], f"qwC{wi_}",
                            pre=[w_guard[wi_]] + (ctx.bar() if dt == 0 else [])))
                    for wi_, (wb_, src_, dstb) in enumerate(
                            ((wk, xk, kb_), (wv, xv, vb_), (wr, xr, None))):
                        for c in range(NCH):
                            pb = c % 2
                            ctx.waits(t, [evac.get(pb), mx_by_wi[wi_], wtok[wi_]])
                            for di in range(NDT):
                                mm = t.matmul(ps_mm[pb][:],
                                              wb_[:, di * 128:(di + 1) * 128],
                                              src_[:, di * T + c * CH:di * T + (c + 1) * CH],
                                              start=(di == 0), stop=(di == NDT - 1))
                            pt = ctx.inc(mm, "pe")
                            w_guard[wi_] = pt
                            if dstb is not None:
                                ctx.waits(s, [pt, kv_consumed[idx]])
                                evac[pb] = s.activation(out=dstb[:, c * CH:(c + 1) * CH],
                                                        in_=ps_mm[pb][:], func=AF.Identity)
                            else:
                                ctx.waits(s, [pt, prev_y])
                                evac[pb] = s.activation(out=rbuf[:, c * CH:(c + 1) * CH],
                                                        in_=ps_mm[pb][:], func=AF.Sigmoid)
                    kv_tok = [evac.get(0), evac.get(1)]
                    ctx.waits(s, kv_tok + [prev_y, prev_ecv])
                    ek_tok = s.activation(out=ekv[:], in_=kb_[:], func=AF.Exp)
                    ctx.waits(g, [ek_tok])
                    ekv_tok = g.tensor_tensor(out=ekv[:], in0=ekv[:], in1=vb_[:], op=ALU.mult)
                    ctx.waits(v, [ekv_tok])
                    v.memset(pst[:, 0:1], -1.0e30)
                    v.memset(nst[:, 0:1], 0.0)
                    dec_b = vecs[:, DEC * NDT + dt: DEC * NDT + dt + 1].broadcast_to((128, T))
                    ew_b = vecs[:, EW * NDT + dt: EW * NDT + dt + 1].broadcast_to((128, T))
                    u_sc = vecs[:, FIR * NDT + dt: FIR * NDT + dt + 1]
                    v.tensor_tensor_scan(out=pst[:, 1:T + 1], data0=dec_b, data1=kb_[:],
                                         initial=-1.0e30, op0=ALU.add, op1=ALU.max)
                    v.tensor_tensor_scan(out=nst[:, 1:T + 1], data0=ew_b, data1=ekv[:],
                                         initial=0.0, op0=ALU.mult, op1=ALU.add)
                    v.scalar_tensor_tensor(out=A1[:], in0=kb_[:], scalar=u_sc, in1=pst[:, 0:T],
                                           op0=ALU.add, op1=ALU.max)
                    A2 = pst[:, 0:T]
                    v.tensor_tensor(out=A2, in0=pst[:, 0:T], in1=A1[:], op=ALU.subtract)
                    tv = v.tensor_scalar(out=A2, in0=A2, scalar1=-30.0, scalar2=None, op0=ALU.max)
                    ctx.waits(s, [tv])
                    eo_tok = s.activation(out=A2, in_=A2, func=AF.Exp)
                    v.scalar_tensor_tensor(out=ekv[:], in0=kb_[:], scalar=u_sc, in1=A1[:],
                                           op0=ALU.add, op1=ALU.subtract)
                    tv = v.tensor_scalar(out=ekv[:], in0=ekv[:], scalar1=-30.0, scalar2=None, op0=ALU.max)
                    ctx.waits(s, [tv, eo_tok])
                    ec_tok = s.activation(out=ekv[:], in_=ekv[:], func=AF.Exp)
                    em_tok = s.activation(out=A1[:], in_=A1[:], func=AF.Exp, scale=-1.0)
                    ctx.waits(v, [ec_tok, em_tok])
                    tv = v.scalar_tensor_tensor(out=A2, in0=A2, scalar=EPS9, in1=ekv[:],
                                                op0=ALU.add, op1=ALU.add)
                    ctx.waits(g, [tv])
                    ecv_tok = g.tensor_tensor(out=ekv[:], in0=ekv[:], in1=vb_[:], op=ALU.mult)
                    prev_ecv = ecv_tok
                    ctx.waits(s, [tv])
                    s.activation(out=A2, in_=A2, func=AF.Ln)
                    rcp = s.activation(out=A2, in_=A2, func=AF.Exp, scale=-1.0)
                    ctx.waits(v, [rcp])
                    v.tensor_tensor(out=A1[:], in0=A1[:], in1=nst[:, 0:T], op=ALU.mult)
                    ctx.waits(v, [ecv_tok])
                    v.tensor_tensor(out=A1[:], in0=A1[:], in1=ekv[:], op=ALU.add)
                    v.tensor_tensor(out=A1[:], in0=A1[:], in1=A2, op=ALU.mult)
                    y_tok = v.tensor_tensor(out=Ybf[:, dt * T:(dt + 1) * T], in0=A1[:],
                                            in1=rbuf[:], op=ALU.mult)
                    kv_consumed[idx] = y_tok
                    prev_y = y_tok
                y_last = [y_tok]

        with contextlib.ExitStack() as es4:
            S4 = lambda n, sh, dt: es4.enter_context(nc.sbuf_tensor(n, sh, dt))
            P4 = lambda n, sh: es4.enter_context(nc.psum_tensor(n, sh, F32))
            wo = S4("C_wo", [128, NDT * D], BF16)
            Yp = S4("C_Yp", [128, NDT * T], F32)
            ps_mm = [P4("C_pmm2", [128, CH]), P4("C_pmm3", [128, CH])]
            wtok = [load_wcast(ctx, W["woT"], wo[:], "qwC0", pre=ctx.bar())]
            evac = {}
            for ot in range(NDT):
                for c in range(NCH):
                    pb = (ot * NCH + c) % 2
                    ctx.waits(t, [evac.get(pb)] + wtok + y_last)
                    for dt in range(NDT):
                        mm = t.matmul(ps_mm[pb][:],
                                      wo[:, dt * D + ot * 128: dt * D + (ot + 1) * 128],
                                      Ybf[:, dt * T + c * CH:dt * T + (c + 1) * CH],
                                      start=(dt == 0), stop=(dt == NDT - 1))
                    pt = ctx.inc(mm, "pe")
                    ctx.wait(s, pt)
                    evac[pb] = s.activation(out=Yp[:, ot * T + c * CH: ot * T + (c + 1) * CH],
                                            in_=ps_mm[pb][:], func=AF.Identity)
            ytok = [evac.get(0), evac.get(1)]
            with contextlib.ExitStack() as es5:
                S5 = lambda n, sh, dt: es5.enter_context(nc.sbuf_tensor(n, sh, dt))
                P5 = lambda n, sh: es5.enter_context(nc.psum_tensor(n, sh, F32))
                return emit_ln_residual_store(ctx, Yp, xin, xout, CONST, S5, P5, ytok, tag="CT")


def host_prep_C(inputs):
    vecs = np.stack([
        np.asarray(inputs["L1_attn_time_mix_k"]),
        np.asarray(inputs["L1_attn_time_mix_v"]),
        np.asarray(inputs["L1_attn_time_mix_r"]),
        np.asarray(inputs["L1_attn_time_decay"]),
        np.asarray(inputs["L1_attn_time_first"]),
    ])
    vecs_pp = np.concatenate([vec_pp(vv) for vv in vecs], axis=1)
    def ilv_dtblocks(w):
        wT = np.asarray(w).T
        return np.concatenate([ilv(wT[:, dt * 128:(dt + 1) * 128]) for dt in range(NDT)], 1)
    return {
        "wkT": ilv_dtblocks(inputs["L1_attn_W_k"]),
        "wvT": ilv_dtblocks(inputs["L1_attn_W_v"]),
        "wrT": ilv_dtblocks(inputs["L1_attn_W_r"]),
        "woT": ilv(np.asarray(inputs["L1_attn_W_o"]).T),
        "vecs_pp": np.ascontiguousarray(vecs_pp),
    }


def build_phaseC_program():
    nc = bass.Bass("TRN2", target_bir_lowering=False)
    es = contextlib.ExitStack()
    ctx = Ctx(nc, es)
    xin = nc.dram_tensor("xin", [D, T], F32, kind="ExternalInput").ap()
    xout = nc.dram_tensor("xout", [D, T], F32, kind="ExternalOutput").ap()
    W = {wn: nc.dram_tensor(wn, [128, NDT * D], F32, kind="ExternalInput").ap()
         for wn in ("wkT", "wvT", "wrT", "woT")}
    W["vecs_pp"] = nc.dram_tensor("vecs_pp", [128, 5 * NDT], F32, kind="ExternalInput").ap()
    with nc.Block():
        CONST = build_consts(ctx)
        for eng in (nc.tensor, nc.vector, nc.scalar, nc.gpsimd):
            ctx.waits(eng, CONST["_ready"])
        st = phase_l1attn(ctx, xin, xout, W, CONST)
        ctx.waits(nc.gpsimd, st)
    es.close()
    return nc



FD4 = 4096


def phase_l1ffn(ctx, xin, xout, W, CONST, start_toks=()):
    entry = ctx.bar() + list(start_toks)
    for _eng in (ctx.t, ctx.v, ctx.s, ctx.g, ctx.y):
        ctx.waits(_eng, entry)
    nc = ctx.nc
    v, s, t, g, y = ctx.v, ctx.s, ctx.t, ctx.g, ctx.y

    with contextlib.ExitStack() as es0:
        S0 = lambda n, sh, dt: es0.enter_context(nc.sbuf_tensor(n, sh, dt))
        U = S0("D_U", [128, NDT * T], BF16)
        Y = S0("D_Y", [128, NDT * T], F32)

        with contextlib.ExitStack() as es1:
            S1 = lambda n, sh, dt: es1.enter_context(nc.sbuf_tensor(n, sh, dt))
            P1 = lambda n, sh: es1.enter_context(nc.psum_tensor(n, sh, F32))
            xc = S1("D_xc", [128, NDT * CH], F32)
            sq = S1("D_sq", [128, NDT * CH], BF16)
            rows = LNRows(S1, "D_r1")
            ps_mean = P1("D_psm", [1, CH])
            ps_msq = P1("D_psq", [1, CH])
            ps_bc1 = P1("D_psb1", [128, CH])
            ps_bc2 = P1("D_psb2", [128, CH])
            xld = [None] * NDT
            uc_tok = [None] * NDT
            psum_free = {}
            bc_free = []
            for c in range(NCH):
                for dt in range(NDT):
                    xld[dt] = emit_dma_waited(
                        ctx, y, xc[:, dt * CH:(dt + 1) * CH],
                        xin[dt * 128:(dt + 1) * 128, c * CH:(c + 1) * CH],
                        chan=f"qx{dt}",
                        pre=[uc_tok[dt]] + ((ctx.bar() + list(start_toks)) if c == 0 else []))
                xtiles = [xc[:, dt * CH:(dt + 1) * CH] for dt in range(NDT)]
                st_tok = emit_ln_stats_chunk(ctx, xtiles, ps_mean, ps_msq, sq,
                                             CONST, [[x] for x in xld], psum_free)
                rdy = emit_ln_rowmath(ctx, ps_mean, ps_msq, rows, [st_tok])
                psum_free["mean"] = rdy
                psum_free["msq"] = rdy
                bt = emit_bcast2(ctx, CONST, rows.rstd, rows.b, ps_bc1, ps_bc2, rdy, bc_free)
                for dt in range(NDT):
                    ctx.waits(v, [bt])
                    v.tensor_tensor(out=U[:, dt * T + c * CH:dt * T + (c + 1) * CH],
                                    in0=xc[:, dt * CH:(dt + 1) * CH], in1=ps_bc1[:], op=ALU.mult)
                    uc_tok[dt] = v.tensor_tensor(
                        out=U[:, dt * T + c * CH:dt * T + (c + 1) * CH],
                        in0=U[:, dt * T + c * CH:dt * T + (c + 1) * CH],
                        in1=ps_bc2[:], op=ALU.add)
                bc_free = [uc_tok[NDT - 1]]
            u_tok = [uc_tok[NDT - 1]]

        with contextlib.ExitStack() as es2:
            S2 = lambda n, sh, dt: es2.enter_context(nc.sbuf_tensor(n, sh, dt))
            P2 = lambda n, sh: es2.enter_context(nc.psum_tensor(n, sh, F32))
            gw = S2("D_gw", [128, NDT * NEXP], BF16)
            gb = S2("D_gb", [NEXP, 1], F32)
            at_ = S2("D_at", [128, NDT * 32], BF16)
            bt_ = S2("D_bt", [32, FD4], BF16)
            e32 = S2("D_e32", [NEXP, 32], F32)
            logitT = S2("D_logT", [NEXP, T], F32)
            maskT = S2("D_maskT", [NEXP, T], F32)
            ltm = S2("D_ltm", [128, NTT * NEXP], F32)
            m1 = S2("D_m1", [128, 1], F32)
            xa = S2("D_xa", [32, T], F32)
            xam = S2("D_xam", [32, T], BF16)
            ps_g = P2("D_psg", [NEXP, CH])
            ps_t = [P2("D_pst0", [128, 128]), P2("D_pst1", [128, 128])]
            ps_xa = P2("D_psxa", [32, CH])

            gt = [load_wcast(ctx, W["gate_WT"], gw[:], "qwD1", pre=ctx.bar())]
            gbt = emit_dma_waited(ctx, y, gb[:], W["gate_b"], chan="qv0", pre=ctx.bar())
            e32t = emit_dma_waited(ctx, y, e32[:], W["e32"], chan="qv1", pre=ctx.bar())
            att = [load_wcast(ctx, W["AT"], at_[:], "qwD2", pre=[])]
            bt_tok = [load_wcast(ctx, W["BT_f32"], bt_[:], "qwD3", pre=[])]

            glast = None
            for c in range(NCH):
                ctx.waits(t, gt + [glast] + u_tok)
                for dt in range(NDT):
                    mm = t.matmul(ps_g[:], gw[:, dt * NEXP:(dt + 1) * NEXP],
                                  U[:, dt * T + c * CH:dt * T + (c + 1) * CH],
                                  start=(dt == 0), stop=(dt == NDT - 1))
                pt = ctx.inc(mm, "pe")
                ctx.waits(s, [pt, gbt])
                glast = s.activation(out=logitT[:, c * CH:(c + 1) * CH], in_=ps_g[:],
                                     func=AF.Identity, bias=gb[:])
            tr_done = [None, None]
            tm_tok = [None] * NTT
            for tt in range(NTT):
                b = tt % 2
                ctx.waits(t, [glast, tr_done[b]])
                tr = t.transpose(ps_t[b][0:128, 0:NEXP], logitT[:, tt * 128:(tt + 1) * 128],
                                 CONST["ident"][0:NEXP, 0:NEXP])
                ptok = ctx.inc(tr, "pe")
                ctx.wait(s, ptok)
                tm_tok[tt] = s.activation(out=ltm[:, tt * NEXP:(tt + 1) * NEXP],
                                          in_=ps_t[b][0:128, 0:NEXP], func=AF.Identity)
                tr_done[b] = tm_tok[tt]
            for tt in range(NTT):
                sl = ltm[:, tt * NEXP:(tt + 1) * NEXP]
                ctx.waits(v, [tm_tok[tt]])
                v.tensor_reduce(out=m1[:], in_=sl, axis=AX.X, op=ALU.max)
                v.tensor_scalar(out=sl, in0=sl, scalar1=m1[:], scalar2=None, op0=ALU.is_ge)
            tr_done = [None, None]
            mk_last = None
            for tt in range(NTT):
                b = tt % 2
                ctx.waits(t, [("dve", ctx.counts["dve"]), tr_done[b]])
                tr = t.transpose(ps_t[b][0:NEXP, 0:128], ltm[:, tt * NEXP:(tt + 1) * NEXP],
                                 CONST["ident"][:])
                ptok = ctx.inc(tr, "pe")
                ctx.wait(s, ptok)
                mk_last = s.activation(out=maskT[:, tt * 128:(tt + 1) * 128],
                                       in_=ps_t[b][0:NEXP, 0:128], func=AF.Identity)
                tr_done[b] = mk_last
            xa_last = None
            for c in range(NCH):
                ctx.waits(t, att + [xa_last] + u_tok)
                for dt in range(NDT):
                    mm = t.matmul(ps_xa[:], at_[:, dt * 32:(dt + 1) * 32],
                                  U[:, dt * T + c * CH:dt * T + (c + 1) * CH],
                                  start=(dt == 0), stop=(dt == NDT - 1))
                pt = ctx.inc(mm, "pe")
                ctx.wait(s, pt)
                xa_last = s.activation(out=xa[:, c * CH:(c + 1) * CH], in_=ps_xa[:],
                                       func=AF.Identity)
            xam_last = None
            for c in range(NCH):
                ctx.waits(t, [e32t, mk_last, xam_last, xa_last])
                mm = t.matmul(ps_xa[:], e32[:], maskT[:, c * CH:(c + 1) * CH],
                              start=True, stop=True)
                pt = ctx.inc(mm, "pe")
                ctx.waits(v, [pt, xa_last])
                xam_last = v.tensor_tensor(out=xam[0:32, c * CH:(c + 1) * CH],
                                           in0=xa[0:32, c * CH:(c + 1) * CH],
                                           in1=ps_xa[0:32, :], op=ALU.mult)
            xam_tok = [xam_last]

            with contextlib.ExitStack() as es3:
                S3 = lambda n, sh, dt: es3.enter_context(nc.sbuf_tensor(n, sh, dt))
                P3 = lambda n, sh: es3.enter_context(nc.psum_tensor(n, sh, F32))
                wib = [S3("D_wib0", [128, NDT * 1024], BF16), S3("D_wib1", [128, NDT * 1024], BF16)]
                wob = [S3("D_wob0", [128, NDT * 1024], BF16), S3("D_wob1", [128, NDT * 1024], BF16)]
                wibias = S3("D_wibias", [128, 32], F32)
                wobias = S3("D_wobias", [128, NDT], F32)
                inter = S3("D_inter", [128, NDT * CH], BF16)
                ps_i = [P3("D_psi0", [128, CH]), P3("D_psi1", [128, CH])]
                ps_o = [P3("D_pso0", [128, CH]), P3("D_pso1", [128, CH])]
                wibt = emit_dma_waited(ctx, y, wibias[:], W["wi_b_pp"], chan="qv0",
                                       pre=ctx.bar())
                wobt = emit_dma_waited(ctx, y, wobias[:], W["wo_b_pp"], chan="qv1",
                                       pre=ctx.bar())
                wi_used = [None, None]
                wo_used = [None, None]
                i_free = {}
                o_free = {}
                y_last = None
                for fsb in range(4):
                    wslot = fsb % 2
                    wtoki = [load_wcast(ctx, W["wiT"][:, fsb * 8192:(fsb + 1) * 8192], wib[wslot][:],
                                        f"qwiD{wslot}",
                                        pre=[wi_used[wslot]] + (ctx.bar() if fsb < 2 else []))]
                    wtoko = [load_wcast(ctx, W["woT2"][:, fsb * 8192:(fsb + 1) * 8192], wob[wslot][:],
                                        f"qwoD{wslot}",
                                        pre=[wo_used[wslot]] + (ctx.bar() if fsb < 2 else []))]
                    for c in range(NCH):
                        for fi in range(NDT):
                            fs = fsb * NDT + fi
                            pb = fi % 2
                            ctx.waits(t, [i_free.get(pb)] + wtoki + u_tok + xam_tok)
                            for dt in range(NDT):
                                t.matmul(ps_i[pb][:],
                                         wib[wslot][:, dt * 1024 + fi * 128: dt * 1024 + (fi + 1) * 128],
                                         U[:, dt * T + c * CH:dt * T + (c + 1) * CH],
                                         start=(dt == 0), stop=False)
                            mm = t.matmul(ps_i[pb][:], bt_[:, fs * 128:(fs + 1) * 128],
                                          xam[:, c * CH:(c + 1) * CH],
                                          start=False, stop=True)
                            pt = ctx.inc(mm, "pe")
                            wi_used[wslot] = pt
                            ctx.waits(s, [pt, wibt, bt_tok[-1] if fsb == 0 else None])
                            i_free[pb] = s.activation(out=inter[:, fi * CH:(fi + 1) * CH],
                                                      in_=ps_i[pb][:], func=AF.Relu,
                                                      bias=wibias[:, fs:fs + 1])
                        it_tok = [i_free.get(0), i_free.get(1)]
                        for ot in range(NDT):
                            pb = ot % 2
                            ctx.waits(t, [o_free.get(pb)] + wtoko + it_tok)
                            for fi in range(NDT):
                                mm = t.matmul(ps_o[pb][:],
                                              wob[wslot][:, fi * 1024 + ot * 128: fi * 1024 + (ot + 1) * 128],
                                              inter[:, fi * CH:(fi + 1) * CH],
                                              start=(fi == 0), stop=(fi == NDT - 1))
                            pt = ctx.inc(mm, "pe")
                            wo_used[wslot] = pt
                            if fsb == 0:
                                ctx.waits(s, [pt, wobt])
                                y_last = s.activation(out=Y[:, ot * T + c * CH: ot * T + (c + 1) * CH],
                                                      in_=ps_o[pb][:], func=AF.Identity,
                                                      bias=wobias[:, ot:ot + 1])
                                o_free[pb] = y_last
                            else:
                                ctx.waits(v, [pt, ("act", ctx.counts["act"])])
                                y_last = v.tensor_tensor(
                                    out=Y[:, ot * T + c * CH: ot * T + (c + 1) * CH],
                                    in0=Y[:, ot * T + c * CH: ot * T + (c + 1) * CH],
                                    in1=ps_o[pb][:], op=ALU.add)
                                o_free[pb] = y_last
                y_toks = [y_last]

        with contextlib.ExitStack() as es4:
            S4 = lambda n, sh, dt: es4.enter_context(nc.sbuf_tensor(n, sh, dt))
            P4 = lambda n, sh: es4.enter_context(nc.psum_tensor(n, sh, F32))
            return emit_ln_residual_store(ctx, Y, xin, xout, CONST, S4, P4, y_toks, tag="DT")


def host_prep_D(inputs):
    la = np.asarray(inputs["L1_ffn_lora_A"])
    lb = np.asarray(inputs["L1_ffn_lora_B"])
    e32 = np.zeros((NEXP, 32), np.float32)
    for e in range(NEXP):
        e32[e, e * 4:(e + 1) * 4] = 1.0
    wiT = np.asarray(inputs["L1_ffn_wi_W"]).T
    woT2 = np.asarray(inputs["L1_ffn_wo_W"]).T
    wiT_i = np.concatenate([ilv(wiT[:, f * 1024:(f + 1) * 1024]) for f in range(4)], 1)
    woT2_i = np.concatenate([ilv(woT2[f * 1024:(f + 1) * 1024, :]) for f in range(4)], 1)
    return {
        "gate_WT": ilv(np.asarray(inputs["L1_ffn_gate_W"]).T),
        "gate_b": np.ascontiguousarray(np.asarray(inputs["L1_ffn_gate_b"]).reshape(NEXP, 1)),
        "wiT": np.ascontiguousarray(wiT_i),
        "woT2": np.ascontiguousarray(woT2_i),
        "AT": ilv(la.transpose(2, 0, 1).reshape(1024, 32)),
        "BT_f32": np.ascontiguousarray(lb.transpose(0, 2, 1).reshape(32, FD4)),
        "wi_b_pp": vec_pp(inputs["L1_ffn_wi_b"]),
        "wo_b_pp": vec_pp(inputs["L1_ffn_wo_b"]),
        "e32": e32,
    }


def build_phaseD_program():
    nc = bass.Bass("TRN2", target_bir_lowering=False)
    es = contextlib.ExitStack()
    ctx = Ctx(nc, es)
    xin = nc.dram_tensor("xin", [D, T], F32, kind="ExternalInput").ap()
    xout = nc.dram_tensor("xout", [D, T], F32, kind="ExternalOutput").ap()
    W = {
        "gate_WT": nc.dram_tensor("gate_WT", [128, NDT * NEXP], F32, kind="ExternalInput").ap(),
        "gate_b": nc.dram_tensor("gate_b", [NEXP, 1], F32, kind="ExternalInput").ap(),
        "wiT": nc.dram_tensor("wiT", [128, 4 * 8192], F32, kind="ExternalInput").ap(),
        "woT2": nc.dram_tensor("woT2", [128, 4 * 8192], F32, kind="ExternalInput").ap(),
        "AT": nc.dram_tensor("AT", [128, NDT * 32], F32, kind="ExternalInput").ap(),
        "BT_f32": nc.dram_tensor("BT_f32", [32, FD4], F32, kind="ExternalInput").ap(),
        "wi_b_pp": nc.dram_tensor("wi_b_pp", [128, 32], F32, kind="ExternalInput").ap(),
        "wo_b_pp": nc.dram_tensor("wo_b_pp", [128, NDT], F32, kind="ExternalInput").ap(),
        "e32": nc.dram_tensor("e32", [NEXP, 32], F32, kind="ExternalInput").ap(),
    }
    with nc.Block():
        CONST = build_consts(ctx)
        for eng in (nc.tensor, nc.vector, nc.scalar, nc.gpsimd):
            ctx.waits(eng, CONST["_ready"])
        st = phase_l1ffn(ctx, xin, xout, W, CONST)
        ctx.waits(nc.gpsimd, st)
    es.close()
    return nc



def make_host_consts():
    con = {}
    con["c_ident"] = np.eye(128, dtype=np.float32)
    con["c_tri"] = np.triu(np.ones((128, 128), np.float32))
    return con


def build_consts(ctx):
    nc = ctx.nc
    C = {}
    C["ident"] = ctx.S("C_ident", [128, 128], F32)
    C["identb"] = ctx.S("C_identb", [128, 128], BF16)
    C["trimask"] = ctx.S("C_trimask", [128, 128], BF16)
    C["ones_scaled"] = ctx.S("C_onessc", [128, 1], F32)
    C["ones_scaled_b"] = ctx.S("C_onesscb", [128, 1], BF16)
    C["onesrow"] = ctx.S("C_onesrow", [1, 128], F32)
    C["onesrow_b"] = ctx.S("C_onesrowb", [1, 128], BF16)
    tri_f = ctx.S("C_trif", [128, 128], F32)
    C["eps_row"] = ctx.S("C_epsrow", [1, 1], F32)
    ctx.v.memset(C["eps_row"][:], LN_EPS)
    ctx.eps_row = C["eps_row"][:]
    ident_h = nc.dram_tensor("c_ident", [128, 128], F32, kind="ExternalInput").ap()
    tri_h = nc.dram_tensor("c_tri", [128, 128], F32, kind="ExternalInput").ap()
    t1 = emit_dma_waited(ctx, ctx.y, C["ident"][:], ident_h, chan="qc1")
    t3 = emit_dma_waited(ctx, ctx.y, tri_f[:], tri_h, chan="qc2")
    ctx.waits(ctx.g, [t1, t3])
    ctx.g.tensor_copy(out=C["identb"][:], in_=C["ident"][:])
    cv = ctx.g.tensor_copy(out=C["trimask"][:], in_=tri_f[:])
    ctx.v.memset(C["ones_scaled"][:], 1.0 / 1024.0)
    ctx.v.memset(C["ones_scaled_b"][:], 1.0 / 1024.0)
    ctx.v.memset(C["onesrow_b"][:], 1.0)
    ms = ctx.v.memset(C["onesrow"][:], 1.0)
    C["_ready"] = [cv, ms]
    return C



def build_phaseA_program():
    nc = bass.Bass("TRN2", target_bir_lowering=False)
    es = contextlib.ExitStack()
    ctx = Ctx(nc, es)
    xin = nc.dram_tensor("xin", [D, T], F32, kind="ExternalInput").ap()
    xout = nc.dram_tensor("xout", [D, T], F32, kind="ExternalOutput").ap()
    W = {wn: nc.dram_tensor(wn, [128, NDT * D], F32, kind="ExternalInput").ap()
         for wn in ("wqT", "wkT", "wvT", "woT")}
    with nc.Block():
        CONST = build_consts(ctx)
        for eng in (nc.tensor, nc.vector, nc.scalar, nc.gpsimd):
            ctx.waits(eng, CONST["_ready"])
        st = phase_l0attn(ctx, xin, xout, W, CONST)
        ctx.waits(nc.gpsimd, st)
    es.close()
    return nc


def host_prep_A(inputs):
    shared = {
        "wqT": ilv(np.asarray(inputs["L0_attn_q_heads"]).T),
        "wkT": ilv(np.asarray(inputs["L0_attn_k_heads"]).T),
        "wvT": ilv(np.asarray(inputs["L0_attn_v_heads"]).T),
        "woT": ilv(np.asarray(inputs["L0_attn_output_proj"]).T),
    }
    shared.update(make_host_consts())
    return shared


_BUILDERS = None


def kernel_multi(**inputs):
    x = np.asarray(inputs["L0_attn_x"])
    consts = make_host_consts()
    phase_w = [host_prep_A(inputs), host_prep_B(inputs),
               host_prep_C(inputs), host_prep_D(inputs)]
    builders = [build_phaseA_program, build_phaseB_program,
                build_phaseC_program, build_phaseD_program]
    xT = np.ascontiguousarray(x.transpose(0, 2, 1)).astype(np.float32, copy=False)
    cur = [np.ascontiguousarray(xT[b]) for b in range(8)]
    for build, Wd in zip(builders, phase_w):
        nc = build()
        in_maps = []
        for b in range(8):
            m = dict(Wd)
            m.update(consts)
            m["xin"] = cur[b]
            in_maps.append(m)
        res = run_bass_kernel_spmd(nc, in_maps, core_ids=list(range(8)))
        cur = [np.asarray(res.results[b]["xout"]) for b in range(8)]
    return np.stack([c.T for c in cur]).astype(np.float32, copy=False)


def ilv(w):
    K, M = w.shape
    return np.ascontiguousarray(np.asarray(w).reshape(K // 128, 128, M).transpose(1, 0, 2).reshape(128, -1))


def qkv_perm():
    perm = np.zeros(1536, dtype=np.int64)
    for h in range(16):
        for d0 in range(64):
            orig = h * 64 + d0
            if d0 < 32:
                new = (h // 4) * 128 + (h % 4) * 32 + d0
            else:
                new = 512 + (h // 4) * 128 + (h % 4) * 32 + (d0 - 32)
            perm[new] = orig
    for kv in range(4):
        for d0 in range(64):
            orig = 1024 + kv * 64 + d0
            if d0 < 32:
                new = 1024 + kv * 32 + d0
            else:
                new = 1152 + kv * 32 + (d0 - 32)
            perm[new] = orig
    for j in range(1280, 1536):
        perm[j] = j
    return perm


def rope_tables():
    inv_freq = 1.0 / (10000.0 ** (np.arange(0, HD, 2, dtype=np.float32) / HD))
    freqs = np.arange(T, dtype=np.float32)[:, None] * inv_freq[None, :]
    cosH = np.cos(freqs).T.astype(np.float32)
    sinH = np.sin(freqs).T.astype(np.float32)
    return np.tile(cosH, (4, 1)).copy(), np.tile(sinH, (4, 1)).copy()


def vec_pp(vec):
    return np.ascontiguousarray(np.asarray(vec).reshape(-1, 128).T)


def host_prep_B(inputs):
    perm = qkv_perm()
    wqkvT = np.asarray(inputs["L0_ffn_wqkv"]).T[:, perm]
    wqkvT_i = np.concatenate([ilv(wqkvT[:, 0:768]), ilv(wqkvT[:, 768:1536])], axis=1)
    cosF, sinF = rope_tables()
    w13 = np.asarray(inputs["L0_ffn_w13"])
    w1T = w13[:, :FFN, :].transpose(0, 2, 1)
    w3T = w13[:, FFN:, :].transpose(0, 2, 1)
    w1T_i = np.stack([np.concatenate([ilv(w1T[e][:, s * CH:(s + 1) * CH]) for s in range(4)], 1)
                      for e in range(NEXP)]).reshape(NEXP * 128, 4 * 4096)
    w3T_i = np.stack([np.concatenate([ilv(w3T[e][:, s * CH:(s + 1) * CH]) for s in range(4)], 1)
                      for e in range(NEXP)]).reshape(NEXP * 128, 4 * 4096)
    w2T = np.asarray(inputs["L0_ffn_w2"]).transpose(0, 2, 1)
    w2T_i = np.stack([ilv(w2T[e]) for e in range(NEXP)]).reshape(NEXP * 128, NTT * 1024)
    return {
        "wqkvT": np.ascontiguousarray(wqkvT_i),
        "woT": ilv(np.asarray(inputs["L0_ffn_wo"]).T),
        "gate_wT": ilv(np.asarray(inputs["L0_ffn_gate_w"]).T),
        "ln1_pp": vec_pp(inputs["L0_ffn_ln1_w"]),
        "ln2_pp": vec_pp(inputs["L0_ffn_ln2_w"]),
        "cosF": cosF, "sinF": sinF,
        "w1T": np.ascontiguousarray(w1T_i), "w3T": np.ascontiguousarray(w3T_i),
        "w2T": np.ascontiguousarray(w2T_i),
    }


def build_phaseB_program():
    nc = bass.Bass("TRN2", target_bir_lowering=False)
    es = contextlib.ExitStack()
    ctx = Ctx(nc, es)
    xin = nc.dram_tensor("xin", [D, T], F32, kind="ExternalInput").ap()
    xout = nc.dram_tensor("xout", [D, T], F32, kind="ExternalOutput").ap()
    W = {
        "wqkvT": nc.dram_tensor("wqkvT", [128, 2 * 6144], F32, kind="ExternalInput").ap(),
        "woT": nc.dram_tensor("woT", [128, NDT * D], F32, kind="ExternalInput").ap(),
        "gate_wT": nc.dram_tensor("gate_wT", [128, NDT * NEXP], F32, kind="ExternalInput").ap(),
        "ln1_pp": nc.dram_tensor("ln1_pp", [128, NDT], F32, kind="ExternalInput").ap(),
        "ln2_pp": nc.dram_tensor("ln2_pp", [128, NDT], F32, kind="ExternalInput").ap(),
        "cosF": nc.dram_tensor("cosF", [128, T], F32, kind="ExternalInput").ap(),
        "sinF": nc.dram_tensor("sinF", [128, T], F32, kind="ExternalInput").ap(),
        "w1T": nc.dram_tensor("w1T", [NEXP * 128, 4 * 4096], F32, kind="ExternalInput").ap(),
        "w3T": nc.dram_tensor("w3T", [NEXP * 128, 4 * 4096], F32, kind="ExternalInput").ap(),
        "w2T": nc.dram_tensor("w2T", [NEXP * 128, NTT * 1024], F32, kind="ExternalInput").ap(),
    }
    with nc.Block():
        CONST = build_consts(ctx)
        for eng in (nc.tensor, nc.vector, nc.scalar, nc.gpsimd):
            ctx.waits(eng, CONST["_ready"])
        st = phase_l0ffn(ctx, xin, xout, W, CONST)
        ctx.waits(nc.gpsimd, st)
    es.close()
    return nc


def build_fused_program():
    nc = bass.Bass("TRN2", target_bir_lowering=False)
    es = contextlib.ExitStack()
    ctx = Ctx(nc, es)
    xin = nc.dram_tensor("xin", [D, T], F32, kind="ExternalInput").ap()
    xout = nc.dram_tensor("xout", [D, T], F32, kind="ExternalOutput").ap()
    import os
    _xmk = "ExternalOutput" if os.environ.get("BASS_DEBUG_XM") else "Internal"
    xm = [nc.dram_tensor(f"xm{i}", [D, T], F32, kind=_xmk).ap() for i in range(3)]
    WA = {wn: nc.dram_tensor(wn, [128, NDT * D], F32, kind="ExternalInput").ap()
          for wn in ("wqT", "wkT", "wvT", "woT")}
    WB = {
        "wqkvT": nc.dram_tensor("b_wqkvT", [128, 2 * 6144], F32, kind="ExternalInput").ap(),
        "woT": nc.dram_tensor("b_woT", [128, NDT * D], F32, kind="ExternalInput").ap(),
        "gate_wT": nc.dram_tensor("b_gate_wT", [128, NDT * NEXP], F32, kind="ExternalInput").ap(),
        "ln1_pp": nc.dram_tensor("b_ln1_pp", [128, NDT], F32, kind="ExternalInput").ap(),
        "ln2_pp": nc.dram_tensor("b_ln2_pp", [128, NDT], F32, kind="ExternalInput").ap(),
        "cosF": nc.dram_tensor("b_cosF", [128, T], F32, kind="ExternalInput").ap(),
        "sinF": nc.dram_tensor("b_sinF", [128, T], F32, kind="ExternalInput").ap(),
        "w1T": nc.dram_tensor("b_w1T", [NEXP * 128, 4 * 4096], F32, kind="ExternalInput").ap(),
        "w3T": nc.dram_tensor("b_w3T", [NEXP * 128, 4 * 4096], F32, kind="ExternalInput").ap(),
        "w2T": nc.dram_tensor("b_w2T", [NEXP * 128, NTT * 1024], F32, kind="ExternalInput").ap(),
    }
    WC = {wn: nc.dram_tensor("c_" + wn, [128, NDT * D], F32, kind="ExternalInput").ap()
          for wn in ("wkT", "wvT", "wrT", "woT")}
    WC["vecs_pp"] = nc.dram_tensor("c_vecs_pp", [128, 5 * NDT], F32, kind="ExternalInput").ap()
    WD = {
        "gate_WT": nc.dram_tensor("d_gate_WT", [128, NDT * NEXP], F32, kind="ExternalInput").ap(),
        "gate_b": nc.dram_tensor("d_gate_b", [NEXP, 1], F32, kind="ExternalInput").ap(),
        "wiT": nc.dram_tensor("d_wiT", [128, 4 * 8192], F32, kind="ExternalInput").ap(),
        "woT2": nc.dram_tensor("d_woT2", [128, 4 * 8192], F32, kind="ExternalInput").ap(),
        "AT": nc.dram_tensor("d_AT", [128, NDT * 32], F32, kind="ExternalInput").ap(),
        "BT_f32": nc.dram_tensor("d_BT_f32", [32, FD4], F32, kind="ExternalInput").ap(),
        "wi_b_pp": nc.dram_tensor("d_wi_b_pp", [128, 32], F32, kind="ExternalInput").ap(),
        "wo_b_pp": nc.dram_tensor("d_wo_b_pp", [128, NDT], F32, kind="ExternalInput").ap(),
        "e32": nc.dram_tensor("d_e32", [NEXP, 32], F32, kind="ExternalInput").ap(),
    }
    with nc.Block():
        CONST = build_consts(ctx)
        for eng in (nc.tensor, nc.vector, nc.scalar, nc.gpsimd):
            ctx.waits(eng, CONST["_ready"])
        st = phase_l0attn(ctx, xin, xm[0], WA, CONST)
        st = phase_l0ffn(ctx, xm[0], xm[1], WB, CONST, start_toks=st)
        st = phase_l1attn(ctx, xm[1], xm[2], WC, CONST, start_toks=st)
        st = phase_l1ffn(ctx, xm[2], xout, WD, CONST, start_toks=st)
        ctx.waits(nc.gpsimd, st)
    es.close()
    return nc


def kernel(**inputs):
    x = np.asarray(inputs["L0_attn_x"])
    consts = make_host_consts()
    WA = host_prep_A(inputs)
    WB = {"b_" + k: v for k, v in host_prep_B(inputs).items()}
    WC = {"c_" + k: v for k, v in host_prep_C(inputs).items()}
    WD = {"d_" + k: v for k, v in host_prep_D(inputs).items()}
    shared = {}
    shared.update(WA); shared.update(WB); shared.update(WC); shared.update(WD)
    shared.update(consts)
    nc = build_fused_program()
    xT = np.ascontiguousarray(x.transpose(0, 2, 1)).astype(np.float32, copy=False)
    in_maps = []
    for b in range(8):
        m = dict(shared)
        m["xin"] = np.ascontiguousarray(xT[b])
        in_maps.append(m)
    res = run_bass_kernel_spmd(nc, in_maps, core_ids=list(range(8)))
    return np.stack([np.asarray(res.results[b]["xout"]).T for b in range(8)]).astype(np.float32, copy=False)
